# version 04
# baseline (speedup 1.0000x reference)
from contextlib import ExitStack

import numpy as np

import concourse.bass as bass
import concourse.mybir as mybir
from concourse.bass_utils import run_bass_kernel_spmd

F32 = mybir.dt.float32
BF16 = mybir.dt.bfloat16
AF = mybir.ActivationFunctionType
ALU = mybir.AluOpType
AX = mybir.AxisListType.X

D, KD = 2048, 16
DFF, KF = 1408, 11
NE, NRT = 18, 16
T = 2048
MB, HF = 1024, 512
NH, NKV, HD = 16, 4, 128
SCL = float(1.0 / np.sqrt(128.0))
NEG = -1.0e30
ENGINES = ("tensor", "vector", "scalar", "sync", "gpsimd")
EXPERT_ORDER = [16, 17] + list(range(16))
STATS_BF16 = False


class Sem:
    __slots__ = ("h", "n", "name")


class Rot:

    def __init__(self, n, p=None, tag=None):
        self.n = n
        self.rel = [()] * n
        self.i = -1
        self.sems = ([p.sem(_u(tag + str(i))) for i in range(n)]
                     if p is not None else None)

    def acq(self):
        self.i += 1
        sl = self.i % self.n
        w = self.rel[sl]
        self.rel[sl] = ()
        return sl, list(w)

    def set_rel(self, sl, conds):
        self.rel[sl] = tuple(conds)


class Prog:
    def __init__(self, nc, ctx):
        self.nc, self.ctx = nc, ctx
        self.ops = {e: [] for e in ENGINES}
        self.known = {e: {} for e in ENGINES}
        self.store_sems = []

    def stores_done(self):
        return [(s, s.n) for s in self.store_sems if s.n > 0]

    def sem(self, name):
        s = Sem()
        s.h = self.ctx.enter_context(self.nc.semaphore(name))
        s.n = 0
        s.name = name
        return s

    def emit(self, eng, fn, wait=(), inc=None, incv=1):
        wl = []
        for (s, v) in wait:
            if v > self.known[eng].get(s, 0):
                self.known[eng][s] = v
                wl.append((s, v))
        if inc is not None:
            inc.n += incv
        self.ops[eng].append((wl, fn, inc, incv))

    def dma(self, q, out, in_, wait=(), inc=None):
        self.emit(q, (lambda e, o=out, i=in_: e.dma_start(out=o, in_=i)), wait, inc, 16)

    def mm(self, out, lhsT, rhs, start, stop, wait=(), inc=None):
        self.emit(
            "tensor",
            (lambda e, o=out, l=lhsT, r=rhs, a=start, b=stop:
             e.matmul(o, l, r, start=a, stop=b, skip_group_check=True)),
            wait, inc, 1)

    def tr(self, out, in_, ident, wait=(), inc=None):
        self.emit("tensor", (lambda e, o=out, i=in_, d=ident: e.transpose(o, i, d)),
                  wait, inc, 1)

    def act(self, out, in_, func, scale=1.0, bias=None, wait=(), inc=None,
            acc=None):
        kw = {}
        if bias is not None:
            kw["bias"] = bias
        if acc is not None:
            kw["accum_out"] = acc
        fn = (lambda e, o=out, i=in_, f=func, s=scale, k=kw:
              e.activation(o, i, f, scale=s, **k))
        self.emit("scalar", fn, wait, inc, 1)

    def acopy(self, out, in_, wait=(), inc=None):
        self.act(out, in_, AF.Copy, wait=wait, inc=inc)

    def drain(self, eng="vector"):
        self.emit(eng, lambda e: e.drain())

    def v(self, name, *args, wait=(), inc=None, **kw):
        self.emit("vector",
                  (lambda e, n=name, a=args, k=kw: getattr(e, n)(*a, **k)),
                  wait, inc, 1)

    def finalize(self, block):
        def mk(eng_name):
            ops = self.ops[eng_name]

            def run(engine):
                for (wl, fn, inc, incv) in ops:
                    for (s, vv) in wl:
                        engine.wait_ge(s.h, vv)
                    if fn is None:
                        continue
                    inst = fn(engine)
                    if inc is not None:
                        inst.then_inc(inc.h, incv)
            return run
        block.tensor(mk("tensor"))
        block.vector(mk("vector"))
        block.scalar(mk("scalar"))
        block.sync(mk("sync"))
        block.gpsimd(mk("gpsimd"))


class NS:
    pass


_UID = [0]


def _u(name):
    _UID[0] += 1
    return f"{name}_{_UID[0]}"


def build_nc(cfg):
    nc = bass.Bass("TRN2", target_bir_lowering=False)
    ctx = ExitStack()
    p = Prog(nc, ctx)
    t = NS()
    e = ctx.enter_context

    t.xT = nc.dram_tensor("xT", [2, D, T], F32, kind="ExternalInput").ap()
    t.w1p = nc.dram_tensor("w1p", [NE, KF, 128, KD * 128], F32, kind="ExternalInput").ap()
    t.w2p = nc.dram_tensor("w2p", [NE, KD, 128, KF * 128], F32, kind="ExternalInput").ap()
    t.wqp = nc.dram_tensor("wqp", [NH, 128, KD * 128], F32, kind="ExternalInput").ap()
    t.wop = nc.dram_tensor("wop", [KD, 128, KD * 128], F32, kind="ExternalInput").ap()
    t.wkp = nc.dram_tensor("wkp", [128, KD * 512], F32, kind="ExternalInput").ap()
    t.wvp = nc.dram_tensor("wvp", [128, KD * 512], F32, kind="ExternalInput").ap()
    t.centp = nc.dram_tensor("centp", [128, KD * 16], F32, kind="ExternalInput").ap()
    t.b1c = nc.dram_tensor("b1c", [128, NE * KF], F32, kind="ExternalInput").ap()
    t.b2r = nc.dram_tensor("b2r", [NE, D], F32, kind="ExternalInput").ap()
    t.rbias = nc.dram_tensor("rbias", [16, 1], F32, kind="ExternalInput").ap()
    t.qsc = nc.dram_tensor("qsc", [128, 1], F32, kind="ExternalInput").ap()
    t.ksc = nc.dram_tensor("ksc", [128, 1], F32, kind="ExternalInput").ap()
    t.idf = nc.dram_tensor("idf", [128, 128], F32, kind="ExternalInput").ap()
    t.causal = nc.dram_tensor("causal", [128, 128], F32, kind="ExternalInput").ap()
    t.onescol = nc.dram_tensor("onescol", [128, 1], F32, kind="ExternalInput").ap()
    t.onesrow = nc.dram_tensor("onesrow", [1, 128], F32, kind="ExternalInput").ap()
    t.sel = nc.dram_tensor("sel", [16, 16 * 128], F32, kind="ExternalInput").ap()
    t.cvec = nc.dram_tensor("cvec", [1, 8], F32, kind="ExternalInput").ap()

    y1_kind = ("ExternalInput" if cfg.get("y1_input") else
               "ExternalOutput" if cfg.get("expose_y1") else "Internal")
    y2_kind = "ExternalOutput" if cfg.get("expose_y2") else "Internal"
    t.y1d = nc.dram_tensor("y1d", [2, D, T], F32, kind=y1_kind).ap()
    t.y2d = nc.dram_tensor("y2d", [2, D, T], F32, kind=y2_kind).ap()
    t.outT = nc.dram_tensor("outT", [2, D, T], F32, kind="ExternalOutput").ap()
    t.w1b = nc.dram_tensor("w1b", [NE, KF, 128, KD * 128], BF16, kind="Internal").ap()
    t.w2b = nc.dram_tensor("w2b", [NE, KD, 128, KF * 128], BF16, kind="Internal").ap()
    t.wqb = nc.dram_tensor("wqb", [NH, 128, KD * 128], BF16, kind="Internal").ap()
    t.wob = nc.dram_tensor("wob", [KD, 128, KD * 128], BF16, kind="Internal").ap()
    t.wcache = {}
    t.overlap_stats = not cfg.get("no_overlap", False)
    if cfg.get("dbg"):
        t.dbg_sc = nc.dram_tensor("dbg_sc", [16, MB], F32, kind="ExternalOutput").ap()
        t.dbg_g = nc.dram_tensor("dbg_g", [NE, MB], BF16, kind="ExternalOutput").ap()
        t.dbg_u = nc.dram_tensor("dbg_u", [128, KD * MB], BF16, kind="ExternalOutput").ap()
        t.dbg_y = nc.dram_tensor("dbg_y", [128, KD * MB], F32, kind="ExternalOutput").ap()
        t.dbg_tr = nc.dram_tensor("dbg_tr", [1, 2 * MB], F32, kind="ExternalOutput").ap()
    t.dbg = cfg.get("dbg", False)
    rk = "ExternalOutput" if cfg.get("expose_rows") else "Internal"
    t.y1rows = nc.dram_tensor("y1rows", [2, 2, T], F32, kind=rk).ap()
    t.y2rows = nc.dram_tensor("y2rows", [2, 2, T], F32, kind="Internal").ap()

    t.idf_sb = e(nc.sbuf_tensor("idf_sb", [128, 128], F32)).ap()
    t.idb_sb = e(nc.sbuf_tensor("idb_sb", [128, 128], BF16)).ap()
    t.causal_sb = e(nc.sbuf_tensor("causal_sb", [128, 128], F32)).ap()
    t.onescol_sb = e(nc.sbuf_tensor("onescol_sb", [128, 1], F32)).ap()
    t.onesrow_sb = e(nc.sbuf_tensor("onesrow_sb", [1, 128], F32)).ap()
    t.cvec_sb = e(nc.sbuf_tensor("cvec_sb", [1, 8], F32)).ap()
    t.qsc_sb = e(nc.sbuf_tensor("qsc_sb", [128, 1], F32)).ap()
    t.ksc_sb = e(nc.sbuf_tensor("ksc_sb", [128, 1], F32)).ap()
    t.rbias_sb = e(nc.sbuf_tensor("rbias_sb", [16, 1], F32)).ap()
    t.onescol_bf = e(nc.sbuf_tensor("onescol_bf", [128, 1], BF16)).ap()
    t.r_row = e(nc.sbuf_tensor("r_row", [1, T], F32)).ap()
    t.mr_row = e(nc.sbuf_tensor("mr_row", [1, T], F32)).ap()
    t.trow = e(nc.sbuf_tensor("trow", [1, 3 * HF], F32)).ap()

    t.s_prep = p.sem("s_prep")
    t.s_prep2 = p.sem("s_prep2")
    for nm in ("s_ldx", "s_sq", "s_pex", "s_peq", "s_pec", "s_row", "s_r",
               "s_u", "s_scf", "s_gt", "s_ld1", "s_ld2", "s_p1", "s_rl",
               "s_hd", "s_gb", "s_p2", "s_ya", "s_ev", "s_yst",
               "s_kv", "s_sc2", "s_mx", "s_exp", "s_pn", "s_pc", "s_out",
               "s_oc", "s_qn", "s_yrw", "s_y2rw", "s_arow"):
        setattr(t, nm, p.sem(nm))
    t.s_rwl = [p.sem("s_rwl0"), p.sem("s_rwl1")]

    for (dst, src) in ((t.idf_sb, t.idf), (t.causal_sb, t.causal),
                       (t.onescol_sb, t.onescol), (t.onesrow_sb, t.onesrow),
                       (t.cvec_sb, t.cvec), (t.qsc_sb, t.qsc), (t.ksc_sb, t.ksc),
                       (t.rbias_sb, t.rbias)):
        p.dma("scalar", dst[:, :], src[:, :], inc=t.s_prep)
    p.dma("gpsimd", t.idb_sb[:, :], t.idf[:, :], inc=t.s_prep)
    p.dma("gpsimd", t.onescol_bf[:, :], t.onescol[:, :], inc=t.s_prep)
    t.eps_ln = t.cvec_sb[0:1, 0:1]
    t.eps_qk = t.cvec_sb[0:1, 1:2]

    if cfg.get("ffn1", True):
        with ExitStack() as fctx:
            emit_ffn_arena(p, t, nc, fctx, barrier=[])
            emit_ffn(p, t, t.xT, t.y1d, cfg.get("n_mb", 4), first_call=True,
                     rows_dst=t.y1rows)
    if cfg.get("attn", True):
        with ExitStack() as actx:
            emit_attn(p, t, nc, actx, cfg)
    if cfg.get("ffn2", True):
        with ExitStack() as fctx2:
            emit_ffn_arena(p, t, nc, fctx2, barrier=p.stores_done())
            emit_ffn(p, t, t.y2d, t.outT, cfg.get("n_mb", 4), first_call=False,
                     rows_src=t.y2rows)
    p.emit("vector", None, wait=p.stores_done())

    with nc.Block() as block:
        p.finalize(block)
    ctx.close()
    return nc


def load_cached(p, t, key, src_f32, cache, slot_ap, slot_sem, ww):
    if key not in t.wcache:
        p.dma("gpsimd", slot_ap, src_f32, wait=ww, inc=slot_sem)
        vld = slot_sem.n
        p.dma("sync", cache, slot_ap, wait=[(slot_sem, vld)], inc=slot_sem)
        t.wcache[key] = True
        return vld, [(slot_sem, slot_sem.n)]
    p.dma("gpsimd", slot_ap, cache, wait=ww, inc=slot_sem)
    return slot_sem.n, []


def emit_ffn_arena(p, t, nc, ctx, barrier):
    e = ctx.enter_context
    t.u_bf = e(nc.sbuf_tensor(_u("u_bf"), [128, KD * MB], BF16)).ap()
    t.y_sb = e(nc.sbuf_tensor(_u("y_sb"), [128, KD * MB], F32)).ap()
    t.h_bf = e(nc.sbuf_tensor(_u("h_bf"), [128, KF * MB], BF16)).ap()
    t.h_tmp = e(nc.sbuf_tensor(_u("h_tmp"), [128, MB], F32)).ap()
    t.w1buf = e(nc.sbuf_tensor(_u("w1buf"), [128, 3 * KD * 128], BF16)).ap()
    t.w2buf = e(nc.sbuf_tensor(_u("w2buf"), [128, 2 * KF * 128], BF16)).ap()
    t.g_bc = e(nc.sbuf_tensor(_u("g_bc"), [128, 2 * MB], BF16)).ap()
    t.xa = e(nc.sbuf_tensor(_u("xa"), [128, 4 * HF * 2], F32)).ap()
    t.scr = e(nc.sbuf_tensor(_u("scr"), [128, 2 * HF], F32)).ap()
    t.sel_sb = e(nc.sbuf_tensor(_u("sel_sb"), [16, 16 * 128], BF16)).ap()
    t.g_aug = e(nc.sbuf_tensor(_u("g_aug"), [NE, MB], BF16)).ap()
    t.b2_sb = e(nc.sbuf_tensor(_u("b2_sb"), [NE, D], BF16)).ap()
    t.scx_sb = e(nc.sbuf_tensor(_u("scx_sb"), [16, HF], F32)).ap()
    t.b1c_sb = e(nc.sbuf_tensor(_u("b1c_sb"), [128, NE * KF], F32)).ap()
    t.cent_sb = e(nc.sbuf_tensor(_u("cent_sb"), [128, KD * 16], F32)).ap()
    t.csum_sb = e(nc.sbuf_tensor(_u("csum_sb"), [16, 1], F32)).ap()
    t.sc_f32 = e(nc.sbuf_tensor(_u("sc_f32"), [16, MB], F32)).ap()
    t.sc_pre = e(nc.sbuf_tensor(_u("sc_pre"), [16, HF], F32)).ap()
    t.sc_tok = e(nc.sbuf_tensor(_u("sc_tok"), [128, 16], F32)).ap()
    t.gmax = e(nc.sbuf_tensor(_u("gmax"), [128, 8], F32)).ap()
    t.gpop = e(nc.sbuf_tensor(_u("gpop"), [128, 16], F32)).ap()
    t.gmsk = e(nc.sbuf_tensor(_u("gmsk"), [128, 16], F32)).ap()
    t.gsum = e(nc.sbuf_tensor(_u("gsum"), [128, 2], F32)).ap()
    t.gtok = e(nc.sbuf_tensor(_u("gtok"), [128, 16], F32)).ap()
    t.ybf = t.scr[:, 0:HF].bitcast(BF16)

    t.xa_rot = Rot(2, p, "xas")
    t.scr_rot = Rot(2, p, "scs")
    t.w1_rot = Rot(3, p, "w1s")
    t.w2_rot = Rot(2, p, "w2s")
    t.htmp_rot = Rot(1)
    t.gbc_rot = Rot(2)
    p.store_sems += t.scr_rot.sems
    for rot in (t.xa_rot, t.scr_rot, t.w1_rot, t.w2_rot, t.htmp_rot, t.gbc_rot):
        for sl in range(rot.n):
            rot.set_rel(sl, barrier)

    p.dma("scalar", t.b1c_sb[:, :], t.b1c[:, :], wait=barrier, inc=t.s_prep)
    p.dma("scalar", t.cent_sb[:, :], t.centp[:, :], inc=t.s_prep)
    p.dma("gpsimd", t.b2_sb[:, :], t.b2r[:, :], wait=barrier, inc=t.s_prep)
    p.dma("gpsimd", t.sel_sb[:, :], t.sel[:, :], inc=t.s_prep)
    pv = t.s_prep.n
    p.v("tensor_scalar_mul", t.b1c_sb[:, NRT * KF:NE * KF],
        t.b1c_sb[:, NRT * KF:NE * KF], 0.5, wait=[(t.s_prep, pv)], inc=t.s_prep2)
    p.v("memset", t.g_aug[0:NE, :], 0.5, inc=t.s_prep2)
    with nc.psum_tensor(_u("ps_csum"), [16, 1], F32) as ps_csum_h:
        ps_csum = ps_csum_h.ap()
        for k in range(KD):
            p.mm(ps_csum[:, :], t.cent_sb[:, k * 16:(k + 1) * 16], t.onescol_sb[:, :],
                 start=(k == 0), stop=(k == KD - 1),
                 wait=(([(t.s_prep, pv), (t.s_ev, t.s_ev.n), (t.s_oc, t.s_oc.n),
                         (t.s_pc, t.s_pc.n), (t.s_exp, t.s_exp.n)] + barrier)
                       if k == 0 else []),
                 inc=(t.s_pec if k == KD - 1 else None))
        p.v("tensor_copy", t.csum_sb[:, :], ps_csum[:, :],
            wait=[(t.s_pec, t.s_pec.n)], inc=t.s_prep2)



def emit_ffn(p, t, IN, OUT, n_mb, first_call, rows_src=None, rows_dst=None):
    nc = p.nc
    st = NS()
    with ExitStack() as ps:
        en = ps.enter_context
        st.hps = [en(nc.psum_tensor(_u(f"hps{i}"), [128, MB], F32)).ap()
                  for i in range(2)]
        st.yps = [en(nc.psum_tensor(_u(f"yps{i}"), [128, HF], F32)).ap()
                  for i in range(2)]
        st.mps = [en(nc.psum_tensor(_u(f"mps{i}"), [128, HF], F32)).ap()
                  for i in range(2)]
        g0 = [(t.s_ev, t.s_ev.n), (t.s_ya, t.s_ya.n), (t.s_u, t.s_u.n),
              (t.s_hd, t.s_hd.n), (t.s_rl, t.s_rl.n), (t.s_gt, t.s_gt.n),
              (t.s_row, t.s_row.n), (t.s_prep2, t.s_prep2.n),
              (t.s_oc, t.s_oc.n), (t.s_pc, t.s_pc.n), (t.s_exp, t.s_exp.n),
              (t.s_kv, t.s_kv.n), (t.s_qn, t.s_qn.n), (t.s_pn, t.s_pn.n)]
        st.hrel = [list(g0), list(g0)]
        st.yrel = [list(g0), list(g0)]
        st.mrel = [list(g0), list(g0)]
        st.rowv = [None, None]
        st.overlap = getattr(t, "overlap_stats", True)
        if first_call:
            _stats_compute(p, t, st, IN, 0, 0, parity=0)
        for mb in range(n_mb):
            _ffn_mb(p, t, st, IN, OUT, mb, n_mb, first_call, rows_src, rows_dst)
        if rows_dst is not None:
            mbl = n_mb - 1
            _stats_compute(p, t, st, OUT, mbl // 2, (mbl % 2) * MB, parity=None,
                           dram_rows=(rows_dst, mbl // 2, (mbl % 2) * MB),
                           wait0=[(s, s.n) for s in t.scr_rot.sems])


def _stats_compute(p, t, st, SRC, sq, c0, parity, dram_rows=None, wait0=()):
    srcr = SRC[sq].rearrange("(k pp) tt -> pp k tt", pp=128)
    first = True
    ybf_guard = [[(s, s.n) for s in t.scr_rot.sems]
                 + [(t.s_u, t.s_u.n), (t.s_ev, t.s_ev.n)]]
    rowv = []
    for h in range(2):
        hcol = c0 + h * HF
        sumv = st.mps[0][0:1, :]
        ssqv = st.mps[1][0:1, :]
        for kg in range(4):
            sl, w = t.xa_rot.acq()
            xsem = t.xa_rot.sems[sl]
            xgrp = t.xa[:, sl * 2048:(sl + 1) * 2048]
            p.dma("scalar", xgrp.rearrange("pp (k tt) -> pp k tt", k=4),
                  srcr[:, 4 * kg:4 * kg + 4, hcol:hcol + HF],
                  wait=w + (list(wait0) if first else []), inc=xsem)
            first = False
            ldv = xsem.n
            for kk in range(4):
                k = 4 * kg + kk
                xab = t.xa[:, sl * 2048 + kk * HF:sl * 2048 + (kk + 1) * HF]
                if STATS_BF16:
                    xc = t.ybf[:, 0:HF]
                    xq = t.ybf[:, HF:2 * HF]
                    p.v("tensor_copy", xc, xab,
                        wait=ybf_guard[0] + [(xsem, ldv)])
                    p.drain()
                    p.v("tensor_mul", xq, xc, xc, inc=t.s_sq)
                    ones = t.onescol_bf
                else:
                    xc = xab
                    xq = t.scr[:, 0:HF]
                    p.v("tensor_mul", xq, xab, xab,
                        wait=ybf_guard[0] + [(xsem, ldv)], inc=t.s_sq)
                    ones = t.onescol_sb
                sqv = t.s_sq.n
                p.mm(sumv, ones[:, :], xc, start=(k == 0),
                     stop=(k == KD - 1),
                     wait=((st.mrel[0] + st.mrel[1]) if k == 0 else [])
                     + [(t.s_sq, sqv), (xsem, ldv)], inc=t.s_pex)
                p.mm(ssqv, ones[:, :], xq, start=(k == 0),
                     stop=(k == KD - 1), inc=t.s_peq)
                ybf_guard[0] = [(t.s_peq, t.s_peq.n), (t.s_pex, t.s_pex.n)]
            t.xa_rot.set_rel(sl, [(t.s_pex, t.s_pex.n), (t.s_peq, t.s_peq.n)])
        mean = t.trow[0:1, 0:HF]
        e2 = t.trow[0:1, HF:2 * HF]
        tmp2 = t.trow[0:1, 2 * HF:3 * HF]
        wstat = [(t.s_pex, t.s_pex.n), (t.s_peq, t.s_peq.n)]
        p.v("tensor_scalar_mul", mean, sumv, 1.0 / D, wait=wstat)
        p.drain()
        p.v("tensor_scalar_mul", e2, ssqv, 1.0 / D)
        p.drain()
        p.v("tensor_mul", tmp2, mean, mean)
        p.drain()
        p.v("tensor_sub", e2, e2, tmp2, inc=t.s_row)
        vr0 = t.s_row.n
        st.mrel[0] = [(t.s_row, vr0)]
        st.mrel[1] = [(t.s_row, vr0)]
        p.drain()

        p.act(tmp2, e2, AF.Ln, scale=1.0, bias=t.eps_ln, wait=[(t.s_row, vr0)])
        p.drain("scalar")
        if parity is not None:
            rdst = t.r_row[0:1, parity * MB + h * HF:parity * MB + (h + 1) * HF]
            mrdst = t.mr_row[0:1, parity * MB + h * HF:parity * MB + (h + 1) * HF]
        else:
            rdst = t.sc_pre[0:1, 0:HF]
            mrdst = e2
        war = ([(t.s_pec, t.s_pec.n), (t.s_ev, t.s_ev.n)]
               if parity is not None else [])
        p.act(rdst, tmp2, AF.Exp, scale=-0.5, wait=war, inc=t.s_r)
        vr = t.s_r.n
        p.v("scalar_tensor_tensor", mrdst, mean, -1.0, rdst, ALU.mult, ALU.mult,
            wait=[(t.s_r, vr)], inc=t.s_row)
        vmr = t.s_row.n
        p.drain()
        if dram_rows is not None:
            rt, rsq, rc0 = dram_rows
            hc = rc0 + h * HF
            p.dma("sync", rt[rsq, 0:1, hc:hc + HF], rdst,
                  wait=[(t.s_r, vr)], inc=t.s_yrw)
            p.dma("sync", rt[rsq, 1:2, hc:hc + HF], mrdst,
                  wait=[(t.s_row, vmr)], inc=t.s_yrw)
        rowv.append(((t.s_r, vr), (t.s_row, vmr)))
    if parity is not None:
        st.rowv[parity] = rowv
        if getattr(t, "dbg", False) and not hasattr(t, "_dbgd"):
            t._dbgd = True
            p.dma("scalar", t.dbg_tr[0:1, 0:MB],
                  t.r_row[0:1, parity * MB:(parity + 1) * MB],
                  wait=[rowv[0][0], rowv[1][0]], inc=t.s_yrw)
            p.dma("scalar", t.dbg_tr[0:1, MB:2 * MB],
                  t.mr_row[0:1, parity * MB:(parity + 1) * MB],
                  wait=[rowv[0][1], rowv[1][1]], inc=t.s_yrw)
    t.scr_rot.rel[0] = tuple(list(t.scr_rot.rel[0])
                             + [(t.s_peq, t.s_peq.n)])


def _ffn_mb(p, t, st, IN, OUT, mb, n_mb, first_call, rows_src, rows_dst):
    nc = p.nc
    sq = mb // 2
    c0 = (mb % 2) * MB
    par = mb % 2

    if rows_src is not None:
        rsem = t.s_rwl[par]
        p.dma("scalar", t.r_row[0:1, par * MB:(par + 1) * MB],
              rows_src[sq, 0:1, c0:c0 + MB],
              wait=[(t.s_y2rw, t.s_y2rw.n), (t.s_row, t.s_row.n),
                    (t.s_pec, t.s_pec.n)], inc=rsem)
        p.dma("scalar", t.mr_row[0:1, par * MB:(par + 1) * MB],
              rows_src[sq, 1:2, c0:c0 + MB], inc=rsem)
        rv = (rsem, rsem.n)
        st.rowv[par] = [(rv, rv), (rv, rv)]
    rowv = st.rowv[par]

    inr = IN[sq].rearrange("(k pp) tt -> pp k tt", pp=128)
    for h in range(2):
        hcol = c0 + h * HF
        scxv = st.mps[0][0:16, :]
        for kg in range(4):
            sl, w = t.xa_rot.acq()
            xsem = t.xa_rot.sems[sl]
            xgrp = t.xa[:, sl * 2048:(sl + 1) * 2048]
            p.dma("scalar", xgrp.rearrange("pp (k tt) -> pp k tt", k=4),
                  inr[:, 4 * kg:4 * kg + 4, hcol:hcol + HF], wait=w, inc=xsem)
            ldv = xsem.n
            for kk in range(4):
                k = 4 * kg + kk
                xab = t.xa[:, sl * 2048 + kk * HF:sl * 2048 + (kk + 1) * HF]
                p.mm(scxv, t.cent_sb[:, k * 16:(k + 1) * 16], xab,
                     start=(k == 0), stop=(k == KD - 1),
                     wait=(st.mrel[0] if k == 0 else []) + [(xsem, ldv)],
                     inc=t.s_pec)
            t.xa_rot.set_rel(sl, [(t.s_pec, t.s_pec.n)])
        vscx = t.s_pec.n
        scx_sb = t.scx_sb[:, 0:HF]
        p.v("tensor_copy", scx_sb, scxv, wait=[(t.s_pec, vscx)], inc=t.s_gt)
        vcopy = t.s_gt.n
        p.drain()
        rh = t.r_row[0:1, par * MB + h * HF:par * MB + (h + 1) * HF]
        mrh = t.mr_row[0:1, par * MB + h * HF:par * MB + (h + 1) * HF]
        p.mm(st.mps[0][0:16, :], t.onesrow_sb[0:1, 0:16], rh, start=True,
             stop=True, wait=[(t.s_gt, vcopy), rowv[h][0]], inc=t.s_pec)
        v16a = t.s_pec.n
        p.mm(st.mps[1][0:16, :], t.onesrow_sb[0:1, 0:16], mrh, start=True,
             stop=True, wait=st.mrel[1] + [rowv[h][1]], inc=t.s_pec)
        v16b = t.s_pec.n
        p.v("tensor_mul", t.sc_pre[:, :], scx_sb, st.mps[0][0:16, :],
            wait=[(t.s_pec, v16a)])
        p.drain()
        p.v("scalar_tensor_tensor", t.sc_pre[:, :], st.mps[1][0:16, :],
            t.csum_sb[:, :], t.sc_pre[:, :], ALU.mult, ALU.add,
            wait=[(t.s_pec, v16b)], inc=t.s_row)
        vpre = t.s_row.n
        st.mrel[0] = [(t.s_row, vpre)]
        st.mrel[1] = [(t.s_row, vpre)]
        p.drain()
        scf = t.sc_f32[0:16, h * HF:(h + 1) * HF]
        p.act(scf, t.sc_pre[:, :], AF.Exp, scale=-1.0, wait=[(t.s_row, vpre)],
              inc=t.s_r)
        p.v("tensor_scalar_add", scf, scf, 1.0, wait=[(t.s_r, t.s_r.n)])
        p.drain()
        p.v("reciprocal", t.sc_pre[:, :], scf)
        p.drain()
        p.v("tensor_scalar_add", scf, t.sc_pre[:, :], t.rbias_sb[:, :],
            inc=t.s_scf)
        p.drain()
    scf_total = t.s_scf.n
    tp_rel = list(st.hrel[0])
    gt_rel = list(st.hrel[1])
    for tt in range(8):
        p.tr(st.hps[0][:, 0:16], t.sc_f32[0:16, tt * 128:(tt + 1) * 128],
             t.idf_sb[0:16, 0:16], wait=tp_rel + [(t.s_scf, scf_total)],
             inc=t.s_pec)
        vtp = t.s_pec.n
        stok = t.sc_tok[:, 0:16]
        p.v("tensor_copy", stok, st.hps[0][:, 0:16], wait=[(t.s_pec, vtp)],
            inc=t.s_gt)
        tp_rel = [(t.s_gt, t.s_gt.n)]
        p.drain()
        p.v("max", t.gmax[:, :], stok)
        p.drain()
        p.v("memset", t.gmax[:, 4:8], NEG)
        p.drain()
        p.v("match_replace", t.gpop[:, :], t.gmax[:, :], stok, NEG)
        p.drain()
        p.v("tensor_tensor", t.gmsk[:, :], stok, t.gpop[:, :], ALU.is_gt)
        p.drain()
        p.v("tensor_mul", t.gpop[:, :], stok, t.gmsk[:, :])
        p.drain()
        p.v("reduce_sum", t.gsum[:, 0:1], t.gpop[:, :], AX)
        p.drain()
        p.v("tensor_scalar_max", t.gsum[:, 0:1], t.gsum[:, 0:1], 1e-8)
        p.drain()
        p.v("reciprocal", t.gsum[:, 1:2], t.gsum[:, 0:1])
        p.drain()
        p.v("tensor_scalar", t.gtok[:, :], t.gpop[:, :], t.gsum[:, 1:2], 0.5,
            ALU.mult, ALU.mult, inc=t.s_gt)
        vg = t.s_gt.n
        p.drain()
        p.tr(st.hps[1][0:16, 0:128], t.gtok[:, :], t.idf_sb[:, :],
             wait=gt_rel + [(t.s_gt, vg)], inc=t.s_pec)
        vgt = t.s_pec.n
        p.v("tensor_copy", t.g_aug[0:16, tt * 128:(tt + 1) * 128],
            st.hps[1][0:16, 0:128], wait=[(t.s_pec, vgt)], inc=t.s_gt)
        gt_rel = [(t.s_gt, t.s_gt.n)]
        p.drain()
    gaug_done = t.s_gt.n
    st.hrel[0] = list(tp_rel)
    st.hrel[1] = list(gt_rel)
    if t.dbg and mb == 0:
        p.dma("sync", t.dbg_sc[:, :], t.sc_f32[0:16, :],
              wait=[(t.s_scf, scf_total)], inc=t.s_yrw)
        p.dma("sync", t.dbg_g[:, :], t.g_aug[0:NE, :],
              wait=[(t.s_gt, gaug_done)], inc=t.s_yrw)

    uvals = {}
    for h in range(2):
        hcol = c0 + h * HF
        rh = t.r_row[0:1, par * MB + h * HF:par * MB + (h + 1) * HF]
        mrh = t.mr_row[0:1, par * MB + h * HF:par * MB + (h + 1) * HF]
        p.mm(st.yps[0][:, :], t.onesrow_sb[0:1, :], rh, start=True, stop=True,
             wait=st.yrel[0] + [rowv[h][0]], inc=t.s_pec)
        vub = t.s_pec.n
        p.mm(st.yps[1][:, :], t.onesrow_sb[0:1, :], mrh, start=True, stop=True,
             wait=st.yrel[1] + [rowv[h][1]], inc=t.s_pec)
        vub2 = t.s_pec.n
        for kg in range(4):
            sl, w = t.xa_rot.acq()
            xsem = t.xa_rot.sems[sl]
            xgrp = t.xa[:, sl * 2048:(sl + 1) * 2048]
            p.dma("scalar", xgrp.rearrange("pp (k tt) -> pp k tt", k=4),
                  inr[:, 4 * kg:4 * kg + 4, hcol:hcol + HF], wait=w, inc=xsem)
            ldv = xsem.n
            for kk in range(4):
                k = 4 * kg + kk
                xab = t.xa[:, sl * 2048 + kk * HF:sl * 2048 + (kk + 1) * HF]
                ssl, sw = t.scr_rot.acq()
                tmp = t.scr[:, ssl * HF:(ssl + 1) * HF]
                p.v("tensor_mul", tmp, xab, st.yps[0][:, :],
                    wait=sw + [(xsem, ldv), (t.s_pec, vub2)])
                p.v("tensor_add",
                    t.u_bf[:, k * MB + h * HF:k * MB + h * HF + HF],
                    tmp, st.yps[1][:, :], inc=t.s_u)
                uvals[(h, k)] = t.s_u.n
                t.scr_rot.set_rel(ssl, [(t.s_u, t.s_u.n)])
            t.xa_rot.set_rel(sl, [(t.s_u, t.s_u.n)])
        st.yrel[0] = [(t.s_u, t.s_u.n)]
        st.yrel[1] = [(t.s_u, t.s_u.n)]
    if t.dbg and mb == 0:
        p.dma("sync", t.dbg_u[:, :], t.u_bf[:, :],
              wait=[(t.s_u, t.s_u.n)], inc=t.s_yrw)

    idx = 0
    for d in range(KD):
        for h in range(2):
            yp = st.yps[idx % 2]
            p.mm(yp[:, :], t.b2_sb[0:NE, d * 128:(d + 1) * 128],
                 t.g_aug[0:NE, h * HF:(h + 1) * HF], start=True, stop=True,
                 wait=st.yrel[idx % 2] + [(t.s_gt, gaug_done)], inc=t.s_p2)
            vp2 = t.s_p2.n
            p.v("tensor_copy", t.y_sb[:, d * MB + h * HF:d * MB + h * HF + HF],
                yp[:, :], wait=[(t.s_p2, vp2)], inc=t.s_ya)
            st.yrel[idx % 2] = [(t.s_ya, t.s_ya.n)]
            idx += 1

    inj_x = first_call and (mb + 1 < n_mb) and st.overlap
    inj_y = first_call and (mb >= 1) and st.overlap
    if first_call and not st.overlap and mb + 1 < n_mb:
        _stats_compute(p, t, st, IN, (mb + 1) // 2, ((mb + 1) % 2) * MB,
                       parity=(mb + 1) % 2)
    if first_call and not st.overlap and mb >= 1 and rows_dst is not None:
        _stats_compute(p, t, st, OUT, (mb - 1) // 2, ((mb - 1) % 2) * MB,
                       parity=None,
                       dram_rows=(rows_dst, (mb - 1) // 2, ((mb - 1) % 2) * MB),
                       wait0=[(s, s.n) for s in t.scr_rot.sems])
    for ei, e_ in enumerate(EXPERT_ORDER):
        if ei == 5 and inj_x:
            _stats_compute(p, t, st, IN, (mb + 1) // 2, ((mb + 1) % 2) * MB,
                           parity=(mb + 1) % 2)
        if ei == 10 and inj_y:
            _stats_compute(p, t, st, OUT, (mb - 1) // 2, ((mb - 1) % 2) * MB,
                           parity=None,
                           dram_rows=(rows_dst, (mb - 1) // 2,
                                      ((mb - 1) % 2) * MB),
                           wait0=[(s, s.n) for s in t.scr_rot.sems])
        routed = e_ < NRT
        if routed:
            gsl, gw = t.gbc_rot.acq()
            for h in range(2):
                p.mm(st.hps[1][:, h * HF:(h + 1) * HF],
                     t.sel_sb[0:16, e_ * 128:(e_ + 1) * 128],
                     t.g_aug[0:16, h * HF:(h + 1) * HF], start=True, stop=True,
                     wait=st.hrel[1] + gw + [(t.s_gt, gaug_done)],
                     inc=(t.s_pec if h == 1 else None))
            vsel = t.s_pec.n
            gslice = t.g_bc[:, gsl * MB:(gsl + 1) * MB]
            p.v("tensor_copy", gslice, st.hps[1][:, :], wait=[(t.s_pec, vsel)],
                inc=t.s_gb)
            vgb = t.s_gb.n
            st.hrel[1] = [(t.s_gb, vgb)]
        hd_val = []
        for c in range(KF):
            pr = c % 2
            hp = st.hps[pr]
            wsl, ww = t.w1_rot.acq()
            wsem = t.w1_rot.sems[wsl]
            w1s = t.w1buf[:, wsl * 2048:(wsl + 1) * 2048]
            vld, wrel = load_cached(p, t, ("w1", e_, c), t.w1p[e_, c],
                                    t.w1b[e_, c], w1s, wsem, ww)
            for h in range(2):
                for k in range(KD):
                    wl = [(t.s_u, uvals[(h, k)])]
                    if h == 0 and k == 0:
                        wl = st.hrel[pr] + [(wsem, vld)] + wl
                    p.mm(hp[:, h * HF:(h + 1) * HF],
                         w1s[:, k * 128:(k + 1) * 128],
                         t.u_bf[:, k * MB + h * HF:k * MB + h * HF + HF],
                         start=(k == 0), stop=(k == KD - 1), wait=wl,
                         inc=(t.s_p1 if (h == 1 and k == KD - 1) else None))
            vp1 = t.s_p1.n
            t.w1_rot.set_rel(wsl, [(t.s_p1, vp1)] + wrel)
            bcol = t.b1c_sb[:, e_ * KF + c:e_ * KF + c + 1]
            if routed:
                hsl, hw = t.htmp_rot.acq()
                htt = t.h_tmp[:, :]
                p.act(htt, hp[:, :], AF.Relu, scale=1.0, bias=bcol,
                      wait=hw + [(t.s_p1, vp1)], inc=t.s_rl)
                vrl = t.s_rl.n
                st.hrel[pr] = [(t.s_rl, vrl)]
                p.v("tensor_mul", t.h_bf[:, c * MB:(c + 1) * MB], htt, gslice,
                    wait=[(t.s_rl, vrl), (t.s_gb, vgb)], inc=t.s_hd)
                t.htmp_rot.set_rel(hsl, [(t.s_hd, t.s_hd.n)])
            else:
                p.act(t.h_bf[:, c * MB:(c + 1) * MB], hp[:, :], AF.Relu,
                      scale=0.5, bias=bcol, wait=[(t.s_p1, vp1)], inc=t.s_hd)
                st.hrel[pr] = [(t.s_hd, t.s_hd.n)]
            hd_val.append(t.s_hd.n)
        for d in range(KD):
            wsl2, ww2 = t.w2_rot.acq()
            wsem2 = t.w2_rot.sems[wsl2]
            w2s = t.w2buf[:, wsl2 * 1408:(wsl2 + 1) * 1408]
            vld2, wrel2 = load_cached(p, t, ("w2", e_, d), t.w2p[e_, d],
                                      t.w2b[e_, d], w2s, wsem2, ww2)
            for h in range(2):
                yp = st.yps[idx % 2]
                for k in range(KF):
                    wl = [(t.s_hd, hd_val[k])]
                    if k == 0:
                        wl = st.yrel[idx % 2] + [(wsem2, vld2)] + wl
                    p.mm(yp[:, :], w2s[:, k * 128:(k + 1) * 128],
                         t.h_bf[:, k * MB + h * HF:k * MB + h * HF + HF],
                         start=(k == 0), stop=(k == KF - 1), wait=wl,
                         inc=(t.s_p2 if k == KF - 1 else None))
                vp2 = t.s_p2.n
                p.v("tensor_add",
                    t.y_sb[:, d * MB + h * HF:d * MB + h * HF + HF],
                    t.y_sb[:, d * MB + h * HF:d * MB + h * HF + HF],
                    yp[:, :], wait=[(t.s_p2, vp2)], inc=t.s_ya)
                st.yrel[idx % 2] = [(t.s_ya, t.s_ya.n)]
                idx += 1
            t.w2_rot.set_rel(wsl2, [(t.s_p2, t.s_p2.n)] + wrel2)
        if routed:
            t.gbc_rot.set_rel(gsl, [(t.s_hd, t.s_hd.n)])

    if t.dbg and mb == 0:
        p.dma("sync", t.dbg_y[:, :], t.y_sb[:, :],
              wait=[(t.s_ya, t.s_ya.n)], inc=t.s_yrw)

    evg = [(t.s_ya, t.s_ya.n), (t.s_hd, t.s_hd.n), (t.s_rl, t.s_rl.n),
           (t.s_gb, t.s_gb.n)]
    for h in range(2):
        hcol = c0 + h * HF
        c1r = t.trow[0:1, 0:HF]
        c0r = t.trow[0:1, HF:2 * HF]
        rh = t.r_row[0:1, par * MB + h * HF:par * MB + (h + 1) * HF]
        mrh = t.mr_row[0:1, par * MB + h * HF:par * MB + (h + 1) * HF]
        p.v("tensor_scalar", c1r, rh, 0.5, 1.0, ALU.mult, ALU.add,
            wait=[(t.s_r, t.s_r.n), (t.s_row, t.s_row.n)])
        p.drain()
        p.v("tensor_scalar_mul", c0r, mrh, 0.5, inc=t.s_row)
        vcr = t.s_row.n
        p.drain()
        p.mm(st.mps[0][:, :], t.onesrow_sb[0:1, :], c1r, start=True, stop=True,
             wait=evg + st.mrel[0] + [(t.s_row, vcr)], inc=t.s_pec)
        vb1 = t.s_pec.n
        p.mm(st.mps[1][:, :], t.onesrow_sb[0:1, :], c0r, start=True, stop=True,
             wait=st.mrel[1], inc=t.s_pec)
        vb0 = t.s_pec.n
        for dg in range(4):
            sl, w = t.xa_rot.acq()
            xsem = t.xa_rot.sems[sl]
            xgrp = t.xa[:, sl * 2048:(sl + 1) * 2048]
            p.dma("scalar", xgrp.rearrange("pp (k tt) -> pp k tt", k=4),
                  inr[:, 4 * dg:4 * dg + 4, hcol:hcol + HF], wait=w, inc=xsem)
            ldv = xsem.n
            for dd in range(4):
                d = 4 * dg + dd
                xab = t.xa[:, sl * 2048 + dd * HF:sl * 2048 + (dd + 1) * HF]
                ssl, sw = t.scr_rot.acq()
                ssem = t.scr_rot.sems[ssl]
                stg = t.scr[:, ssl * HF:(ssl + 1) * HF]
                tmp1 = t.h_tmp[:, (d % 2) * HF:(d % 2) * HF + HF]
                p.v("tensor_mul", tmp1, xab, st.mps[0][:, :],
                    wait=[(xsem, ldv), (t.s_pec, vb1), (t.s_hd, t.s_hd.n)])
                p.v("tensor_add", tmp1, tmp1, st.mps[1][:, :],
                    wait=[(t.s_pec, vb0)])
                p.v("tensor_add", stg, tmp1,
                    t.y_sb[:, d * MB + h * HF:d * MB + h * HF + HF],
                    wait=sw, inc=t.s_ev)
                vev = t.s_ev.n
                p.dma("sync", OUT[sq, d * 128:(d + 1) * 128, hcol:hcol + HF],
                      stg, wait=[(t.s_ev, vev)], inc=ssem)
                t.scr_rot.set_rel(ssl, [(ssem, ssem.n)])
            t.xa_rot.set_rel(sl, [(t.s_ev, t.s_ev.n)])
        st.mrel[0] = [(t.s_ev, t.s_ev.n)]
        st.mrel[1] = [(t.s_ev, t.s_ev.n)]
    t.htmp_rot.set_rel(0, [(t.s_ev, t.s_ev.n)])


def emit_attn(p, t, nc, ctx, cfg):
    e = ctx.enter_context
    bar = p.stores_done()
    pe_bar = bar + [(t.s_ev, t.s_ev.n), (t.s_ya, t.s_ya.n), (t.s_u, t.s_u.n),
                    (t.s_hd, t.s_hd.n), (t.s_rl, t.s_rl.n), (t.s_gt, t.s_gt.n),
                    (t.s_scf, t.s_scf.n)]
    t.KT = e(nc.sbuf_tensor("KT", [128, NKV * T], BF16)).ap()
    t.V = e(nc.sbuf_tensor("V", [128, 16 * 512], BF16)).ap()
    t.u2 = e(nc.sbuf_tensor("u2", [128, KD * 512], BF16)).ap()
    t.ya = e(nc.sbuf_tensor("ya", [128, 2 * 4 * HF], F32)).ap()
    t.stg2 = e(nc.sbuf_tensor("stg2", [128, 2 * HF], F32)).ap()
    t.sq2 = e(nc.sbuf_tensor("sq2", [128, 2 * HF], F32)).ap()
    t.probs = e(nc.sbuf_tensor("probs", [128, 2 * T], BF16)).ap()
    t.probsT = e(nc.sbuf_tensor("probsT", [128, T], BF16)).ap()
    t.attnT = e(nc.sbuf_tensor("attnT", [128, NH * 512], BF16)).ap()
    t.abf = e(nc.sbuf_tensor("abf", [128, 4 * HF], BF16)).ap()
    t.mxc = e(nc.sbuf_tensor("mxc", [128, 2], F32)).ap()
    t.srec = e(nc.sbuf_tensor("srec", [128, 4], F32)).ap()
    ya_rot = Rot(2, p, "yas")
    st2_rot = Rot(2, p, "st2s")
    sq2_rot = Rot(2)
    s_wkv = p.sem(_u("wkv"))
    s_wq = p.sem(_u("wq"))
    p.store_sems += st2_rot.sems
    for rot in (ya_rot, st2_rot, sq2_rot):
        for sl in range(rot.n):
            rot.set_rel(sl, bar)

    def _rows(sum_ps, ssq_ps, off):
        mean = t.trow[0:1, 0:HF]
        e2 = t.trow[0:1, HF:2 * HF]
        msq = t.trow[0:1, 2 * HF:3 * HF]
        rh = t.r_row[0:1, off:off + HF]
        mrh = t.mr_row[0:1, off:off + HF]
        wstat = [(t.s_pex, t.s_pex.n), (t.s_peq, t.s_peq.n)]
        p.v("tensor_scalar_mul", mean, sum_ps[:, :], 1.0 / D, wait=wstat)
        p.drain()
        p.v("tensor_scalar_mul", e2, ssq_ps[:, :], 1.0 / D)
        p.drain()
        p.v("tensor_mul", msq, mean, mean)
        p.drain()
        p.v("tensor_sub", e2, e2, msq, inc=t.s_row)
        vrow = t.s_row.n
        p.drain()
        lnv = t.trow[0:1, 2 * HF:3 * HF]
        p.act(lnv, e2, AF.Ln, scale=1.0, bias=t.eps_ln, wait=[(t.s_row, vrow)])
        p.drain("scalar")
        p.act(rh, lnv, AF.Exp, scale=-0.5, inc=t.s_r)
        vr = t.s_r.n
        p.v("scalar_tensor_tensor", mrh, mean, -1.0, rh, ALU.mult, ALU.mult,
            wait=[(t.s_r, vr)], inc=t.s_row)
        p.drain()
        return t.s_row.n

    def _u2_group(s_i, gc, rb_ps, mrb_ps, vrows, extra_pe_wait):
        p.mm(rb_ps[:, :], t.onesrow_sb[0:1, :], t.r_row[0:1, gc:gc + HF],
             start=True, stop=True,
             wait=extra_pe_wait + [(t.s_arow, vrows), (t.s_u, t.s_u.n)],
             inc=t.s_pec)
        p.mm(mrb_ps[:, :], t.onesrow_sb[0:1, :], t.mr_row[0:1, gc:gc + HF],
             start=True, stop=True, inc=t.s_pec)
        vub = t.s_pec.n
        y1r = t.y1d[s_i].rearrange("(k pp) tt -> pp k tt", pp=128)
        for kg in range(4):
            sl, w = ya_rot.acq()
            ysem = ya_rot.sems[sl]
            ygrp = t.ya[:, sl * 2048:(sl + 1) * 2048]
            p.dma("scalar", ygrp.rearrange("pp (k tt) -> pp k tt", k=4),
                  y1r[:, 4 * kg:4 * kg + 4, gc:gc + HF], wait=w, inc=ysem)
            ldv = ysem.n
            for kk in range(4):
                k = 4 * kg + kk
                yab = t.ya[:, sl * 2048 + kk * HF:sl * 2048 + (kk + 1) * HF]
                ssl, sw = sq2_rot.acq()
                tmp = t.sq2[:, ssl * HF:(ssl + 1) * HF]
                p.v("tensor_mul", tmp, yab, rb_ps[:, :],
                    wait=sw + [(ysem, ldv), (t.s_pec, vub), (t.s_p1, t.s_p1.n)])
                p.v("tensor_add", t.u2[:, k * HF:(k + 1) * HF], tmp,
                    mrb_ps[:, :], inc=t.s_u)
                uv = t.s_u.n
                sq2_rot.set_rel(ssl, [(t.s_u, uv)])
            ya_rot.set_rel(sl, [(t.s_u, t.s_u.n)])
        return t.s_u.n

    for s_i in range(2):
        p.dma("scalar", t.r_row[0:1, :], t.y1rows[s_i, 0:1, :],
              wait=[(t.s_yrw, t.s_yrw.n), (t.s_pec, t.s_pec.n),
                    (t.s_row, t.s_row.n), (t.s_ev, t.s_ev.n)], inc=t.s_arow)
        p.dma("scalar", t.mr_row[0:1, :], t.y1rows[s_i, 1:2, :], inc=t.s_arow)
        vrows_ld = t.s_arow.n

        with ExitStack() as ps:
            en = ps.enter_context
            wk_sb = en(nc.sbuf_tensor(_u("wk_sb"), [128, KD * 512], BF16)).ap()
            wv_sb = en(nc.sbuf_tensor(_u("wv_sb"), [128, KD * 512], BF16)).ap()
            kraw = en(nc.sbuf_tensor(_u("kraw"), [128, HF], F32)).ap()
            ksq = en(nc.sbuf_tensor(_u("ksq"), [128, HF], F32)).ap()
            rb_ps = en(nc.psum_tensor(_u("b_rb"), [128, HF], F32)).ap()
            mrb_ps = en(nc.psum_tensor(_u("b_mrb"), [128, HF], F32)).ap()
            kv_ps = [en(nc.psum_tensor(_u(f"b_kv{i}"), [128, HF], F32)).ap()
                     for i in range(2)]
            kn_ps = en(nc.psum_tensor(_u("b_kn"), [1, HF], F32)).ap()
            knb_ps = en(nc.psum_tensor(_u("b_knb"), [128, HF], F32)).ap()
            p.dma("gpsimd", wk_sb[:, :], t.wkp[:, :], wait=bar, inc=s_wkv)
            p.dma("gpsimd", wv_sb[:, :], t.wvp[:, :], inc=s_wkv)
            vwk = s_wkv.n
            kvrel = [pe_bar, pe_bar]
            for grp in range(4):
                gc = grp * HF
                uv = _u2_group(s_i, gc, rb_ps, mrb_ps, vrows_ld, pe_bar)
                pend_k = [None]
                for c in range(NKV):
                    pr = c % 2
                    for k in range(KD):
                        p.mm(kv_ps[pr][:, :],
                             wk_sb[:, k * 512 + c * 128:k * 512 + c * 128 + 128],
                             t.u2[:, k * HF:(k + 1) * HF],
                             start=(k == 0), stop=(k == KD - 1),
                             wait=(kvrel[pr] + [(s_wkv, vwk), (t.s_u, uv)]
                                   if k == 0 else []),
                             inc=(t.s_p1 if k == KD - 1 else None))
                    vk = t.s_p1.n
                    if pend_k[0] is not None:
                        pend_k[0]()

                    def make_kn(c=c, pr=pr, vk=vk, gc=gc):
                        def run():
                            p.v("tensor_copy", kraw[:, :], kv_ps[pr][:, :],
                                wait=[(t.s_p1, vk), (t.s_kv, t.s_kv.n)],
                                inc=t.s_kv)
                            kvrel[pr] = [(t.s_kv, t.s_kv.n)]
                            p.drain()
                            p.v("tensor_mul", ksq[:, :], kraw[:, :], kraw[:, :],
                                inc=t.s_sq)
                            vsq = t.s_sq.n
                            p.mm(kn_ps[:, :], t.onescol_sb[:, :], ksq[:, :],
                                 start=True, stop=True,
                                 wait=[(t.s_sq, vsq), (t.s_kv, t.s_kv.n)],
                                 inc=t.s_pec)
                            vkn = t.s_pec.n
                            kr0 = t.trow[0:1, 0:HF]
                            kr1 = t.trow[0:1, HF:2 * HF]
                            p.act(kr0, kn_ps[:, :], AF.Ln, scale=1.0 / 128.0,
                                  bias=t.eps_qk, wait=[(t.s_pec, vkn)])
                            p.drain("scalar")
                            p.act(kr1, kr0, AF.Exp, scale=-0.5, inc=t.s_r)
                            vr2 = t.s_r.n
                            p.mm(knb_ps[:, :], t.onesrow_sb[0:1, :], kr1,
                                 start=True, stop=True,
                                 wait=[(t.s_r, vr2), (t.s_kv, t.s_kv.n)],
                                 inc=t.s_pec)
                            vknb = t.s_pec.n
                            p.v("scalar_tensor_tensor",
                                t.KT[:, c * T + gc:c * T + gc + HF],
                                kraw[:, :], t.ksc_sb[:, :], knb_ps[:, :],
                                ALU.mult, ALU.mult,
                                wait=[(t.s_pec, vknb)], inc=t.s_kv)
                            p.drain()
                        return run
                    pend_k[0] = make_kn()
                pend_k[0]()
                for tt in range(4):
                    pr = tt % 2
                    for k in range(KD):
                        p.mm(kv_ps[pr][:, :],
                             t.u2[:, k * HF + tt * 128:k * HF + tt * 128 + 128],
                             wv_sb[:, k * 512:(k + 1) * 512],
                             start=(k == 0), stop=(k == KD - 1),
                             wait=(kvrel[pr] if k == 0 else []),
                             inc=(t.s_p1 if k == KD - 1 else None))
                    vv = t.s_p1.n
                    p.v("tensor_copy",
                        t.V[:, (grp * 4 + tt) * 512:(grp * 4 + tt) * 512 + 512],
                        kv_ps[pr][:, :], wait=[(t.s_p1, vv)], inc=t.s_kv)
                    kvrel[pr] = [(t.s_kv, t.s_kv.n)]
            b2_p1 = t.s_p1.n
            b2_kv = t.s_kv.n

        with ExitStack() as ps:
            en = ps.enter_context
            wq_sb = en(nc.sbuf_tensor(_u("wq_sb"), [128, 2 * 2048], BF16)).ap()
            wq_rot = Rot(2, p, "wqs")
            q4 = en(nc.sbuf_tensor(_u("q4"), [128, NH * 512], BF16)).ap()
            qraw = en(nc.sbuf_tensor(_u("qraw"), [128, HF], F32)).ap()
            qsq = en(nc.sbuf_tensor(_u("qsq"), [128, HF], F32)).ap()
            wo_buf = en(nc.sbuf_tensor(_u("wo_buf"), [128, 2 * 2048], BF16)).ap()
            wo_rot = Rot(2, p, "wos")
            for sl in range(2):
                wo_rot.set_rel(sl, [(t.s_p1, b2_p1), (t.s_kv, b2_kv)])
                wq_rot.set_rel(sl, [(t.s_p1, b2_p1), (t.s_kv, b2_kv)])
            oc_prev_grp = t.s_oc.n
            for grp in range(4):
                gc = grp * HF
                with ExitStack() as p2:
                    en2 = p2.enter_context
                    rb_ps = en2(nc.psum_tensor(_u("c_rb"), [128, HF], F32)).ap()
                    mrb_ps = en2(nc.psum_tensor(_u("c_mrb"), [128, HF], F32)).ap()
                    q_ps = [en2(nc.psum_tensor(_u(f"c_q{i}"), [128, HF], F32)).ap()
                            for i in range(2)]
                    qn_ps = en2(nc.psum_tensor(_u("c_qn"), [1, HF], F32)).ap()
                    qnb_ps = en2(nc.psum_tensor(_u("c_qnb"), [128, HF], F32)).ap()
                    ga = [(t.s_ev, t.s_ev.n), (t.s_kv, t.s_kv.n),
                          (t.s_qn, t.s_qn.n)]
                    uv = _u2_group(s_i, gc, rb_ps, mrb_ps, vrows_ld, ga)
                    qrel = [ga, ga]
                    pend_q = [None]
                    for m in range(NH):
                        pr = m % 2
                        qwsl, qww = wq_rot.acq()
                        qwsem = wq_rot.sems[qwsl]
                        wqs = wq_sb[:, qwsl * 2048:(qwsl + 1) * 2048]
                        vldq, qwrel = load_cached(p, t, ("wq", m), t.wqp[m],
                                                  t.wqb[m], wqs, qwsem, qww)
                        for k in range(KD):
                            p.mm(q_ps[pr][:, :],
                                 wqs[:, k * 128:(k + 1) * 128],
                                 t.u2[:, k * HF:(k + 1) * HF],
                                 start=(k == 0), stop=(k == KD - 1),
                                 wait=(qrel[pr] + [(qwsem, vldq), (t.s_u, uv)]
                                       if k == 0 else []),
                                 inc=(t.s_p1 if k == KD - 1 else None))
                        vq = t.s_p1.n
                        wq_rot.set_rel(qwsl, [(t.s_p1, vq)] + qwrel)
                        if pend_q[0] is not None:
                            pend_q[0]()

                        def make_qn(m=m, pr=pr, vq=vq):
                            def run():
                                p.v("tensor_copy", qraw[:, :], q_ps[pr][:, :],
                                    wait=[(t.s_p1, vq), (t.s_qn, t.s_qn.n)],
                                    inc=t.s_kv)
                                qrel[pr] = [(t.s_kv, t.s_kv.n)]
                                p.drain()
                                p.v("tensor_mul", qsq[:, :], qraw[:, :],
                                    qraw[:, :], inc=t.s_sq)
                                vsq = t.s_sq.n
                                p.mm(qn_ps[:, :], t.onescol_sb[:, :], qsq[:, :],
                                     start=True, stop=True,
                                     wait=[(t.s_sq, vsq), (t.s_qn, t.s_qn.n)],
                                     inc=t.s_pec)
                                vqn = t.s_pec.n
                                qr0 = t.trow[0:1, 0:HF]
                                qr1 = t.trow[0:1, HF:2 * HF]
                                p.act(qr0, qn_ps[:, :], AF.Ln, scale=1.0 / 128.0,
                                      bias=t.eps_qk, wait=[(t.s_pec, vqn)])
                                p.drain("scalar")
                                p.act(qr1, qr0, AF.Exp, scale=-0.5, inc=t.s_r)
                                vr2 = t.s_r.n
                                p.mm(qnb_ps[:, :], t.onesrow_sb[0:1, :], qr1,
                                     start=True, stop=True,
                                     wait=[(t.s_r, vr2), (t.s_qn, t.s_qn.n)],
                                     inc=t.s_pec)
                                vqnb = t.s_pec.n
                                p.v("scalar_tensor_tensor",
                                    q4[:, m * 512 + 0:m * 512 + 512],
                                    qraw[:, :], t.qsc_sb[:, :], qnb_ps[:, :],
                                    ALU.mult, ALU.mult,
                                    wait=[(t.s_pec, vqnb)], inc=t.s_qn)
                                p.drain()
                            return run
                        pend_q[0] = make_qn()
                    pend_q[0]()
                    q4_done = t.s_qn.n
                with ExitStack() as p2:
                    en2 = p2.enter_context
                    R = en2(nc.psum_tensor(_u("c_R"), [128, 5 * HF], F32)).ap()
                    pT_ps = [en2(nc.psum_tensor(_u(f"c_pT{i}"), [128, 512], BF16)).ap()
                             for i in range(2)]
                    out_ps = en2(nc.psum_tensor(_u("c_out"), [128, 128], F32)).ap()
                    ga0 = [(t.s_qn, q4_done), (t.s_kv, t.s_kv.n)]
                    unit_rel = [list(ga0) for _ in range(5)]
                    pT_rel = [list(ga0), list(ga0)]
                    out_rel = [list(ga0)]
                    pp_rel = [[], []]
                    upos = 0
                    pend_tm = [None]
                    for tl in range(4):
                        ti = grp * 4 + tl
                        nch = ti + 1
                        tsl = nch * 128
                        nun = (tsl + HF - 1) // HF
                        for hh in range(NH):
                            kv = hh // NKV
                            pp = hh % 2
                            units = [(upos + i) % 5 for i in range(nun)]
                            upos = (upos + nun) % 5
                            segs = []
                            sb = 0
                            for i in range(1, nun + 1):
                                if i == nun or units[i] != units[i - 1] + 1:
                                    segs.append((sb, i))
                                    sb = i
                            for b, u in enumerate(units):
                                nb = min(HF, tsl - b * HF)
                                p.mm(R[:, u * HF:u * HF + nb],
                                     q4[:, hh * 512 + tl * 128:hh * 512 + tl * 128 + 128],
                                     t.KT[:, kv * T + b * HF:kv * T + b * HF + nb],
                                     start=True, stop=True,
                                     wait=unit_rel[u] + ([(t.s_qn, q4_done)]
                                                         if b == 0 else []),
                                     inc=(t.s_sc2 if b == nun - 1 else None))
                            vsc = t.s_sc2.n
                            bd = (ti * 128) // HF
                            ud = units[bd]
                            off = (ti * 128) % HF
                            p.v("tensor_add", R[:, ud * HF + off:ud * HF + off + 128],
                                R[:, ud * HF + off:ud * HF + off + 128],
                                t.causal_sb[:, :], wait=[(t.s_sc2, vsc)])
                            p.drain()
                            for si, (a, b_) in enumerate(segs):
                                u0 = units[a]
                                ln = min((b_ - a) * HF, tsl - a * HF)
                                p.v("reduce_max", t.mxc[:, si:si + 1],
                                    R[:, u0 * HF:u0 * HF + ln], AX, negate=True)
                                p.drain()
                            if len(segs) == 2:
                                p.v("tensor_tensor", t.mxc[:, 0:1], t.mxc[:, 0:1],
                                    t.mxc[:, 1:2], ALU.min)
                                p.drain()
                            p.v("tensor_scalar_mul", t.mxc[:, 1:2], t.mxc[:, 0:1],
                                SCL, inc=t.s_mx)
                            vmx = t.s_mx.n
                            for si, (a, b_) in enumerate(segs):
                                u0 = units[a]
                                ln = min((b_ - a) * HF, tsl - a * HF)
                                p.act(t.probs[:, pp * T + a * HF:pp * T + a * HF + ln],
                                      R[:, u0 * HF:u0 * HF + ln], AF.Exp,
                                      scale=SCL, bias=t.mxc[:, 1:2],
                                      wait=(([(t.s_mx, vmx)] + pp_rel[pp])
                                            if si == 0 else []),
                                      inc=t.s_exp, acc=t.srec[:, 2 + si:3 + si])
                                self_rel = [(t.s_exp, t.s_exp.n)]
                                for u in units[a:b_]:
                                    unit_rel[u] = self_rel
                            vexp = t.s_exp.n
                            prb = t.probs[:, pp * T:pp * T + tsl]
                            if len(segs) == 2:
                                p.v("tensor_add", t.srec[:, 2:3], t.srec[:, 2:3],
                                    t.srec[:, 3:4], wait=[(t.s_exp, vexp)])
                                p.drain()
                            p.v("reciprocal", t.srec[:, 1:2], t.srec[:, 2:3],
                                wait=[(t.s_exp, vexp)])
                            p.drain()
                            p.v("tensor_scalar_mul", prb, prb, t.srec[:, 1:2],
                                inc=t.s_pn)
                            vpn = t.s_pn.n
                            if pend_tm[0] is not None:
                                pend_tm[0]()

                            def make_tm(nch=nch, pp=pp, kv=kv, tl=tl, hh=hh,
                                        vpn=vpn):
                                def run():
                                    nbt = (nch + 3) // 4
                                    vpc_b = {}

                                    def emit_tbatch(g):
                                        j0, j1 = 4 * g, min(4 * g + 4, nch)
                                        for j in range(j0, j1):
                                            p.tr(pT_ps[g % 2][:, (j - j0) * 128:(j - j0 + 1) * 128],
                                                 t.probs[:, pp * T + j * 128:pp * T + (j + 1) * 128],
                                                 t.idb_sb[:, :],
                                                 wait=(pT_rel[g % 2] + [(t.s_pn, vpn)]
                                                       if j == j0 else []),
                                                 inc=(t.s_pec if j == j1 - 1 else None))
                                        vtr = t.s_pec.n
                                        p.acopy(t.probsT[:, j0 * 128:j1 * 128],
                                                pT_ps[g % 2][:, 0:(j1 - j0) * 128],
                                                wait=[(t.s_pec, vtr)], inc=t.s_pc)
                                        vpc_b[g] = t.s_pc.n
                                        pT_rel[g % 2] = [(t.s_pc, vpc_b[g])]

                                    def emit_mbatch(g):
                                        j0, j1 = 4 * g, min(4 * g + 4, nch)
                                        for j in range(j0, j1):
                                            p.mm(out_ps[:, :],
                                                 t.V[:, j * 512 + kv * 128:j * 512 + kv * 128 + 128],
                                                 t.probsT[:, j * 128:(j + 1) * 128],
                                                 start=(j == 0), stop=(j == nch - 1),
                                                 wait=(([(t.s_pc, vpc_b[g])] +
                                                        (out_rel[0] if j == 0 else []))
                                                       if j == j0 else []),
                                                 inc=(t.s_out if j == nch - 1 else None))

                                    emit_tbatch(0)
                                    for g in range(1, nbt):
                                        emit_tbatch(g)
                                        emit_mbatch(g - 1)
                                    emit_mbatch(nbt - 1)
                                    pp_rel[pp] = [(t.s_pec, t.s_pec.n)]
                                    vout = t.s_out.n
                                    p.acopy(t.attnT[:, hh * 512 + tl * 128:hh * 512 + tl * 128 + 128],
                                            out_ps[:, :],
                                            wait=[(t.s_out, vout), (t.s_p2, t.s_p2.n)],
                                            inc=t.s_oc)
                                    out_rel[0] = [(t.s_oc, t.s_oc.n)]
                                return run
                            pend_tm[0] = make_tm()
                    pend_tm[0]()
                    grp_oc = t.s_oc.n
                with ExitStack() as p2:
                    en2 = p2.enter_context
                    y2_ps = [en2(nc.psum_tensor(_u(f"c_y2{i}"), [128, HF], F32)).ap()
                             for i in range(2)]
                    ysum_ps = en2(nc.psum_tensor(_u("c_ysum"), [1, HF], F32)).ap()
                    yssq_ps = en2(nc.psum_tensor(_u("c_yssq"), [1, HF], F32)).ap()
                    y2rel = [[(t.s_oc, grp_oc)], [(t.s_oc, grp_oc)]]
                    abf_guard = [[], []]
                    y1r = t.y1d[s_i].rearrange("(k pp) tt -> pp k tt", pp=128)
                    for d in range(KD):
                        pr = d % 2
                        if d % 4 == 0:
                            ysl, yw = ya_rot.acq()
                            ysem = ya_rot.sems[ysl]
                            ygrp = t.ya[:, ysl * 2048:(ysl + 1) * 2048]
                            p.dma("scalar",
                                  ygrp.rearrange("pp (k tt) -> pp k tt", k=4),
                                  y1r[:, d:d + 4, gc:gc + HF], wait=yw, inc=ysem)
                            vldy = ysem.n
                        wsl, ww = wo_rot.acq()
                        wosem = wo_rot.sems[wsl]
                        wos = wo_buf[:, wsl * 2048:(wsl + 1) * 2048]
                        vldo, worel = load_cached(p, t, ("wo", d), t.wop[d],
                                                  t.wob[d], wos, wosem, ww)
                        for k in range(KD):
                            p.mm(y2_ps[pr][:, :], wos[:, k * 128:(k + 1) * 128],
                                 t.attnT[:, k * 512:(k + 1) * 512],
                                 start=(k == 0), stop=(k == KD - 1),
                                 wait=(y2rel[pr] + [(wosem, vldo),
                                                    (t.s_oc, grp_oc)]
                                       if k == 0 else []),
                                 inc=(t.s_p2 if k == KD - 1 else None))
                        vwo = t.s_p2.n
                        wo_rot.set_rel(wsl, [(t.s_p2, vwo)] + worel)
                        yab = t.ya[:, ysl * 2048 + (d % 4) * HF:
                                   ysl * 2048 + (d % 4 + 1) * HF]
                        ssl, sw = st2_rot.acq()
                        ssem = st2_rot.sems[ssl]
                        stg = t.stg2[:, ssl * HF:(ssl + 1) * HF]
                        p.v("tensor_add", stg, y2_ps[pr][:, :], yab,
                            wait=sw + [(t.s_p2, vwo), (ysem, vldy)],
                            inc=t.s_ev)
                        vev = t.s_ev.n
                        y2rel[pr] = [(t.s_ev, vev)]
                        p.dma("sync", t.y2d[s_i, d * 128:(d + 1) * 128, gc:gc + HF],
                              stg, wait=[(t.s_ev, vev)], inc=ssem)
                        st2_rot.set_rel(ssl, [(ssem, ssem.n)])
                        if d % 4 == 3:
                            ya_rot.set_rel(ysl, [(t.s_ev, t.s_ev.n)])
                        base = (d % 2) * 2 * HF
                        yc = t.abf[:, base:base + HF]
                        yq = t.abf[:, base + HF:base + 2 * HF]
                        p.v("tensor_copy", yc, stg, wait=abf_guard[d % 2])
                        p.drain()
                        p.v("tensor_mul", yq, yc, yc, inc=t.s_sq)
                        vsq2 = t.s_sq.n
                        p.mm(ysum_ps[:, :], t.onescol_bf[:, :], yc,
                             start=(d == 0), stop=(d == KD - 1),
                             wait=([(t.s_row, t.s_row.n)] if d == 0 else [])
                             + [(t.s_sq, vsq2)], inc=t.s_pex)
                        p.mm(yssq_ps[:, :], t.onescol_bf[:, :], yq,
                             start=(d == 0), stop=(d == KD - 1), inc=t.s_peq)
                        abf_guard[d % 2] = [(t.s_peq, t.s_peq.n)]
                    mean = t.trow[0:1, 0:HF]
                    e2r = t.trow[0:1, HF:2 * HF]
                    tmp2 = t.trow[0:1, 2 * HF:3 * HF]
                    rr = t.sq2[0:1, 0:HF]
                    wstat = [(t.s_pex, t.s_pex.n), (t.s_peq, t.s_peq.n)]
                    p.v("tensor_scalar_mul", mean, ysum_ps[:, :], 1.0 / D,
                        wait=wstat)
                    p.drain()
                    p.v("tensor_scalar_mul", e2r, yssq_ps[:, :], 1.0 / D)
                    p.drain()
                    p.v("tensor_mul", tmp2, mean, mean)
                    p.drain()
                    p.v("tensor_sub", e2r, e2r, tmp2, inc=t.s_row)
                    vr0 = t.s_row.n
                    p.drain()
                    p.act(tmp2, e2r, AF.Ln, scale=1.0, bias=t.eps_ln,
                          wait=[(t.s_row, vr0)])
                    p.drain("scalar")
                    p.act(rr, tmp2, AF.Exp, scale=-0.5, inc=t.s_r)
                    vrr = t.s_r.n
                    p.v("scalar_tensor_tensor", e2r, mean, -1.0, rr,
                        ALU.mult, ALU.mult, wait=[(t.s_r, vrr)], inc=t.s_row)
                    vmr2 = t.s_row.n
                    p.drain()
                    p.dma("sync", t.y2rows[s_i, 0:1, gc:gc + HF], rr,
                          wait=[(t.s_r, vrr)], inc=t.s_y2rw)
                    p.dma("sync", t.y2rows[s_i, 1:2, gc:gc + HF], e2r,
                          wait=[(t.s_row, vmr2)], inc=t.s_y2rw)



def _prep_host(inputs):
    def gi(k):
        return np.ascontiguousarray(np.asarray(inputs[k], dtype=np.float32))
    x = gi("L0_attn_x")
    w1 = np.concatenate([gi("L0_ffn_routed_w1"), gi("L0_ffn_shared_w1")], 0)
    w2 = np.concatenate([gi("L0_ffn_routed_w2"), gi("L0_ffn_shared_w2")], 0)
    b1 = np.concatenate([gi("L0_ffn_routed_b1"), gi("L0_ffn_shared_b1")], 0)
    b2 = np.concatenate([gi("L0_ffn_routed_b2"), gi("L0_ffn_shared_b2")], 0)
    wq, wk = gi("L0_attn_Wq"), gi("L0_attn_Wk")
    wv, wo = gi("L0_attn_Wv"), gi("L0_attn_Wo")
    cent = gi("L0_ffn_centroids")
    d = {}
    d["w1p"] = np.ascontiguousarray(
        w1.reshape(NE, KD, 128, KF, 128).transpose(0, 3, 2, 1, 4).reshape(NE, KF, 128, KD * 128))
    d["w2p"] = np.ascontiguousarray(
        w2.reshape(NE, KF, 128, KD, 128).transpose(0, 3, 2, 1, 4).reshape(NE, KD, 128, KF * 128))
    d["wqp"] = np.ascontiguousarray(
        wq.reshape(NH, 128, KD, 128).transpose(0, 3, 2, 1).reshape(NH, 128, KD * 128))
    d["wop"] = np.ascontiguousarray(
        wo.reshape(KD, 128, KD, 128).transpose(0, 3, 2, 1).reshape(KD, 128, KD * 128))
    d["wkp"] = np.ascontiguousarray(wk.reshape(512, KD, 128).transpose(2, 1, 0).reshape(128, KD * 512))
    d["wvp"] = np.ascontiguousarray(wv.reshape(512, KD, 128).transpose(2, 1, 0).reshape(128, KD * 512))
    d["centp"] = np.ascontiguousarray(cent.reshape(16, KD, 128).transpose(2, 1, 0).reshape(128, KD * 16))
    d["b1c"] = np.ascontiguousarray(b1.reshape(NE, KF, 128).transpose(2, 0, 1).reshape(128, NE * KF))
    d["b2r"] = b2
    d["rbias"] = gi("L0_ffn_bias").reshape(16, 1)
    d["qsc"] = gi("L0_attn_q_scale").reshape(128, 1)
    d["ksc"] = gi("L0_attn_k_scale").reshape(128, 1)
    d["idf"] = np.eye(128, dtype=np.float32)
    d["causal"] = np.triu(np.full((128, 128), NEG, dtype=np.float32), k=1)
    d["onescol"] = np.ones((128, 1), np.float32)
    d["onesrow"] = np.ones((1, 128), np.float32)
    d["sel"] = np.repeat(np.eye(16, dtype=np.float32), 128, axis=1)
    d["cvec"] = np.array([[1e-5, 1e-6, 0, 0, 0, 0, 0, 0]], np.float32)
    xcores = [np.ascontiguousarray(x[2 * c:2 * c + 2].transpose(0, 2, 1)) for c in range(8)]
    return d, xcores


def kernel(**inputs):
    shared, xcores = _prep_host(inputs)
    cfg = dict(ffn1=True, attn=True, ffn2=True, n_mb=4)
    nc = build_nc(cfg)
    in_maps = [dict(shared, xT=xcores[c]) for c in range(8)]
    res = run_bass_kernel_spmd(nc, in_maps, core_ids=list(range(8)))
    parts = [res.results[c]["outT"].transpose(0, 2, 1) for c in range(8)]
    return np.ascontiguousarray(np.concatenate(parts, axis=0))


# version 05
# speedup vs baseline: 1.0029x; 1.0029x over previous
from contextlib import ExitStack

import numpy as np

import concourse.bass as bass
import concourse.mybir as mybir
from concourse.bass_utils import run_bass_kernel_spmd

F32 = mybir.dt.float32
BF16 = mybir.dt.bfloat16
AF = mybir.ActivationFunctionType
ALU = mybir.AluOpType
AX = mybir.AxisListType.X

D, KD = 2048, 16
DFF, KF = 1408, 11
NE, NRT = 18, 16
T = 2048
MB, HF = 1024, 512
NH, NKV, HD = 16, 4, 128
SCL = float(1.0 / np.sqrt(128.0))
NEG = -1.0e30
ENGINES = ("tensor", "vector", "scalar", "sync", "gpsimd")
EXPERT_ORDER = [16, 17] + list(range(16))
STATS_BF16 = False


class Sem:
    __slots__ = ("h", "n", "name")


class Rot:

    def __init__(self, n, p=None, tag=None):
        self.n = n
        self.rel = [()] * n
        self.i = -1
        self.sems = ([p.sem(_u(tag + str(i))) for i in range(n)]
                     if p is not None else None)

    def acq(self):
        self.i += 1
        sl = self.i % self.n
        w = self.rel[sl]
        self.rel[sl] = ()
        return sl, list(w)

    def set_rel(self, sl, conds):
        self.rel[sl] = tuple(conds)


class Prog:
    def __init__(self, nc, ctx):
        self.nc, self.ctx = nc, ctx
        self.ops = {e: [] for e in ENGINES}
        self.known = {e: {} for e in ENGINES}
        self.store_sems = []

    def stores_done(self):
        return [(s, s.n) for s in self.store_sems if s.n > 0]

    def sem(self, name):
        s = Sem()
        s.h = self.ctx.enter_context(self.nc.semaphore(name))
        s.n = 0
        s.name = name
        return s

    def emit(self, eng, fn, wait=(), inc=None, incv=1):
        wl = []
        for (s, v) in wait:
            if v > self.known[eng].get(s, 0):
                self.known[eng][s] = v
                wl.append((s, v))
        if inc is not None:
            inc.n += incv
        self.ops[eng].append((wl, fn, inc, incv))

    def dma(self, q, out, in_, wait=(), inc=None):
        self.emit(q, (lambda e, o=out, i=in_: e.dma_start(out=o, in_=i)), wait, inc, 16)

    def mm(self, out, lhsT, rhs, start, stop, wait=(), inc=None):
        self.emit(
            "tensor",
            (lambda e, o=out, l=lhsT, r=rhs, a=start, b=stop:
             e.matmul(o, l, r, start=a, stop=b, skip_group_check=True)),
            wait, inc, 1)

    def tr(self, out, in_, ident, wait=(), inc=None):
        self.emit("tensor", (lambda e, o=out, i=in_, d=ident: e.transpose(o, i, d)),
                  wait, inc, 1)

    def act(self, out, in_, func, scale=1.0, bias=None, wait=(), inc=None,
            acc=None):
        kw = {}
        if bias is not None:
            kw["bias"] = bias
        if acc is not None:
            kw["accum_out"] = acc
        fn = (lambda e, o=out, i=in_, f=func, s=scale, k=kw:
              e.activation(o, i, f, scale=s, **k))
        self.emit("scalar", fn, wait, inc, 1)

    def acopy(self, out, in_, wait=(), inc=None):
        self.act(out, in_, AF.Copy, wait=wait, inc=inc)

    def drain(self, eng="vector"):
        self.emit(eng, lambda e: e.drain())

    def v(self, name, *args, wait=(), inc=None, **kw):
        self.emit("vector",
                  (lambda e, n=name, a=args, k=kw: getattr(e, n)(*a, **k)),
                  wait, inc, 1)

    def finalize(self, block):
        def mk(eng_name):
            ops = self.ops[eng_name]

            def run(engine):
                for (wl, fn, inc, incv) in ops:
                    for (s, vv) in wl:
                        engine.wait_ge(s.h, vv)
                    if fn is None:
                        continue
                    inst = fn(engine)
                    if inc is not None:
                        inst.then_inc(inc.h, incv)
            return run
        block.tensor(mk("tensor"))
        block.vector(mk("vector"))
        block.scalar(mk("scalar"))
        block.sync(mk("sync"))
        block.gpsimd(mk("gpsimd"))


class NS:
    pass


_UID = [0]


def _u(name):
    _UID[0] += 1
    return f"{name}_{_UID[0]}"


def build_nc(cfg):
    nc = bass.Bass("TRN2", target_bir_lowering=False)
    ctx = ExitStack()
    p = Prog(nc, ctx)
    t = NS()
    e = ctx.enter_context

    t.xT = nc.dram_tensor("xT", [2, D, T], F32, kind="ExternalInput").ap()
    t.w1p = nc.dram_tensor("w1p", [NE, KF, 128, KD * 128], F32, kind="ExternalInput").ap()
    t.w2p = nc.dram_tensor("w2p", [NE, KD, 128, KF * 128], F32, kind="ExternalInput").ap()
    t.wqp = nc.dram_tensor("wqp", [NH, 128, KD * 128], F32, kind="ExternalInput").ap()
    t.wop = nc.dram_tensor("wop", [KD, 128, KD * 128], F32, kind="ExternalInput").ap()
    t.wkp = nc.dram_tensor("wkp", [128, KD * 512], F32, kind="ExternalInput").ap()
    t.wvp = nc.dram_tensor("wvp", [128, KD * 512], F32, kind="ExternalInput").ap()
    t.centp = nc.dram_tensor("centp", [128, KD * 16], F32, kind="ExternalInput").ap()
    t.b1c = nc.dram_tensor("b1c", [128, NE * KF], F32, kind="ExternalInput").ap()
    t.b2r = nc.dram_tensor("b2r", [NE, D], F32, kind="ExternalInput").ap()
    t.rbias = nc.dram_tensor("rbias", [16, 1], F32, kind="ExternalInput").ap()
    t.qsc = nc.dram_tensor("qsc", [128, 1], F32, kind="ExternalInput").ap()
    t.ksc = nc.dram_tensor("ksc", [128, 1], F32, kind="ExternalInput").ap()
    t.idf = nc.dram_tensor("idf", [128, 128], F32, kind="ExternalInput").ap()
    t.causal = nc.dram_tensor("causal", [128, 128], F32, kind="ExternalInput").ap()
    t.onescol = nc.dram_tensor("onescol", [128, 1], F32, kind="ExternalInput").ap()
    t.onesrow = nc.dram_tensor("onesrow", [1, 128], F32, kind="ExternalInput").ap()
    t.sel = nc.dram_tensor("sel", [16, 16 * 128], F32, kind="ExternalInput").ap()
    t.cvec = nc.dram_tensor("cvec", [1, 8], F32, kind="ExternalInput").ap()

    y1_kind = ("ExternalInput" if cfg.get("y1_input") else
               "ExternalOutput" if cfg.get("expose_y1") else "Internal")
    y2_kind = "ExternalOutput" if cfg.get("expose_y2") else "Internal"
    t.y1d = nc.dram_tensor("y1d", [2, D, T], F32, kind=y1_kind).ap()
    t.y2d = nc.dram_tensor("y2d", [2, D, T], F32, kind=y2_kind).ap()
    t.outT = nc.dram_tensor("outT", [2, D, T], F32, kind="ExternalOutput").ap()
    t.w1b = nc.dram_tensor("w1b", [NE, KF, 128, KD * 128], BF16, kind="Internal").ap()
    t.w2b = nc.dram_tensor("w2b", [NE, KD, 128, KF * 128], BF16, kind="Internal").ap()
    t.wqb = nc.dram_tensor("wqb", [NH, 128, KD * 128], BF16, kind="Internal").ap()
    t.wob = nc.dram_tensor("wob", [KD, 128, KD * 128], BF16, kind="Internal").ap()
    t.wcache = {}
    t.overlap_stats = not cfg.get("no_overlap", False)
    if cfg.get("dbg"):
        t.dbg_sc = nc.dram_tensor("dbg_sc", [16, MB], F32, kind="ExternalOutput").ap()
        t.dbg_g = nc.dram_tensor("dbg_g", [NE, MB], BF16, kind="ExternalOutput").ap()
        t.dbg_u = nc.dram_tensor("dbg_u", [128, KD * MB], BF16, kind="ExternalOutput").ap()
        t.dbg_y = nc.dram_tensor("dbg_y", [128, KD * MB], F32, kind="ExternalOutput").ap()
        t.dbg_tr = nc.dram_tensor("dbg_tr", [1, 2 * MB], F32, kind="ExternalOutput").ap()
    t.dbg = cfg.get("dbg", False)
    rk = ("ExternalInput" if cfg.get("rows_input") else
          "ExternalOutput" if cfg.get("expose_rows") else "Internal")
    t.y1rows = nc.dram_tensor("y1rows", [2, 2, T], F32, kind=rk).ap()
    t.y2rows = nc.dram_tensor("y2rows", [2, 2, T], F32, kind="Internal").ap()

    t.idf_sb = e(nc.sbuf_tensor("idf_sb", [128, 128], F32)).ap()
    t.idb_sb = e(nc.sbuf_tensor("idb_sb", [128, 128], BF16)).ap()
    t.causal_sb = e(nc.sbuf_tensor("causal_sb", [128, 128], F32)).ap()
    t.onescol_sb = e(nc.sbuf_tensor("onescol_sb", [128, 1], F32)).ap()
    t.onesrow_sb = e(nc.sbuf_tensor("onesrow_sb", [1, 128], F32)).ap()
    t.cvec_sb = e(nc.sbuf_tensor("cvec_sb", [1, 8], F32)).ap()
    t.qsc_sb = e(nc.sbuf_tensor("qsc_sb", [128, 1], F32)).ap()
    t.ksc_sb = e(nc.sbuf_tensor("ksc_sb", [128, 1], F32)).ap()
    t.rbias_sb = e(nc.sbuf_tensor("rbias_sb", [16, 1], F32)).ap()
    t.onescol_bf = e(nc.sbuf_tensor("onescol_bf", [128, 1], BF16)).ap()
    t.r_row = e(nc.sbuf_tensor("r_row", [1, T], F32)).ap()
    t.mr_row = e(nc.sbuf_tensor("mr_row", [1, T], F32)).ap()
    t.trow = e(nc.sbuf_tensor("trow", [1, 3 * HF], F32)).ap()

    t.s_prep = p.sem("s_prep")
    t.s_prep2 = p.sem("s_prep2")
    for nm in ("s_ldx", "s_sq", "s_pex", "s_peq", "s_pec", "s_row", "s_r",
               "s_u", "s_scf", "s_gt", "s_ld1", "s_ld2", "s_p1", "s_rl",
               "s_hd", "s_gb", "s_p2", "s_ya", "s_ev", "s_yst",
               "s_kv", "s_sc2", "s_mx", "s_exp", "s_pn", "s_pc", "s_out",
               "s_oc", "s_qn", "s_yrw", "s_y2rw", "s_arow"):
        setattr(t, nm, p.sem(nm))
    t.s_rwl = [p.sem("s_rwl0"), p.sem("s_rwl1")]

    for (dst, src) in ((t.idf_sb, t.idf), (t.causal_sb, t.causal),
                       (t.onescol_sb, t.onescol), (t.onesrow_sb, t.onesrow),
                       (t.cvec_sb, t.cvec), (t.qsc_sb, t.qsc), (t.ksc_sb, t.ksc),
                       (t.rbias_sb, t.rbias)):
        p.dma("scalar", dst[:, :], src[:, :], inc=t.s_prep)
    p.dma("gpsimd", t.idb_sb[:, :], t.idf[:, :], inc=t.s_prep)
    p.dma("gpsimd", t.onescol_bf[:, :], t.onescol[:, :], inc=t.s_prep)
    t.eps_ln = t.cvec_sb[0:1, 0:1]
    t.eps_qk = t.cvec_sb[0:1, 1:2]

    if cfg.get("ffn1", True):
        with ExitStack() as fctx:
            emit_ffn_arena(p, t, nc, fctx, barrier=[])
            emit_ffn(p, t, t.xT, t.y1d, cfg.get("n_mb", 4), first_call=True,
                     rows_dst=t.y1rows)
    if cfg.get("attn", True):
        with ExitStack() as actx:
            emit_attn(p, t, nc, actx, cfg)
    if cfg.get("ffn2", True):
        with ExitStack() as fctx2:
            emit_ffn_arena(p, t, nc, fctx2, barrier=p.stores_done())
            emit_ffn(p, t, t.y2d, t.outT, cfg.get("n_mb", 4), first_call=False,
                     rows_src=t.y2rows)
    p.emit("vector", None, wait=p.stores_done())

    with nc.Block() as block:
        p.finalize(block)
    ctx.close()
    return nc


def load_cached(p, t, key, src_f32, cache, slot_ap, slot_sem, ww):
    if key not in t.wcache:
        p.dma("gpsimd", slot_ap, src_f32, wait=ww, inc=slot_sem)
        vld = slot_sem.n
        p.dma("sync", cache, slot_ap, wait=[(slot_sem, vld)], inc=slot_sem)
        t.wcache[key] = True
        return vld, [(slot_sem, slot_sem.n)]
    p.dma("gpsimd", slot_ap, cache, wait=ww, inc=slot_sem)
    return slot_sem.n, []


def emit_ffn_arena(p, t, nc, ctx, barrier):
    e = ctx.enter_context
    t.u_bf = e(nc.sbuf_tensor(_u("u_bf"), [128, KD * MB], BF16)).ap()
    t.y_sb = e(nc.sbuf_tensor(_u("y_sb"), [128, KD * MB], F32)).ap()
    t.h_bf = e(nc.sbuf_tensor(_u("h_bf"), [128, KF * MB], BF16)).ap()
    t.h_tmp = e(nc.sbuf_tensor(_u("h_tmp"), [128, MB], F32)).ap()
    t.w1buf = e(nc.sbuf_tensor(_u("w1buf"), [128, 3 * KD * 128], BF16)).ap()
    t.w2buf = e(nc.sbuf_tensor(_u("w2buf"), [128, 2 * KF * 128], BF16)).ap()
    t.g_bc = e(nc.sbuf_tensor(_u("g_bc"), [128, 2 * MB], BF16)).ap()
    t.xa = e(nc.sbuf_tensor(_u("xa"), [128, 4 * HF * 2], F32)).ap()
    t.scr = e(nc.sbuf_tensor(_u("scr"), [128, 2 * HF], F32)).ap()
    t.sel_sb = e(nc.sbuf_tensor(_u("sel_sb"), [16, 16 * 128], BF16)).ap()
    t.g_aug = e(nc.sbuf_tensor(_u("g_aug"), [NE, MB], BF16)).ap()
    t.b2_sb = e(nc.sbuf_tensor(_u("b2_sb"), [NE, D], BF16)).ap()
    t.scx_sb = e(nc.sbuf_tensor(_u("scx_sb"), [16, HF], F32)).ap()
    t.b1c_sb = e(nc.sbuf_tensor(_u("b1c_sb"), [128, NE * KF], F32)).ap()
    t.cent_sb = e(nc.sbuf_tensor(_u("cent_sb"), [128, KD * 16], F32)).ap()
    t.csum_sb = e(nc.sbuf_tensor(_u("csum_sb"), [16, 1], F32)).ap()
    t.sc_f32 = e(nc.sbuf_tensor(_u("sc_f32"), [16, MB], F32)).ap()
    t.sc_pre = e(nc.sbuf_tensor(_u("sc_pre"), [16, HF], F32)).ap()
    t.sc_tok = e(nc.sbuf_tensor(_u("sc_tok"), [128, 16], F32)).ap()
    t.gmax = e(nc.sbuf_tensor(_u("gmax"), [128, 8], F32)).ap()
    t.gpop = e(nc.sbuf_tensor(_u("gpop"), [128, 16], F32)).ap()
    t.gmsk = e(nc.sbuf_tensor(_u("gmsk"), [128, 16], F32)).ap()
    t.gsum = e(nc.sbuf_tensor(_u("gsum"), [128, 2], F32)).ap()
    t.gtok = e(nc.sbuf_tensor(_u("gtok"), [128, 16], F32)).ap()
    t.ybf = t.scr[:, 0:HF].bitcast(BF16)

    t.xa_rot = Rot(2, p, "xas")
    t.scr_rot = Rot(2, p, "scs")
    t.w1_rot = Rot(3, p, "w1s")
    t.w2_rot = Rot(2, p, "w2s")
    t.htmp_rot = Rot(1)
    t.gbc_rot = Rot(2)
    p.store_sems += t.scr_rot.sems
    for rot in (t.xa_rot, t.scr_rot, t.w1_rot, t.w2_rot, t.htmp_rot, t.gbc_rot):
        for sl in range(rot.n):
            rot.set_rel(sl, barrier)

    p.dma("scalar", t.b1c_sb[:, :], t.b1c[:, :], wait=barrier, inc=t.s_prep)
    p.dma("scalar", t.cent_sb[:, :], t.centp[:, :], inc=t.s_prep)
    p.dma("gpsimd", t.b2_sb[:, :], t.b2r[:, :], wait=barrier, inc=t.s_prep)
    p.dma("gpsimd", t.sel_sb[:, :], t.sel[:, :], inc=t.s_prep)
    pv = t.s_prep.n
    p.v("tensor_scalar_mul", t.b1c_sb[:, NRT * KF:NE * KF],
        t.b1c_sb[:, NRT * KF:NE * KF], 0.5, wait=[(t.s_prep, pv)], inc=t.s_prep2)
    p.v("memset", t.g_aug[0:NE, :], 0.5, inc=t.s_prep2)
    with nc.psum_tensor(_u("ps_csum"), [16, 1], F32) as ps_csum_h:
        ps_csum = ps_csum_h.ap()
        for k in range(KD):
            p.mm(ps_csum[:, :], t.cent_sb[:, k * 16:(k + 1) * 16], t.onescol_sb[:, :],
                 start=(k == 0), stop=(k == KD - 1),
                 wait=(([(t.s_prep, pv), (t.s_ev, t.s_ev.n), (t.s_oc, t.s_oc.n),
                         (t.s_pc, t.s_pc.n), (t.s_exp, t.s_exp.n)] + barrier)
                       if k == 0 else []),
                 inc=(t.s_pec if k == KD - 1 else None))
        p.v("tensor_copy", t.csum_sb[:, :], ps_csum[:, :],
            wait=[(t.s_pec, t.s_pec.n)], inc=t.s_prep2)



def emit_ffn(p, t, IN, OUT, n_mb, first_call, rows_src=None, rows_dst=None):
    nc = p.nc
    st = NS()
    with ExitStack() as ps:
        en = ps.enter_context
        st.hps = [en(nc.psum_tensor(_u(f"hps{i}"), [128, MB], F32)).ap()
                  for i in range(2)]
        st.yps = [en(nc.psum_tensor(_u(f"yps{i}"), [128, HF], F32)).ap()
                  for i in range(2)]
        st.mps = [en(nc.psum_tensor(_u(f"mps{i}"), [128, HF], F32)).ap()
                  for i in range(2)]
        g0 = [(t.s_ev, t.s_ev.n), (t.s_ya, t.s_ya.n), (t.s_u, t.s_u.n),
              (t.s_hd, t.s_hd.n), (t.s_rl, t.s_rl.n), (t.s_gt, t.s_gt.n),
              (t.s_row, t.s_row.n), (t.s_prep2, t.s_prep2.n),
              (t.s_oc, t.s_oc.n), (t.s_pc, t.s_pc.n), (t.s_exp, t.s_exp.n),
              (t.s_kv, t.s_kv.n), (t.s_qn, t.s_qn.n), (t.s_pn, t.s_pn.n)]
        st.hrel = [list(g0), list(g0)]
        st.yrel = [list(g0), list(g0)]
        st.mrel = [list(g0), list(g0)]
        st.rowv = [None, None]
        st.overlap = getattr(t, "overlap_stats", True)
        if first_call:
            _stats_compute(p, t, st, IN, 0, 0, parity=0)
        pend_evac = None
        for mb in range(n_mb):
            pend_evac = _ffn_mb(p, t, st, IN, OUT, mb, n_mb, first_call,
                                rows_src, rows_dst, pend_evac)
        pend_evac()
        if rows_dst is not None:
            mbl = n_mb - 1
            _stats_compute(p, t, st, OUT, mbl // 2, (mbl % 2) * MB, parity=None,
                           dram_rows=(rows_dst, mbl // 2, (mbl % 2) * MB),
                           wait0=[(s, s.n) for s in t.scr_rot.sems])


def _stats_compute(p, t, st, SRC, sq, c0, parity, dram_rows=None, wait0=()):
    srcr = SRC[sq].rearrange("(k pp) tt -> pp k tt", pp=128)
    first = True
    ybf_guard = [[(s, s.n) for s in t.scr_rot.sems]
                 + [(t.s_u, t.s_u.n), (t.s_ev, t.s_ev.n)]]
    rowv = []
    for h in range(2):
        hcol = c0 + h * HF
        sumv = st.mps[0][0:1, :]
        ssqv = st.mps[1][0:1, :]
        for kg in range(4):
            sl, w = t.xa_rot.acq()
            xsem = t.xa_rot.sems[sl]
            xgrp = t.xa[:, sl * 2048:(sl + 1) * 2048]
            p.dma("scalar", xgrp.rearrange("pp (k tt) -> pp k tt", k=4),
                  srcr[:, 4 * kg:4 * kg + 4, hcol:hcol + HF],
                  wait=w + (list(wait0) if first else []), inc=xsem)
            first = False
            ldv = xsem.n
            for kk in range(4):
                k = 4 * kg + kk
                xab = t.xa[:, sl * 2048 + kk * HF:sl * 2048 + (kk + 1) * HF]
                if STATS_BF16:
                    xc = t.ybf[:, 0:HF]
                    xq = t.ybf[:, HF:2 * HF]
                    p.v("tensor_copy", xc, xab,
                        wait=ybf_guard[0] + [(xsem, ldv)])
                    p.drain()
                    p.v("tensor_mul", xq, xc, xc, inc=t.s_sq)
                    ones = t.onescol_bf
                else:
                    xc = xab
                    xq = t.scr[:, 0:HF]
                    p.v("tensor_mul", xq, xab, xab,
                        wait=ybf_guard[0] + [(xsem, ldv)], inc=t.s_sq)
                    ones = t.onescol_sb
                sqv = t.s_sq.n
                p.mm(sumv, ones[:, :], xc, start=(k == 0),
                     stop=(k == KD - 1),
                     wait=((st.mrel[0] + st.mrel[1]) if k == 0 else [])
                     + [(t.s_sq, sqv), (xsem, ldv)], inc=t.s_pex)
                p.mm(ssqv, ones[:, :], xq, start=(k == 0),
                     stop=(k == KD - 1), inc=t.s_peq)
                ybf_guard[0] = [(t.s_peq, t.s_peq.n), (t.s_pex, t.s_pex.n)]
            t.xa_rot.set_rel(sl, [(t.s_pex, t.s_pex.n), (t.s_peq, t.s_peq.n)])
        mean = t.trow[0:1, 0:HF]
        e2 = t.trow[0:1, HF:2 * HF]
        tmp2 = t.trow[0:1, 2 * HF:3 * HF]
        wstat = [(t.s_pex, t.s_pex.n), (t.s_peq, t.s_peq.n)]
        p.v("tensor_scalar_mul", mean, sumv, 1.0 / D, wait=wstat)
        p.drain()
        p.v("tensor_scalar_mul", e2, ssqv, 1.0 / D)
        p.drain()
        p.v("tensor_mul", tmp2, mean, mean)
        p.drain()
        p.v("tensor_sub", e2, e2, tmp2, inc=t.s_row)
        vr0 = t.s_row.n
        st.mrel[0] = [(t.s_row, vr0)]
        st.mrel[1] = [(t.s_row, vr0)]
        p.drain()

        p.act(tmp2, e2, AF.Ln, scale=1.0, bias=t.eps_ln, wait=[(t.s_row, vr0)])
        p.drain("scalar")
        if parity is not None:
            rdst = t.r_row[0:1, parity * MB + h * HF:parity * MB + (h + 1) * HF]
            mrdst = t.mr_row[0:1, parity * MB + h * HF:parity * MB + (h + 1) * HF]
        else:
            rdst = t.sc_pre[0:1, 0:HF]
            mrdst = e2
        war = ([(t.s_pec, t.s_pec.n), (t.s_ev, t.s_ev.n)]
               if parity is not None else [])
        p.act(rdst, tmp2, AF.Exp, scale=-0.5, wait=war, inc=t.s_r)
        vr = t.s_r.n
        p.v("scalar_tensor_tensor", mrdst, mean, -1.0, rdst, ALU.mult, ALU.mult,
            wait=[(t.s_r, vr)], inc=t.s_row)
        vmr = t.s_row.n
        p.drain()
        if dram_rows is not None:
            rt, rsq, rc0 = dram_rows
            hc = rc0 + h * HF
            p.dma("sync", rt[rsq, 0:1, hc:hc + HF], rdst,
                  wait=[(t.s_r, vr)], inc=t.s_yrw)
            p.dma("sync", rt[rsq, 1:2, hc:hc + HF], mrdst,
                  wait=[(t.s_row, vmr)], inc=t.s_yrw)
        rowv.append(((t.s_r, vr), (t.s_row, vmr)))
    if parity is not None:
        st.rowv[parity] = rowv
        if getattr(t, "dbg", False) and not hasattr(t, "_dbgd"):
            t._dbgd = True
            p.dma("scalar", t.dbg_tr[0:1, 0:MB],
                  t.r_row[0:1, parity * MB:(parity + 1) * MB],
                  wait=[rowv[0][0], rowv[1][0]], inc=t.s_yrw)
            p.dma("scalar", t.dbg_tr[0:1, MB:2 * MB],
                  t.mr_row[0:1, parity * MB:(parity + 1) * MB],
                  wait=[rowv[0][1], rowv[1][1]], inc=t.s_yrw)
    t.scr_rot.rel[0] = tuple(list(t.scr_rot.rel[0])
                             + [(t.s_peq, t.s_peq.n)])


def _ffn_mb(p, t, st, IN, OUT, mb, n_mb, first_call, rows_src, rows_dst,
            pend_evac):
    nc = p.nc
    sq = mb // 2
    c0 = (mb % 2) * MB
    par = mb % 2

    if rows_src is not None:
        rsem = t.s_rwl[par]
        p.dma("scalar", t.r_row[0:1, par * MB:(par + 1) * MB],
              rows_src[sq, 0:1, c0:c0 + MB],
              wait=[(t.s_y2rw, t.s_y2rw.n), (t.s_row, t.s_row.n),
                    (t.s_pec, t.s_pec.n)], inc=rsem)
        p.dma("scalar", t.mr_row[0:1, par * MB:(par + 1) * MB],
              rows_src[sq, 1:2, c0:c0 + MB], inc=rsem)
        rv = (rsem, rsem.n)
        st.rowv[par] = [(rv, rv), (rv, rv)]
    rowv = st.rowv[par]

    inr = IN[sq].rearrange("(k pp) tt -> pp k tt", pp=128)
    uvals = {}
    for h in range(2):
        hcol = c0 + h * HF
        rh = t.r_row[0:1, par * MB + h * HF:par * MB + (h + 1) * HF]
        mrh = t.mr_row[0:1, par * MB + h * HF:par * MB + (h + 1) * HF]
        p.mm(st.yps[0][:, :], t.onesrow_sb[0:1, :], rh, start=True, stop=True,
             wait=st.yrel[0] + [rowv[h][0]], inc=t.s_pec)
        vub = t.s_pec.n
        p.mm(st.yps[1][:, :], t.onesrow_sb[0:1, :], mrh, start=True, stop=True,
             wait=st.yrel[1] + [rowv[h][1]], inc=t.s_pec)
        vub2 = t.s_pec.n
        for kg in range(4):
            sl, w = t.xa_rot.acq()
            xsem = t.xa_rot.sems[sl]
            xgrp = t.xa[:, sl * 2048:(sl + 1) * 2048]
            p.dma("scalar", xgrp.rearrange("pp (k tt) -> pp k tt", k=4),
                  inr[:, 4 * kg:4 * kg + 4, hcol:hcol + HF], wait=w, inc=xsem)
            ldv = xsem.n
            for kk in range(4):
                k = 4 * kg + kk
                xab = t.xa[:, sl * 2048 + kk * HF:sl * 2048 + (kk + 1) * HF]
                ssl, sw = t.scr_rot.acq()
                tmp = t.scr[:, ssl * HF:(ssl + 1) * HF]
                p.v("tensor_mul", tmp, xab, st.yps[0][:, :],
                    wait=sw + [(xsem, ldv), (t.s_pec, vub2)])
                p.v("tensor_add",
                    t.u_bf[:, k * MB + h * HF:k * MB + h * HF + HF],
                    tmp, st.yps[1][:, :], inc=t.s_u)
                uvals[(h, k)] = t.s_u.n
                t.scr_rot.set_rel(ssl, [(t.s_u, t.s_u.n)])
            t.xa_rot.set_rel(sl, [(t.s_u, t.s_u.n)])
        st.yrel[0] = [(t.s_u, t.s_u.n)]
        st.yrel[1] = [(t.s_u, t.s_u.n)]
    if t.dbg and mb == 0:
        p.dma("sync", t.dbg_u[:, :], t.u_bf[:, :],
              wait=[(t.s_u, t.s_u.n)], inc=t.s_yrw)

    for h in range(2):
        hcol = c0 + h * HF
        scxv = st.mps[0][0:16, :]
        for kg in range(4):
            sl, w = t.xa_rot.acq()
            xsem = t.xa_rot.sems[sl]
            xgrp = t.xa[:, sl * 2048:(sl + 1) * 2048]
            p.dma("scalar", xgrp.rearrange("pp (k tt) -> pp k tt", k=4),
                  inr[:, 4 * kg:4 * kg + 4, hcol:hcol + HF], wait=w, inc=xsem)
            ldv = xsem.n
            for kk in range(4):
                k = 4 * kg + kk
                xab = t.xa[:, sl * 2048 + kk * HF:sl * 2048 + (kk + 1) * HF]
                p.mm(scxv, t.cent_sb[:, k * 16:(k + 1) * 16], xab,
                     start=(k == 0), stop=(k == KD - 1),
                     wait=(st.mrel[0] if k == 0 else []) + [(xsem, ldv)],
                     inc=t.s_pec)
            t.xa_rot.set_rel(sl, [(t.s_pec, t.s_pec.n)])
        vscx = t.s_pec.n
        scx_sb = t.scx_sb[:, 0:HF]
        p.v("tensor_copy", scx_sb, scxv, wait=[(t.s_pec, vscx)], inc=t.s_gt)
        vcopy = t.s_gt.n
        p.drain()
        rh = t.r_row[0:1, par * MB + h * HF:par * MB + (h + 1) * HF]
        mrh = t.mr_row[0:1, par * MB + h * HF:par * MB + (h + 1) * HF]
        p.mm(st.mps[0][0:16, :], t.onesrow_sb[0:1, 0:16], rh, start=True,
             stop=True, wait=[(t.s_gt, vcopy), rowv[h][0]], inc=t.s_pec)
        v16a = t.s_pec.n
        p.mm(st.mps[1][0:16, :], t.onesrow_sb[0:1, 0:16], mrh, start=True,
             stop=True, wait=st.mrel[1] + [rowv[h][1]], inc=t.s_pec)
        v16b = t.s_pec.n
        p.v("tensor_mul", t.sc_pre[:, :], scx_sb, st.mps[0][0:16, :],
            wait=[(t.s_pec, v16a)])
        p.drain()
        p.v("scalar_tensor_tensor", t.sc_pre[:, :], st.mps[1][0:16, :],
            t.csum_sb[:, :], t.sc_pre[:, :], ALU.mult, ALU.add,
            wait=[(t.s_pec, v16b)], inc=t.s_row)
        vpre = t.s_row.n
        st.mrel[0] = [(t.s_row, vpre)]
        st.mrel[1] = [(t.s_row, vpre)]
        p.drain()
        scf = t.sc_f32[0:16, h * HF:(h + 1) * HF]
        p.act(scf, t.sc_pre[:, :], AF.Exp, scale=-1.0, wait=[(t.s_row, vpre)],
              inc=t.s_r)
        p.v("tensor_scalar_add", scf, scf, 1.0, wait=[(t.s_r, t.s_r.n)])
        p.drain()
        p.v("reciprocal", t.sc_pre[:, :], scf)
        p.drain()
        p.v("tensor_scalar_add", scf, t.sc_pre[:, :], t.rbias_sb[:, :],
            inc=t.s_scf)
        p.drain()
    scf_total = t.s_scf.n
    tp_rel = list(st.hrel[0])
    gt_rel = list(st.hrel[1])
    for tt in range(8):
        p.tr(st.hps[0][:, 0:16], t.sc_f32[0:16, tt * 128:(tt + 1) * 128],
             t.idf_sb[0:16, 0:16], wait=tp_rel + [(t.s_scf, scf_total)],
             inc=t.s_pec)
        vtp = t.s_pec.n
        stok = t.sc_tok[:, 0:16]
        p.v("tensor_copy", stok, st.hps[0][:, 0:16], wait=[(t.s_pec, vtp)],
            inc=t.s_gt)
        tp_rel = [(t.s_gt, t.s_gt.n)]
        p.drain()
        p.v("max", t.gmax[:, :], stok)
        p.drain()
        p.v("memset", t.gmax[:, 4:8], NEG)
        p.drain()
        p.v("match_replace", t.gpop[:, :], t.gmax[:, :], stok, NEG)
        p.drain()
        p.v("tensor_tensor", t.gmsk[:, :], stok, t.gpop[:, :], ALU.is_gt)
        p.drain()
        p.v("tensor_mul", t.gpop[:, :], stok, t.gmsk[:, :])
        p.drain()
        p.v("reduce_sum", t.gsum[:, 0:1], t.gpop[:, :], AX)
        p.drain()
        p.v("tensor_scalar_max", t.gsum[:, 0:1], t.gsum[:, 0:1], 1e-8)
        p.drain()
        p.v("reciprocal", t.gsum[:, 1:2], t.gsum[:, 0:1])
        p.drain()
        p.v("tensor_scalar", t.gtok[:, :], t.gpop[:, :], t.gsum[:, 1:2], 0.5,
            ALU.mult, ALU.mult, inc=t.s_gt)
        vg = t.s_gt.n
        p.drain()
        p.tr(st.hps[1][0:16, 0:128], t.gtok[:, :], t.idf_sb[:, :],
             wait=gt_rel + [(t.s_gt, vg)], inc=t.s_pec)
        vgt = t.s_pec.n
        p.v("tensor_copy", t.g_aug[0:16, tt * 128:(tt + 1) * 128],
            st.hps[1][0:16, 0:128], wait=[(t.s_pec, vgt)], inc=t.s_gt)
        gt_rel = [(t.s_gt, t.s_gt.n)]
        p.drain()
    gaug_done = t.s_gt.n
    st.hrel[0] = list(tp_rel)
    st.hrel[1] = list(gt_rel)
    if t.dbg and mb == 0:
        p.dma("sync", t.dbg_sc[:, :], t.sc_f32[0:16, :],
              wait=[(t.s_scf, scf_total)], inc=t.s_yrw)
        p.dma("sync", t.dbg_g[:, :], t.g_aug[0:NE, :],
              wait=[(t.s_gt, gaug_done)], inc=t.s_yrw)

    if pend_evac is not None:
        pend_evac()

    idx = 0
    for d in range(KD):
        for h in range(2):
            yp = st.yps[idx % 2]
            p.mm(yp[:, :], t.b2_sb[0:NE, d * 128:(d + 1) * 128],
                 t.g_aug[0:NE, h * HF:(h + 1) * HF], start=True, stop=True,
                 wait=st.yrel[idx % 2] + [(t.s_gt, gaug_done)], inc=t.s_p2)
            vp2 = t.s_p2.n
            p.v("tensor_copy", t.y_sb[:, d * MB + h * HF:d * MB + h * HF + HF],
                yp[:, :], wait=[(t.s_p2, vp2)], inc=t.s_ya)
            st.yrel[idx % 2] = [(t.s_ya, t.s_ya.n)]
            idx += 1

    inj_x = first_call and (mb + 1 < n_mb) and st.overlap
    inj_y = first_call and (mb >= 1) and st.overlap
    if first_call and not st.overlap and mb + 1 < n_mb:
        _stats_compute(p, t, st, IN, (mb + 1) // 2, ((mb + 1) % 2) * MB,
                       parity=(mb + 1) % 2)
    if first_call and not st.overlap and mb >= 1 and rows_dst is not None:
        _stats_compute(p, t, st, OUT, (mb - 1) // 2, ((mb - 1) % 2) * MB,
                       parity=None,
                       dram_rows=(rows_dst, (mb - 1) // 2, ((mb - 1) % 2) * MB),
                       wait0=[(s, s.n) for s in t.scr_rot.sems])
    for ei, e_ in enumerate(EXPERT_ORDER):
        if ei == 5 and inj_x:
            _stats_compute(p, t, st, IN, (mb + 1) // 2, ((mb + 1) % 2) * MB,
                           parity=(mb + 1) % 2)
        if ei == 10 and inj_y:
            _stats_compute(p, t, st, OUT, (mb - 1) // 2, ((mb - 1) % 2) * MB,
                           parity=None,
                           dram_rows=(rows_dst, (mb - 1) // 2,
                                      ((mb - 1) % 2) * MB),
                           wait0=[(s, s.n) for s in t.scr_rot.sems])
        routed = e_ < NRT
        if routed:
            gsl, gw = t.gbc_rot.acq()
            for h in range(2):
                p.mm(st.hps[1][:, h * HF:(h + 1) * HF],
                     t.sel_sb[0:16, e_ * 128:(e_ + 1) * 128],
                     t.g_aug[0:16, h * HF:(h + 1) * HF], start=True, stop=True,
                     wait=st.hrel[1] + gw + [(t.s_gt, gaug_done)],
                     inc=(t.s_pec if h == 1 else None))
            vsel = t.s_pec.n
            gslice = t.g_bc[:, gsl * MB:(gsl + 1) * MB]
            p.v("tensor_copy", gslice, st.hps[1][:, :], wait=[(t.s_pec, vsel)],
                inc=t.s_gb)
            vgb = t.s_gb.n
            st.hrel[1] = [(t.s_gb, vgb)]
        hd_val = []
        for c in range(KF):
            pr = c % 2
            hp = st.hps[pr]
            wsl, ww = t.w1_rot.acq()
            wsem = t.w1_rot.sems[wsl]
            w1s = t.w1buf[:, wsl * 2048:(wsl + 1) * 2048]
            vld, wrel = load_cached(p, t, ("w1", e_, c), t.w1p[e_, c],
                                    t.w1b[e_, c], w1s, wsem, ww)
            for h in range(2):
                for k in range(KD):
                    wl = [(t.s_u, uvals[(h, k)])]
                    if h == 0 and k == 0:
                        wl = st.hrel[pr] + [(wsem, vld)] + wl
                    p.mm(hp[:, h * HF:(h + 1) * HF],
                         w1s[:, k * 128:(k + 1) * 128],
                         t.u_bf[:, k * MB + h * HF:k * MB + h * HF + HF],
                         start=(k == 0), stop=(k == KD - 1), wait=wl,
                         inc=(t.s_p1 if (h == 1 and k == KD - 1) else None))
            vp1 = t.s_p1.n
            t.w1_rot.set_rel(wsl, [(t.s_p1, vp1)] + wrel)
            bcol = t.b1c_sb[:, e_ * KF + c:e_ * KF + c + 1]
            if routed:
                hsl, hw = t.htmp_rot.acq()
                htt = t.h_tmp[:, :]
                p.act(htt, hp[:, :], AF.Relu, scale=1.0, bias=bcol,
                      wait=hw + [(t.s_p1, vp1)], inc=t.s_rl)
                vrl = t.s_rl.n
                st.hrel[pr] = [(t.s_rl, vrl)]
                p.v("tensor_mul", t.h_bf[:, c * MB:(c + 1) * MB], htt, gslice,
                    wait=[(t.s_rl, vrl), (t.s_gb, vgb)], inc=t.s_hd)
                t.htmp_rot.set_rel(hsl, [(t.s_hd, t.s_hd.n)])
            else:
                p.act(t.h_bf[:, c * MB:(c + 1) * MB], hp[:, :], AF.Relu,
                      scale=0.5, bias=bcol, wait=[(t.s_p1, vp1)], inc=t.s_hd)
                st.hrel[pr] = [(t.s_hd, t.s_hd.n)]
            hd_val.append(t.s_hd.n)
        for d in range(KD):
            wsl2, ww2 = t.w2_rot.acq()
            wsem2 = t.w2_rot.sems[wsl2]
            w2s = t.w2buf[:, wsl2 * 1408:(wsl2 + 1) * 1408]
            vld2, wrel2 = load_cached(p, t, ("w2", e_, d), t.w2p[e_, d],
                                      t.w2b[e_, d], w2s, wsem2, ww2)
            for h in range(2):
                yp = st.yps[idx % 2]
                for k in range(KF):
                    wl = [(t.s_hd, hd_val[k])]
                    if k == 0:
                        wl = st.yrel[idx % 2] + [(wsem2, vld2)] + wl
                    p.mm(yp[:, :], w2s[:, k * 128:(k + 1) * 128],
                         t.h_bf[:, k * MB + h * HF:k * MB + h * HF + HF],
                         start=(k == 0), stop=(k == KF - 1), wait=wl,
                         inc=(t.s_p2 if k == KF - 1 else None))
                vp2 = t.s_p2.n
                p.v("tensor_add",
                    t.y_sb[:, d * MB + h * HF:d * MB + h * HF + HF],
                    t.y_sb[:, d * MB + h * HF:d * MB + h * HF + HF],
                    yp[:, :], wait=[(t.s_p2, vp2)], inc=t.s_ya)
                st.yrel[idx % 2] = [(t.s_ya, t.s_ya.n)]
                idx += 1
            t.w2_rot.set_rel(wsl2, [(t.s_p2, t.s_p2.n)] + wrel2)
        if routed:
            t.gbc_rot.set_rel(gsl, [(t.s_hd, t.s_hd.n)])

    if t.dbg and mb == 0:
        p.dma("sync", t.dbg_y[:, :], t.y_sb[:, :],
              wait=[(t.s_ya, t.s_ya.n)], inc=t.s_yrw)

    def do_evac():
        _ffn_evac(p, t, st, IN, OUT, sq, c0, par)
    return do_evac


def _ffn_evac(p, t, st, IN, OUT, sq, c0, par):
    inr = IN[sq].rearrange("(k pp) tt -> pp k tt", pp=128)
    evg = [(t.s_ya, t.s_ya.n), (t.s_hd, t.s_hd.n), (t.s_rl, t.s_rl.n),
           (t.s_gb, t.s_gb.n)]
    for h in range(2):
        hcol = c0 + h * HF
        c1r = t.trow[0:1, 0:HF]
        c0r = t.trow[0:1, HF:2 * HF]
        rh = t.r_row[0:1, par * MB + h * HF:par * MB + (h + 1) * HF]
        mrh = t.mr_row[0:1, par * MB + h * HF:par * MB + (h + 1) * HF]
        p.v("tensor_scalar", c1r, rh, 0.5, 1.0, ALU.mult, ALU.add,
            wait=[(t.s_r, t.s_r.n), (t.s_row, t.s_row.n)])
        p.drain()
        p.v("tensor_scalar_mul", c0r, mrh, 0.5, inc=t.s_row)
        vcr = t.s_row.n
        p.drain()
        p.mm(st.mps[0][:, :], t.onesrow_sb[0:1, :], c1r, start=True, stop=True,
             wait=evg + st.mrel[0] + [(t.s_row, vcr)], inc=t.s_pec)
        vb1 = t.s_pec.n
        p.mm(st.mps[1][:, :], t.onesrow_sb[0:1, :], c0r, start=True, stop=True,
             wait=st.mrel[1], inc=t.s_pec)
        vb0 = t.s_pec.n
        for dg in range(4):
            sl, w = t.xa_rot.acq()
            xsem = t.xa_rot.sems[sl]
            xgrp = t.xa[:, sl * 2048:(sl + 1) * 2048]
            p.dma("scalar", xgrp.rearrange("pp (k tt) -> pp k tt", k=4),
                  inr[:, 4 * dg:4 * dg + 4, hcol:hcol + HF], wait=w, inc=xsem)
            ldv = xsem.n
            for dd in range(4):
                d = 4 * dg + dd
                xab = t.xa[:, sl * 2048 + dd * HF:sl * 2048 + (dd + 1) * HF]
                ssl, sw = t.scr_rot.acq()
                ssem = t.scr_rot.sems[ssl]
                stg = t.scr[:, ssl * HF:(ssl + 1) * HF]
                p.v("tensor_mul", stg, xab, st.mps[0][:, :],
                    wait=sw + [(xsem, ldv), (t.s_pec, vb1)])
                p.v("tensor_add", stg, stg, st.mps[1][:, :],
                    wait=[(t.s_pec, vb0)])
                p.v("tensor_add", stg, stg,
                    t.y_sb[:, d * MB + h * HF:d * MB + h * HF + HF],
                    inc=t.s_ev)
                vev = t.s_ev.n
                p.dma("sync", OUT[sq, d * 128:(d + 1) * 128, hcol:hcol + HF],
                      stg, wait=[(t.s_ev, vev)], inc=ssem)
                t.scr_rot.set_rel(ssl, [(ssem, ssem.n)])
            t.xa_rot.set_rel(sl, [(t.s_ev, t.s_ev.n)])
        st.mrel[0] = [(t.s_ev, t.s_ev.n)]
        st.mrel[1] = [(t.s_ev, t.s_ev.n)]


def emit_attn(p, t, nc, ctx, cfg):
    e = ctx.enter_context
    bar = p.stores_done()
    pe_bar = bar + [(t.s_ev, t.s_ev.n), (t.s_ya, t.s_ya.n), (t.s_u, t.s_u.n),
                    (t.s_hd, t.s_hd.n), (t.s_rl, t.s_rl.n), (t.s_gt, t.s_gt.n),
                    (t.s_scf, t.s_scf.n)]
    t.KT = e(nc.sbuf_tensor("KT", [128, NKV * T], BF16)).ap()
    t.V = e(nc.sbuf_tensor("V", [128, 16 * 512], BF16)).ap()
    t.u2 = e(nc.sbuf_tensor("u2", [128, KD * 512], BF16)).ap()
    t.ya = e(nc.sbuf_tensor("ya", [128, 2 * 4 * HF], F32)).ap()
    t.stg2 = e(nc.sbuf_tensor("stg2", [128, 2 * HF], F32)).ap()
    t.sq2 = e(nc.sbuf_tensor("sq2", [128, 2 * HF], F32)).ap()
    t.probs = e(nc.sbuf_tensor("probs", [128, 3 * T], BF16)).ap()
    t.probsT = e(nc.sbuf_tensor("probsT", [128, T], BF16)).ap()
    t.attnT = e(nc.sbuf_tensor("attnT", [128, NH * 512], BF16)).ap()
    t.abf = e(nc.sbuf_tensor("abf", [128, 4 * HF], BF16)).ap()
    t.mxc = e(nc.sbuf_tensor("mxc", [128, 2], F32)).ap()
    t.srec = e(nc.sbuf_tensor("srec", [128, 4], F32)).ap()
    ya_rot = Rot(2, p, "yas")
    st2_rot = Rot(2, p, "st2s")
    sq2_rot = Rot(2)
    s_wkv = p.sem(_u("wkv"))
    s_wq = p.sem(_u("wq"))
    p.store_sems += st2_rot.sems
    for rot in (ya_rot, st2_rot, sq2_rot):
        for sl in range(rot.n):
            rot.set_rel(sl, bar)

    def _rows(sum_ps, ssq_ps, off):
        mean = t.trow[0:1, 0:HF]
        e2 = t.trow[0:1, HF:2 * HF]
        msq = t.trow[0:1, 2 * HF:3 * HF]
        rh = t.r_row[0:1, off:off + HF]
        mrh = t.mr_row[0:1, off:off + HF]
        wstat = [(t.s_pex, t.s_pex.n), (t.s_peq, t.s_peq.n)]
        p.v("tensor_scalar_mul", mean, sum_ps[:, :], 1.0 / D, wait=wstat)
        p.drain()
        p.v("tensor_scalar_mul", e2, ssq_ps[:, :], 1.0 / D)
        p.drain()
        p.v("tensor_mul", msq, mean, mean)
        p.drain()
        p.v("tensor_sub", e2, e2, msq, inc=t.s_row)
        vrow = t.s_row.n
        p.drain()
        lnv = t.trow[0:1, 2 * HF:3 * HF]
        p.act(lnv, e2, AF.Ln, scale=1.0, bias=t.eps_ln, wait=[(t.s_row, vrow)])
        p.drain("scalar")
        p.act(rh, lnv, AF.Exp, scale=-0.5, inc=t.s_r)
        vr = t.s_r.n
        p.v("scalar_tensor_tensor", mrh, mean, -1.0, rh, ALU.mult, ALU.mult,
            wait=[(t.s_r, vr)], inc=t.s_row)
        p.drain()
        return t.s_row.n

    def _u2_group(s_i, gc, rb_ps, mrb_ps, vrows, extra_pe_wait):
        p.mm(rb_ps[:, :], t.onesrow_sb[0:1, :], t.r_row[0:1, gc:gc + HF],
             start=True, stop=True,
             wait=extra_pe_wait + [(t.s_arow, vrows), (t.s_u, t.s_u.n)],
             inc=t.s_pec)
        p.mm(mrb_ps[:, :], t.onesrow_sb[0:1, :], t.mr_row[0:1, gc:gc + HF],
             start=True, stop=True, inc=t.s_pec)
        vub = t.s_pec.n
        y1r = t.y1d[s_i].rearrange("(k pp) tt -> pp k tt", pp=128)
        for kg in range(4):
            sl, w = ya_rot.acq()
            ysem = ya_rot.sems[sl]
            ygrp = t.ya[:, sl * 2048:(sl + 1) * 2048]
            p.dma("scalar", ygrp.rearrange("pp (k tt) -> pp k tt", k=4),
                  y1r[:, 4 * kg:4 * kg + 4, gc:gc + HF], wait=w, inc=ysem)
            ldv = ysem.n
            for kk in range(4):
                k = 4 * kg + kk
                yab = t.ya[:, sl * 2048 + kk * HF:sl * 2048 + (kk + 1) * HF]
                ssl, sw = sq2_rot.acq()
                tmp = t.sq2[:, ssl * HF:(ssl + 1) * HF]
                p.v("tensor_mul", tmp, yab, rb_ps[:, :],
                    wait=sw + [(ysem, ldv), (t.s_pec, vub), (t.s_p1, t.s_p1.n)])
                p.v("tensor_add", t.u2[:, k * HF:(k + 1) * HF], tmp,
                    mrb_ps[:, :], inc=t.s_u)
                uv = t.s_u.n
                sq2_rot.set_rel(ssl, [(t.s_u, uv)])
            ya_rot.set_rel(sl, [(t.s_u, t.s_u.n)])
        return t.s_u.n

    for s_i in range(2):
        p.dma("scalar", t.r_row[0:1, :], t.y1rows[s_i, 0:1, :],
              wait=[(t.s_yrw, t.s_yrw.n), (t.s_pec, t.s_pec.n),
                    (t.s_row, t.s_row.n), (t.s_ev, t.s_ev.n)], inc=t.s_arow)
        p.dma("scalar", t.mr_row[0:1, :], t.y1rows[s_i, 1:2, :], inc=t.s_arow)
        vrows_ld = t.s_arow.n

        with ExitStack() as ps:
            en = ps.enter_context
            wk_sb = en(nc.sbuf_tensor(_u("wk_sb"), [128, KD * 512], BF16)).ap()
            wv_sb = en(nc.sbuf_tensor(_u("wv_sb"), [128, KD * 512], BF16)).ap()
            kraw = en(nc.sbuf_tensor(_u("kraw"), [128, HF], F32)).ap()
            ksq = en(nc.sbuf_tensor(_u("ksq"), [128, HF], F32)).ap()
            rb_ps = en(nc.psum_tensor(_u("b_rb"), [128, HF], F32)).ap()
            mrb_ps = en(nc.psum_tensor(_u("b_mrb"), [128, HF], F32)).ap()
            kv_ps = [en(nc.psum_tensor(_u(f"b_kv{i}"), [128, HF], F32)).ap()
                     for i in range(2)]
            kn_ps = en(nc.psum_tensor(_u("b_kn"), [1, HF], F32)).ap()
            knb_ps = en(nc.psum_tensor(_u("b_knb"), [128, HF], F32)).ap()
            p.dma("gpsimd", wk_sb[:, :], t.wkp[:, :], wait=bar, inc=s_wkv)
            p.dma("gpsimd", wv_sb[:, :], t.wvp[:, :], inc=s_wkv)
            vwk = s_wkv.n
            kvrel = [pe_bar, pe_bar]
            for grp in range(4):
                gc = grp * HF
                uv = _u2_group(s_i, gc, rb_ps, mrb_ps, vrows_ld, pe_bar)
                pend_k = [None]
                for c in range(NKV):
                    pr = c % 2
                    for k in range(KD):
                        p.mm(kv_ps[pr][:, :],
                             wk_sb[:, k * 512 + c * 128:k * 512 + c * 128 + 128],
                             t.u2[:, k * HF:(k + 1) * HF],
                             start=(k == 0), stop=(k == KD - 1),
                             wait=(kvrel[pr] + [(s_wkv, vwk), (t.s_u, uv)]
                                   if k == 0 else []),
                             inc=(t.s_p1 if k == KD - 1 else None))
                    vk = t.s_p1.n
                    if pend_k[0] is not None:
                        pend_k[0]()

                    def make_kn(c=c, pr=pr, vk=vk, gc=gc):
                        def run():
                            p.v("tensor_copy", kraw[:, :], kv_ps[pr][:, :],
                                wait=[(t.s_p1, vk), (t.s_kv, t.s_kv.n)],
                                inc=t.s_kv)
                            kvrel[pr] = [(t.s_kv, t.s_kv.n)]
                            p.drain()
                            p.v("tensor_mul", ksq[:, :], kraw[:, :], kraw[:, :],
                                inc=t.s_sq)
                            vsq = t.s_sq.n
                            p.mm(kn_ps[:, :], t.onescol_sb[:, :], ksq[:, :],
                                 start=True, stop=True,
                                 wait=[(t.s_sq, vsq), (t.s_kv, t.s_kv.n)],
                                 inc=t.s_pec)
                            vkn = t.s_pec.n
                            kr0 = t.trow[0:1, 0:HF]
                            kr1 = t.trow[0:1, HF:2 * HF]
                            p.act(kr0, kn_ps[:, :], AF.Ln, scale=1.0 / 128.0,
                                  bias=t.eps_qk, wait=[(t.s_pec, vkn)])
                            p.drain("scalar")
                            p.act(kr1, kr0, AF.Exp, scale=-0.5, inc=t.s_r)
                            vr2 = t.s_r.n
                            p.mm(knb_ps[:, :], t.onesrow_sb[0:1, :], kr1,
                                 start=True, stop=True,
                                 wait=[(t.s_r, vr2), (t.s_kv, t.s_kv.n)],
                                 inc=t.s_pec)
                            vknb = t.s_pec.n
                            p.v("scalar_tensor_tensor",
                                t.KT[:, c * T + gc:c * T + gc + HF],
                                kraw[:, :], t.ksc_sb[:, :], knb_ps[:, :],
                                ALU.mult, ALU.mult,
                                wait=[(t.s_pec, vknb)], inc=t.s_kv)
                            p.drain()
                        return run
                    pend_k[0] = make_kn()
                pend_k[0]()
                for tt in range(4):
                    pr = tt % 2
                    for k in range(KD):
                        p.mm(kv_ps[pr][:, :],
                             t.u2[:, k * HF + tt * 128:k * HF + tt * 128 + 128],
                             wv_sb[:, k * 512:(k + 1) * 512],
                             start=(k == 0), stop=(k == KD - 1),
                             wait=(kvrel[pr] if k == 0 else []),
                             inc=(t.s_p1 if k == KD - 1 else None))
                    vv = t.s_p1.n
                    p.v("tensor_copy",
                        t.V[:, (grp * 4 + tt) * 512:(grp * 4 + tt) * 512 + 512],
                        kv_ps[pr][:, :], wait=[(t.s_p1, vv)], inc=t.s_kv)
                    kvrel[pr] = [(t.s_kv, t.s_kv.n)]
            b2_p1 = t.s_p1.n
            b2_kv = t.s_kv.n

        with ExitStack() as ps:
            en = ps.enter_context
            wq_sb = en(nc.sbuf_tensor(_u("wq_sb"), [128, 2 * 2048], BF16)).ap()
            wq_rot = Rot(2, p, "wqs")
            q4 = en(nc.sbuf_tensor(_u("q4"), [128, NH * 512], BF16)).ap()
            qraw = en(nc.sbuf_tensor(_u("qraw"), [128, HF], F32)).ap()
            qsq = en(nc.sbuf_tensor(_u("qsq"), [128, HF], F32)).ap()
            wo_buf = en(nc.sbuf_tensor(_u("wo_buf"), [128, 2 * 2048], BF16)).ap()
            wo_rot = Rot(2, p, "wos")
            for sl in range(2):
                wo_rot.set_rel(sl, [(t.s_p1, b2_p1), (t.s_kv, b2_kv)])
                wq_rot.set_rel(sl, [(t.s_p1, b2_p1), (t.s_kv, b2_kv)])
            oc_prev_grp = t.s_oc.n
            for grp in range(4):
                gc = grp * HF
                with ExitStack() as p2:
                    en2 = p2.enter_context
                    rb_ps = en2(nc.psum_tensor(_u("c_rb"), [128, HF], F32)).ap()
                    mrb_ps = en2(nc.psum_tensor(_u("c_mrb"), [128, HF], F32)).ap()
                    q_ps = [en2(nc.psum_tensor(_u(f"c_q{i}"), [128, HF], F32)).ap()
                            for i in range(2)]
                    qn_ps = en2(nc.psum_tensor(_u("c_qn"), [1, HF], F32)).ap()
                    qnb_ps = en2(nc.psum_tensor(_u("c_qnb"), [128, HF], F32)).ap()
                    ga = [(t.s_ev, t.s_ev.n), (t.s_kv, t.s_kv.n),
                          (t.s_qn, t.s_qn.n)]
                    uv = _u2_group(s_i, gc, rb_ps, mrb_ps, vrows_ld, ga)
                    qrel = [ga, ga]
                    pend_q = [None]
                    for m in range(NH):
                        pr = m % 2
                        qwsl, qww = wq_rot.acq()
                        qwsem = wq_rot.sems[qwsl]
                        wqs = wq_sb[:, qwsl * 2048:(qwsl + 1) * 2048]
                        vldq, qwrel = load_cached(p, t, ("wq", m), t.wqp[m],
                                                  t.wqb[m], wqs, qwsem, qww)
                        for k in range(KD):
                            p.mm(q_ps[pr][:, :],
                                 wqs[:, k * 128:(k + 1) * 128],
                                 t.u2[:, k * HF:(k + 1) * HF],
                                 start=(k == 0), stop=(k == KD - 1),
                                 wait=(qrel[pr] + [(qwsem, vldq), (t.s_u, uv)]
                                       if k == 0 else []),
                                 inc=(t.s_p1 if k == KD - 1 else None))
                        vq = t.s_p1.n
                        wq_rot.set_rel(qwsl, [(t.s_p1, vq)] + qwrel)
                        if pend_q[0] is not None:
                            pend_q[0]()

                        def make_qn(m=m, pr=pr, vq=vq):
                            def run():
                                p.v("tensor_copy", qraw[:, :], q_ps[pr][:, :],
                                    wait=[(t.s_p1, vq), (t.s_qn, t.s_qn.n)],
                                    inc=t.s_kv)
                                qrel[pr] = [(t.s_kv, t.s_kv.n)]
                                p.drain()
                                p.v("tensor_mul", qsq[:, :], qraw[:, :],
                                    qraw[:, :], inc=t.s_sq)
                                vsq = t.s_sq.n
                                p.mm(qn_ps[:, :], t.onescol_sb[:, :], qsq[:, :],
                                     start=True, stop=True,
                                     wait=[(t.s_sq, vsq), (t.s_qn, t.s_qn.n)],
                                     inc=t.s_pec)
                                vqn = t.s_pec.n
                                qr0 = t.trow[0:1, 0:HF]
                                qr1 = t.trow[0:1, HF:2 * HF]
                                p.act(qr0, qn_ps[:, :], AF.Ln, scale=1.0 / 128.0,
                                      bias=t.eps_qk, wait=[(t.s_pec, vqn)])
                                p.drain("scalar")
                                p.act(qr1, qr0, AF.Exp, scale=-0.5, inc=t.s_r)
                                vr2 = t.s_r.n
                                p.mm(qnb_ps[:, :], t.onesrow_sb[0:1, :], qr1,
                                     start=True, stop=True,
                                     wait=[(t.s_r, vr2), (t.s_qn, t.s_qn.n)],
                                     inc=t.s_pec)
                                vqnb = t.s_pec.n
                                p.v("scalar_tensor_tensor",
                                    q4[:, m * 512 + 0:m * 512 + 512],
                                    qraw[:, :], t.qsc_sb[:, :], qnb_ps[:, :],
                                    ALU.mult, ALU.mult,
                                    wait=[(t.s_pec, vqnb)], inc=t.s_qn)
                                p.drain()
                            return run
                        pend_q[0] = make_qn()
                    pend_q[0]()
                    q4_done = t.s_qn.n
                with ExitStack() as p2:
                    en2 = p2.enter_context
                    R = en2(nc.psum_tensor(_u("c_R"), [128, 5 * HF], F32)).ap()
                    pT_ps = [en2(nc.psum_tensor(_u(f"c_pT{i}"), [128, 512], BF16)).ap()
                             for i in range(2)]
                    out_ps = en2(nc.psum_tensor(_u("c_out"), [128, 128], F32)).ap()
                    ga0 = [(t.s_qn, q4_done), (t.s_kv, t.s_kv.n)]
                    unit_rel = [list(ga0) for _ in range(5)]
                    pT_rel = [list(ga0), list(ga0)]
                    out_rel = [list(ga0)]
                    pp_rel = [[], [], []]
                    upos = 0
                    pend_tm = []
                    for tl in range(4):
                        ti = grp * 4 + tl
                        nch = ti + 1
                        tsl = nch * 128
                        nun = (tsl + HF - 1) // HF
                        for hh in range(NH):
                            kv = hh // NKV
                            pp = (tl * NH + hh) % 3
                            units = [(upos + i) % 5 for i in range(nun)]
                            upos = (upos + nun) % 5
                            segs = []
                            sb = 0
                            for i in range(1, nun + 1):
                                if i == nun or units[i] != units[i - 1] + 1:
                                    segs.append((sb, i))
                                    sb = i
                            for b, u in enumerate(units):
                                nb = min(HF, tsl - b * HF)
                                p.mm(R[:, u * HF:u * HF + nb],
                                     q4[:, hh * 512 + tl * 128:hh * 512 + tl * 128 + 128],
                                     t.KT[:, kv * T + b * HF:kv * T + b * HF + nb],
                                     start=True, stop=True,
                                     wait=unit_rel[u] + ([(t.s_qn, q4_done)]
                                                         if b == 0 else []),
                                     inc=(t.s_sc2 if b == nun - 1 else None))
                            vsc = t.s_sc2.n
                            bd = (ti * 128) // HF
                            ud = units[bd]
                            off = (ti * 128) % HF
                            p.v("tensor_add", R[:, ud * HF + off:ud * HF + off + 128],
                                R[:, ud * HF + off:ud * HF + off + 128],
                                t.causal_sb[:, :], wait=[(t.s_sc2, vsc)])
                            p.drain()
                            for si, (a, b_) in enumerate(segs):
                                u0 = units[a]
                                ln = min((b_ - a) * HF, tsl - a * HF)
                                p.v("reduce_max", t.mxc[:, si:si + 1],
                                    R[:, u0 * HF:u0 * HF + ln], AX, negate=True)
                                p.drain()
                            if len(segs) == 2:
                                p.v("tensor_tensor", t.mxc[:, 0:1], t.mxc[:, 0:1],
                                    t.mxc[:, 1:2], ALU.min)
                                p.drain()
                            p.v("tensor_scalar_mul", t.mxc[:, 1:2], t.mxc[:, 0:1],
                                SCL, inc=t.s_mx)
                            vmx = t.s_mx.n
                            for si, (a, b_) in enumerate(segs):
                                u0 = units[a]
                                ln = min((b_ - a) * HF, tsl - a * HF)
                                p.act(t.probs[:, pp * T + a * HF:pp * T + a * HF + ln],
                                      R[:, u0 * HF:u0 * HF + ln], AF.Exp,
                                      scale=SCL, bias=t.mxc[:, 1:2],
                                      wait=(([(t.s_mx, vmx)] + pp_rel[pp])
                                            if si == 0 else []),
                                      inc=t.s_exp, acc=t.srec[:, 2 + si:3 + si])
                                self_rel = [(t.s_exp, t.s_exp.n)]
                                for u in units[a:b_]:
                                    unit_rel[u] = self_rel
                            vexp = t.s_exp.n
                            prb = t.probs[:, pp * T:pp * T + tsl]
                            if len(segs) == 2:
                                p.v("tensor_add", t.srec[:, 2:3], t.srec[:, 2:3],
                                    t.srec[:, 3:4], wait=[(t.s_exp, vexp)])
                                p.drain()
                            p.v("reciprocal", t.srec[:, 1:2], t.srec[:, 2:3],
                                wait=[(t.s_exp, vexp)])
                            p.drain()
                            p.v("tensor_scalar_mul", prb, prb, t.srec[:, 1:2],
                                inc=t.s_pn)
                            vpn = t.s_pn.n
                            if len(pend_tm) >= 2:
                                pend_tm.pop(0)()

                            def make_tm(nch=nch, pp=pp, kv=kv, tl=tl, hh=hh,
                                        vpn=vpn):
                                def run():
                                    nbt = (nch + 3) // 4
                                    vpc_b = {}

                                    def emit_tbatch(g):
                                        j0, j1 = 4 * g, min(4 * g + 4, nch)
                                        for j in range(j0, j1):
                                            p.tr(pT_ps[g % 2][:, (j - j0) * 128:(j - j0 + 1) * 128],
                                                 t.probs[:, pp * T + j * 128:pp * T + (j + 1) * 128],
                                                 t.idb_sb[:, :],
                                                 wait=(pT_rel[g % 2] + [(t.s_pn, vpn)]
                                                       if j == j0 else []),
                                                 inc=(t.s_pec if j == j1 - 1 else None))
                                        vtr = t.s_pec.n
                                        p.acopy(t.probsT[:, j0 * 128:j1 * 128],
                                                pT_ps[g % 2][:, 0:(j1 - j0) * 128],
                                                wait=[(t.s_pec, vtr)], inc=t.s_pc)
                                        vpc_b[g] = t.s_pc.n
                                        pT_rel[g % 2] = [(t.s_pc, vpc_b[g])]

                                    def emit_mbatch(g):
                                        j0, j1 = 4 * g, min(4 * g + 4, nch)
                                        for j in range(j0, j1):
                                            p.mm(out_ps[:, :],
                                                 t.V[:, j * 512 + kv * 128:j * 512 + kv * 128 + 128],
                                                 t.probsT[:, j * 128:(j + 1) * 128],
                                                 start=(j == 0), stop=(j == nch - 1),
                                                 wait=(([(t.s_pc, vpc_b[g])] +
                                                        (out_rel[0] if j == 0 else []))
                                                       if j == j0 else []),
                                                 inc=(t.s_out if j == nch - 1 else None))

                                    emit_tbatch(0)
                                    for g in range(1, nbt):
                                        emit_tbatch(g)
                                        emit_mbatch(g - 1)
                                    emit_mbatch(nbt - 1)
                                    pp_rel[pp] = [(t.s_pec, t.s_pec.n)]
                                    vout = t.s_out.n
                                    p.acopy(t.attnT[:, hh * 512 + tl * 128:hh * 512 + tl * 128 + 128],
                                            out_ps[:, :],
                                            wait=[(t.s_out, vout), (t.s_p2, t.s_p2.n)],
                                            inc=t.s_oc)
                                    out_rel[0] = [(t.s_oc, t.s_oc.n)]
                                return run
                            pend_tm.append(make_tm())
                    while pend_tm:
                        pend_tm.pop(0)()
                    grp_oc = t.s_oc.n
                with ExitStack() as p2:
                    en2 = p2.enter_context
                    y2_ps = [en2(nc.psum_tensor(_u(f"c_y2{i}"), [128, HF], F32)).ap()
                             for i in range(2)]
                    ysum_ps = en2(nc.psum_tensor(_u("c_ysum"), [1, HF], F32)).ap()
                    yssq_ps = en2(nc.psum_tensor(_u("c_yssq"), [1, HF], F32)).ap()
                    y2rel = [[(t.s_oc, grp_oc)], [(t.s_oc, grp_oc)]]
                    abf_guard = [[], []]
                    y1r = t.y1d[s_i].rearrange("(k pp) tt -> pp k tt", pp=128)
                    for d in range(KD):
                        pr = d % 2
                        if d % 4 == 0:
                            ysl, yw = ya_rot.acq()
                            ysem = ya_rot.sems[ysl]
                            ygrp = t.ya[:, ysl * 2048:(ysl + 1) * 2048]
                            p.dma("scalar",
                                  ygrp.rearrange("pp (k tt) -> pp k tt", k=4),
                                  y1r[:, d:d + 4, gc:gc + HF], wait=yw, inc=ysem)
                            vldy = ysem.n
                        wsl, ww = wo_rot.acq()
                        wosem = wo_rot.sems[wsl]
                        wos = wo_buf[:, wsl * 2048:(wsl + 1) * 2048]
                        vldo, worel = load_cached(p, t, ("wo", d), t.wop[d],
                                                  t.wob[d], wos, wosem, ww)
                        for k in range(KD):
                            p.mm(y2_ps[pr][:, :], wos[:, k * 128:(k + 1) * 128],
                                 t.attnT[:, k * 512:(k + 1) * 512],
                                 start=(k == 0), stop=(k == KD - 1),
                                 wait=(y2rel[pr] + [(wosem, vldo),
                                                    (t.s_oc, grp_oc)]
                                       if k == 0 else []),
                                 inc=(t.s_p2 if k == KD - 1 else None))
                        vwo = t.s_p2.n
                        wo_rot.set_rel(wsl, [(t.s_p2, vwo)] + worel)
                        yab = t.ya[:, ysl * 2048 + (d % 4) * HF:
                                   ysl * 2048 + (d % 4 + 1) * HF]
                        ssl, sw = st2_rot.acq()
                        ssem = st2_rot.sems[ssl]
                        stg = t.stg2[:, ssl * HF:(ssl + 1) * HF]
                        p.v("tensor_add", stg, y2_ps[pr][:, :], yab,
                            wait=sw + [(t.s_p2, vwo), (ysem, vldy)],
                            inc=t.s_ev)
                        vev = t.s_ev.n
                        y2rel[pr] = [(t.s_ev, vev)]
                        p.dma("sync", t.y2d[s_i, d * 128:(d + 1) * 128, gc:gc + HF],
                              stg, wait=[(t.s_ev, vev)], inc=ssem)
                        st2_rot.set_rel(ssl, [(ssem, ssem.n)])
                        if d % 4 == 3:
                            ya_rot.set_rel(ysl, [(t.s_ev, t.s_ev.n)])
                        base = (d % 2) * 2 * HF
                        yc = t.abf[:, base:base + HF]
                        yq = t.abf[:, base + HF:base + 2 * HF]
                        p.v("tensor_copy", yc, stg, wait=abf_guard[d % 2])
                        p.drain()
                        p.v("tensor_mul", yq, yc, yc, inc=t.s_sq)
                        vsq2 = t.s_sq.n
                        p.mm(ysum_ps[:, :], t.onescol_bf[:, :], yc,
                             start=(d == 0), stop=(d == KD - 1),
                             wait=([(t.s_row, t.s_row.n)] if d == 0 else [])
                             + [(t.s_sq, vsq2)], inc=t.s_pex)
                        p.mm(yssq_ps[:, :], t.onescol_bf[:, :], yq,
                             start=(d == 0), stop=(d == KD - 1), inc=t.s_peq)
                        abf_guard[d % 2] = [(t.s_peq, t.s_peq.n)]
                    mean = t.trow[0:1, 0:HF]
                    e2r = t.trow[0:1, HF:2 * HF]
                    tmp2 = t.trow[0:1, 2 * HF:3 * HF]
                    rr = t.sq2[0:1, 0:HF]
                    wstat = [(t.s_pex, t.s_pex.n), (t.s_peq, t.s_peq.n)]
                    p.v("tensor_scalar_mul", mean, ysum_ps[:, :], 1.0 / D,
                        wait=wstat)
                    p.drain()
                    p.v("tensor_scalar_mul", e2r, yssq_ps[:, :], 1.0 / D)
                    p.drain()
                    p.v("tensor_mul", tmp2, mean, mean)
                    p.drain()
                    p.v("tensor_sub", e2r, e2r, tmp2, inc=t.s_row)
                    vr0 = t.s_row.n
                    p.drain()
                    p.act(tmp2, e2r, AF.Ln, scale=1.0, bias=t.eps_ln,
                          wait=[(t.s_row, vr0)])
                    p.drain("scalar")
                    p.act(rr, tmp2, AF.Exp, scale=-0.5, inc=t.s_r)
                    vrr = t.s_r.n
                    p.v("scalar_tensor_tensor", e2r, mean, -1.0, rr,
                        ALU.mult, ALU.mult, wait=[(t.s_r, vrr)], inc=t.s_row)
                    vmr2 = t.s_row.n
                    p.drain()
                    p.dma("sync", t.y2rows[s_i, 0:1, gc:gc + HF], rr,
                          wait=[(t.s_r, vrr)], inc=t.s_y2rw)
                    p.dma("sync", t.y2rows[s_i, 1:2, gc:gc + HF], e2r,
                          wait=[(t.s_row, vmr2)], inc=t.s_y2rw)



def _prep_host(inputs):
    def gi(k):
        return np.ascontiguousarray(np.asarray(inputs[k], dtype=np.float32))
    x = gi("L0_attn_x")
    w1 = np.concatenate([gi("L0_ffn_routed_w1"), gi("L0_ffn_shared_w1")], 0)
    w2 = np.concatenate([gi("L0_ffn_routed_w2"), gi("L0_ffn_shared_w2")], 0)
    b1 = np.concatenate([gi("L0_ffn_routed_b1"), gi("L0_ffn_shared_b1")], 0)
    b2 = np.concatenate([gi("L0_ffn_routed_b2"), gi("L0_ffn_shared_b2")], 0)
    wq, wk = gi("L0_attn_Wq"), gi("L0_attn_Wk")
    wv, wo = gi("L0_attn_Wv"), gi("L0_attn_Wo")
    cent = gi("L0_ffn_centroids")
    d = {}
    d["w1p"] = np.ascontiguousarray(
        w1.reshape(NE, KD, 128, KF, 128).transpose(0, 3, 2, 1, 4).reshape(NE, KF, 128, KD * 128))
    d["w2p"] = np.ascontiguousarray(
        w2.reshape(NE, KF, 128, KD, 128).transpose(0, 3, 2, 1, 4).reshape(NE, KD, 128, KF * 128))
    d["wqp"] = np.ascontiguousarray(
        wq.reshape(NH, 128, KD, 128).transpose(0, 3, 2, 1).reshape(NH, 128, KD * 128))
    d["wop"] = np.ascontiguousarray(
        wo.reshape(KD, 128, KD, 128).transpose(0, 3, 2, 1).reshape(KD, 128, KD * 128))
    d["wkp"] = np.ascontiguousarray(wk.reshape(512, KD, 128).transpose(2, 1, 0).reshape(128, KD * 512))
    d["wvp"] = np.ascontiguousarray(wv.reshape(512, KD, 128).transpose(2, 1, 0).reshape(128, KD * 512))
    d["centp"] = np.ascontiguousarray(cent.reshape(16, KD, 128).transpose(2, 1, 0).reshape(128, KD * 16))
    d["b1c"] = np.ascontiguousarray(b1.reshape(NE, KF, 128).transpose(2, 0, 1).reshape(128, NE * KF))
    d["b2r"] = b2
    d["rbias"] = gi("L0_ffn_bias").reshape(16, 1)
    d["qsc"] = gi("L0_attn_q_scale").reshape(128, 1)
    d["ksc"] = gi("L0_attn_k_scale").reshape(128, 1)
    d["idf"] = np.eye(128, dtype=np.float32)
    d["causal"] = np.triu(np.full((128, 128), NEG, dtype=np.float32), k=1)
    d["onescol"] = np.ones((128, 1), np.float32)
    d["onesrow"] = np.ones((1, 128), np.float32)
    d["sel"] = np.repeat(np.eye(16, dtype=np.float32), 128, axis=1)
    d["cvec"] = np.array([[1e-5, 1e-6, 0, 0, 0, 0, 0, 0]], np.float32)
    xcores = [np.ascontiguousarray(x[2 * c:2 * c + 2].transpose(0, 2, 1)) for c in range(8)]
    return d, xcores


def kernel(**inputs):
    shared, xcores = _prep_host(inputs)
    cfg = dict(ffn1=True, attn=True, ffn2=True, n_mb=4)
    nc = build_nc(cfg)
    in_maps = [dict(shared, xT=xcores[c]) for c in range(8)]
    res = run_bass_kernel_spmd(nc, in_maps, core_ids=list(range(8)))
    parts = [res.results[c]["outT"].transpose(0, 2, 1) for c in range(8)]
    return np.ascontiguousarray(np.concatenate(parts, axis=0))


# version 06
# speedup vs baseline: 1.0877x; 1.0846x over previous
from contextlib import ExitStack

import numpy as np

import concourse.bass as bass
import concourse.mybir as mybir
from concourse.bass_utils import run_bass_kernel_spmd

F32 = mybir.dt.float32
BF16 = mybir.dt.bfloat16
AF = mybir.ActivationFunctionType
ALU = mybir.AluOpType
AX = mybir.AxisListType.X

D, KD = 2048, 16
DFF, KF = 1408, 11
NE, NRT = 18, 16
T = 2048
MB, HF = 1024, 512
NH, NKV, HD = 16, 4, 128
SCL = float(1.0 / np.sqrt(128.0))
NEG = -1.0e30
ENGINES = ("tensor", "vector", "scalar", "sync", "gpsimd")
EXPERT_ORDER = [16, 17] + list(range(16))
STATS_BF16 = False


class Sem:
    __slots__ = ("h", "n", "name")


class Rot:

    def __init__(self, n, p=None, tag=None):
        self.n = n
        self.rel = [()] * n
        self.i = -1
        self.sems = ([p.sem(_u(tag + str(i))) for i in range(n)]
                     if p is not None else None)

    def acq(self):
        self.i += 1
        sl = self.i % self.n
        w = self.rel[sl]
        self.rel[sl] = ()
        return sl, list(w)

    def set_rel(self, sl, conds):
        self.rel[sl] = tuple(conds)


class Prog:
    def __init__(self, nc, ctx):
        self.nc, self.ctx = nc, ctx
        self.ops = {e: [] for e in ENGINES}
        self.known = {e: {} for e in ENGINES}
        self.store_sems = []

    def stores_done(self):
        return [(s, s.n) for s in self.store_sems if s.n > 0]

    def sem(self, name):
        s = Sem()
        s.h = self.ctx.enter_context(self.nc.semaphore(name))
        s.n = 0
        s.name = name
        return s

    def emit(self, eng, fn, wait=(), inc=None, incv=1):
        wl = []
        for (s, v) in wait:
            if v > self.known[eng].get(s, 0):
                self.known[eng][s] = v
                wl.append((s, v))
        if inc is not None:
            inc.n += incv
        self.ops[eng].append((wl, fn, inc, incv))

    def dma(self, q, out, in_, wait=(), inc=None):
        self.emit(q, (lambda e, o=out, i=in_: e.dma_start(out=o, in_=i)), wait, inc, 16)

    def mm(self, out, lhsT, rhs, start, stop, wait=(), inc=None):
        self.emit(
            "tensor",
            (lambda e, o=out, l=lhsT, r=rhs, a=start, b=stop:
             e.matmul(o, l, r, start=a, stop=b, skip_group_check=True)),
            wait, inc, 1)

    def tr(self, out, in_, ident, wait=(), inc=None):
        self.emit("tensor", (lambda e, o=out, i=in_, d=ident: e.transpose(o, i, d)),
                  wait, inc, 1)

    def act(self, out, in_, func, scale=1.0, bias=None, wait=(), inc=None,
            acc=None):
        kw = {}
        if bias is not None:
            kw["bias"] = bias
        if acc is not None:
            kw["accum_out"] = acc
        fn = (lambda e, o=out, i=in_, f=func, s=scale, k=kw:
              e.activation(o, i, f, scale=s, **k))
        self.emit("scalar", fn, wait, inc, 1)

    def acopy(self, out, in_, wait=(), inc=None):
        self.act(out, in_, AF.Copy, wait=wait, inc=inc)

    def drain(self, eng="vector"):
        self.emit(eng, lambda e: e.drain())

    def v(self, name, *args, wait=(), inc=None, **kw):
        self.emit("vector",
                  (lambda e, n=name, a=args, k=kw: getattr(e, n)(*a, **k)),
                  wait, inc, 1)

    def finalize(self, block):
        def mk(eng_name):
            ops = self.ops[eng_name]

            def run(engine):
                for (wl, fn, inc, incv) in ops:
                    for (s, vv) in wl:
                        engine.wait_ge(s.h, vv)
                    if fn is None:
                        continue
                    inst = fn(engine)
                    if inc is not None:
                        inst.then_inc(inc.h, incv)
            return run
        block.tensor(mk("tensor"))
        block.vector(mk("vector"))
        block.scalar(mk("scalar"))
        block.sync(mk("sync"))
        block.gpsimd(mk("gpsimd"))


class NS:
    pass


_UID = [0]


def _u(name):
    _UID[0] += 1
    return f"{name}_{_UID[0]}"


def build_nc(cfg):
    nc = bass.Bass("TRN2", target_bir_lowering=False)
    ctx = ExitStack()
    p = Prog(nc, ctx)
    t = NS()
    e = ctx.enter_context

    t.xT = nc.dram_tensor("xT", [2, D, T], F32, kind="ExternalInput").ap()
    t.w1p = nc.dram_tensor("w1p", [NE, KF, 128, KD * 128], F32, kind="ExternalInput").ap()
    t.w2p = nc.dram_tensor("w2p", [NE, KD, 128, KF * 128], F32, kind="ExternalInput").ap()
    t.wqp = nc.dram_tensor("wqp", [NH, 128, KD * 128], F32, kind="ExternalInput").ap()
    t.wop = nc.dram_tensor("wop", [KD, 128, KD * 128], F32, kind="ExternalInput").ap()
    t.wkp = nc.dram_tensor("wkp", [128, KD * 512], F32, kind="ExternalInput").ap()
    t.wvp = nc.dram_tensor("wvp", [128, KD * 512], F32, kind="ExternalInput").ap()
    t.centp = nc.dram_tensor("centp", [128, KD * 16], F32, kind="ExternalInput").ap()
    t.b1c = nc.dram_tensor("b1c", [128, NE * KF], F32, kind="ExternalInput").ap()
    t.b2r = nc.dram_tensor("b2r", [NE, D], F32, kind="ExternalInput").ap()
    t.rbias = nc.dram_tensor("rbias", [16, 1], F32, kind="ExternalInput").ap()
    t.qsc = nc.dram_tensor("qsc", [128, 1], F32, kind="ExternalInput").ap()
    t.ksc = nc.dram_tensor("ksc", [128, 1], F32, kind="ExternalInput").ap()
    t.idf = nc.dram_tensor("idf", [128, 128], F32, kind="ExternalInput").ap()
    t.causal = nc.dram_tensor("causal", [128, 128], F32, kind="ExternalInput").ap()
    t.onescol = nc.dram_tensor("onescol", [128, 1], F32, kind="ExternalInput").ap()
    t.onesrow = nc.dram_tensor("onesrow", [1, 128], F32, kind="ExternalInput").ap()
    t.sel = nc.dram_tensor("sel", [16, 16 * 128], F32, kind="ExternalInput").ap()
    t.cvec = nc.dram_tensor("cvec", [1, 8], F32, kind="ExternalInput").ap()

    y1_kind = ("ExternalInput" if cfg.get("y1_input") else
               "ExternalOutput" if cfg.get("expose_y1") else "Internal")
    y2_kind = "ExternalOutput" if cfg.get("expose_y2") else "Internal"
    t.y1d = nc.dram_tensor("y1d", [2, D, T], F32, kind=y1_kind).ap()
    t.y2d = nc.dram_tensor("y2d", [2, D, T], F32, kind=y2_kind).ap()
    t.outT = nc.dram_tensor("outT", [2, D, T], F32, kind="ExternalOutput").ap()
    t.w1b = nc.dram_tensor("w1b", [NE, KF, 128, KD * 128], BF16, kind="Internal").ap()
    t.w2b = nc.dram_tensor("w2b", [NE, KD, 128, KF * 128], BF16, kind="Internal").ap()
    t.wqb = nc.dram_tensor("wqb", [NH, 128, KD * 128], BF16, kind="Internal").ap()
    t.wob = nc.dram_tensor("wob", [KD, 128, KD * 128], BF16, kind="Internal").ap()
    t.wcache = {}
    t.overlap_stats = not cfg.get("no_overlap", False)
    if cfg.get("dbg"):
        t.dbg_sc = nc.dram_tensor("dbg_sc", [16, MB], F32, kind="ExternalOutput").ap()
        t.dbg_g = nc.dram_tensor("dbg_g", [NE, MB], BF16, kind="ExternalOutput").ap()
        t.dbg_u = nc.dram_tensor("dbg_u", [128, KD * MB], BF16, kind="ExternalOutput").ap()
        t.dbg_y = nc.dram_tensor("dbg_y", [128, KD * MB], F32, kind="ExternalOutput").ap()
        t.dbg_tr = nc.dram_tensor("dbg_tr", [1, 2 * MB], F32, kind="ExternalOutput").ap()
    t.dbg = cfg.get("dbg", False)
    rk = ("ExternalInput" if cfg.get("rows_input") else
          "ExternalOutput" if cfg.get("expose_rows") else "Internal")
    t.y1rows = nc.dram_tensor("y1rows", [2, 2, T], F32, kind=rk).ap()
    t.y2rows = nc.dram_tensor("y2rows", [2, 2, T], F32, kind="Internal").ap()

    t.idf_sb = e(nc.sbuf_tensor("idf_sb", [128, 128], F32)).ap()
    t.idb_sb = e(nc.sbuf_tensor("idb_sb", [128, 128], BF16)).ap()
    t.causal_sb = e(nc.sbuf_tensor("causal_sb", [128, 128], F32)).ap()
    t.onescol_sb = e(nc.sbuf_tensor("onescol_sb", [128, 1], F32)).ap()
    t.onesrow_sb = e(nc.sbuf_tensor("onesrow_sb", [1, 128], F32)).ap()
    t.cvec_sb = e(nc.sbuf_tensor("cvec_sb", [1, 8], F32)).ap()
    t.qsc_sb = e(nc.sbuf_tensor("qsc_sb", [128, 1], F32)).ap()
    t.ksc_sb = e(nc.sbuf_tensor("ksc_sb", [128, 1], F32)).ap()
    t.rbias_sb = e(nc.sbuf_tensor("rbias_sb", [16, 1], F32)).ap()
    t.onescol_bf = e(nc.sbuf_tensor("onescol_bf", [128, 1], BF16)).ap()
    t.r_row = e(nc.sbuf_tensor("r_row", [1, T], F32)).ap()
    t.mr_row = e(nc.sbuf_tensor("mr_row", [1, T], F32)).ap()
    t.trow = e(nc.sbuf_tensor("trow", [1, 3 * HF], F32)).ap()

    t.s_prep = p.sem("s_prep")
    t.s_prep2 = p.sem("s_prep2")
    for nm in ("s_ldx", "s_sq", "s_pex", "s_peq", "s_pec", "s_row", "s_r",
               "s_u", "s_scf", "s_gt", "s_ld1", "s_ld2", "s_p1", "s_rl",
               "s_hd", "s_gb", "s_p2", "s_ya", "s_ev", "s_yst",
               "s_kv", "s_sc2", "s_mx", "s_exp", "s_pn", "s_pc", "s_out",
               "s_oc", "s_qn", "s_yrw", "s_y2rw", "s_arow"):
        setattr(t, nm, p.sem(nm))
    t.s_rwl = [p.sem("s_rwl0"), p.sem("s_rwl1")]

    for (dst, src) in ((t.idf_sb, t.idf), (t.causal_sb, t.causal),
                       (t.onescol_sb, t.onescol), (t.onesrow_sb, t.onesrow),
                       (t.cvec_sb, t.cvec), (t.qsc_sb, t.qsc), (t.ksc_sb, t.ksc),
                       (t.rbias_sb, t.rbias)):
        p.dma("scalar", dst[:, :], src[:, :], inc=t.s_prep)
    p.dma("gpsimd", t.idb_sb[:, :], t.idf[:, :], inc=t.s_prep)
    p.dma("gpsimd", t.onescol_bf[:, :], t.onescol[:, :], inc=t.s_prep)
    t.eps_ln = t.cvec_sb[0:1, 0:1]
    t.eps_qk = t.cvec_sb[0:1, 1:2]

    if cfg.get("ffn1", True):
        with ExitStack() as fctx:
            emit_ffn_arena(p, t, nc, fctx, barrier=[])
            emit_ffn(p, t, t.xT, t.y1d, cfg.get("n_mb", 4), first_call=True,
                     rows_dst=t.y1rows)
    if cfg.get("attn", True):
        with ExitStack() as actx:
            emit_attn(p, t, nc, actx, cfg)
    if cfg.get("ffn2", True):
        with ExitStack() as fctx2:
            emit_ffn_arena(p, t, nc, fctx2, barrier=p.stores_done())
            emit_ffn(p, t, t.y2d, t.outT, cfg.get("n_mb", 4), first_call=False,
                     rows_src=t.y2rows)
    p.emit("vector", None, wait=p.stores_done())

    with nc.Block() as block:
        p.finalize(block)
    ctx.close()
    return nc


def load_cached(p, t, key, src_f32, cache, slot_ap, slot_sem, ww):
    if key not in t.wcache:
        p.dma("gpsimd", slot_ap, src_f32, wait=ww, inc=slot_sem)
        vld = slot_sem.n
        p.dma("sync", cache, slot_ap, wait=[(slot_sem, vld)], inc=slot_sem)
        t.wcache[key] = True
        return vld, [(slot_sem, slot_sem.n)]
    p.dma("gpsimd", slot_ap, cache, wait=ww, inc=slot_sem)
    return slot_sem.n, []


def emit_ffn_arena(p, t, nc, ctx, barrier):
    e = ctx.enter_context
    t.u_bf = e(nc.sbuf_tensor(_u("u_bf"), [128, KD * MB], BF16)).ap()
    t.y_sb = e(nc.sbuf_tensor(_u("y_sb"), [128, KD * MB], F32)).ap()
    t.h_bf = e(nc.sbuf_tensor(_u("h_bf"), [128, KF * MB], BF16)).ap()
    t.h_tmp = e(nc.sbuf_tensor(_u("h_tmp"), [128, MB], F32)).ap()
    t.w1buf = e(nc.sbuf_tensor(_u("w1buf"), [128, 3 * KD * 128], BF16)).ap()
    t.w2buf = e(nc.sbuf_tensor(_u("w2buf"), [128, 2 * KF * 128], BF16)).ap()
    t.g_bc = e(nc.sbuf_tensor(_u("g_bc"), [128, 2 * MB], BF16)).ap()
    t.xa = e(nc.sbuf_tensor(_u("xa"), [128, 4 * HF * 2], F32)).ap()
    t.scr = e(nc.sbuf_tensor(_u("scr"), [128, 2 * HF], F32)).ap()
    t.sel_sb = e(nc.sbuf_tensor(_u("sel_sb"), [16, 16 * 128], BF16)).ap()
    t.g_aug = e(nc.sbuf_tensor(_u("g_aug"), [NE, MB], BF16)).ap()
    t.b2_sb = e(nc.sbuf_tensor(_u("b2_sb"), [NE, D], BF16)).ap()
    t.scx_sb = e(nc.sbuf_tensor(_u("scx_sb"), [16, HF], F32)).ap()
    t.b1c_sb = e(nc.sbuf_tensor(_u("b1c_sb"), [128, NE * KF], F32)).ap()
    t.cent_sb = e(nc.sbuf_tensor(_u("cent_sb"), [128, KD * 16], F32)).ap()
    t.csum_sb = e(nc.sbuf_tensor(_u("csum_sb"), [16, 1], F32)).ap()
    t.sc_f32 = e(nc.sbuf_tensor(_u("sc_f32"), [16, MB], F32)).ap()
    t.sc_pre = e(nc.sbuf_tensor(_u("sc_pre"), [16, HF], F32)).ap()
    t.sc_tok = e(nc.sbuf_tensor(_u("sc_tok"), [128, 16], F32)).ap()
    t.gmax = e(nc.sbuf_tensor(_u("gmax"), [128, 8], F32)).ap()
    t.gpop = e(nc.sbuf_tensor(_u("gpop"), [128, 16], F32)).ap()
    t.gmsk = e(nc.sbuf_tensor(_u("gmsk"), [128, 16], F32)).ap()
    t.gsum = e(nc.sbuf_tensor(_u("gsum"), [128, 2], F32)).ap()
    t.gtok = e(nc.sbuf_tensor(_u("gtok"), [128, 16], F32)).ap()
    t.ybf = t.scr[:, 0:HF].bitcast(BF16)

    t.xa_rot = Rot(2, p, "xas")
    t.scr_rot = Rot(2, p, "scs")
    t.w1_rot = Rot(3, p, "w1s")
    t.w2_rot = Rot(2, p, "w2s")
    t.htmp_rot = Rot(1)
    t.gbc_rot = Rot(2)
    p.store_sems += t.scr_rot.sems
    for rot in (t.xa_rot, t.scr_rot, t.w1_rot, t.w2_rot, t.htmp_rot, t.gbc_rot):
        for sl in range(rot.n):
            rot.set_rel(sl, barrier)

    p.dma("scalar", t.b1c_sb[:, :], t.b1c[:, :], wait=barrier, inc=t.s_prep)
    p.dma("scalar", t.cent_sb[:, :], t.centp[:, :], inc=t.s_prep)
    p.dma("gpsimd", t.b2_sb[:, :], t.b2r[:, :], wait=barrier, inc=t.s_prep)
    p.dma("gpsimd", t.sel_sb[:, :], t.sel[:, :], inc=t.s_prep)
    pv = t.s_prep.n
    p.v("tensor_scalar_mul", t.b1c_sb[:, NRT * KF:NE * KF],
        t.b1c_sb[:, NRT * KF:NE * KF], 0.5, wait=[(t.s_prep, pv)], inc=t.s_prep2)
    p.v("memset", t.g_aug[0:NE, :], 0.5, inc=t.s_prep2)
    with nc.psum_tensor(_u("ps_csum"), [16, 1], F32) as ps_csum_h:
        ps_csum = ps_csum_h.ap()
        for k in range(KD):
            p.mm(ps_csum[:, :], t.cent_sb[:, k * 16:(k + 1) * 16], t.onescol_sb[:, :],
                 start=(k == 0), stop=(k == KD - 1),
                 wait=(([(t.s_prep, pv), (t.s_ev, t.s_ev.n), (t.s_oc, t.s_oc.n),
                         (t.s_pc, t.s_pc.n), (t.s_exp, t.s_exp.n)] + barrier)
                       if k == 0 else []),
                 inc=(t.s_pec if k == KD - 1 else None))
        p.v("tensor_copy", t.csum_sb[:, :], ps_csum[:, :],
            wait=[(t.s_pec, t.s_pec.n)], inc=t.s_prep2)



def emit_ffn(p, t, IN, OUT, n_mb, first_call, rows_src=None, rows_dst=None):
    nc = p.nc
    st = NS()
    with ExitStack() as ps:
        en = ps.enter_context
        st.hps = [en(nc.psum_tensor(_u(f"hps{i}"), [128, MB], F32)).ap()
                  for i in range(2)]
        st.yps = [en(nc.psum_tensor(_u(f"yps{i}"), [128, HF], F32)).ap()
                  for i in range(2)]
        st.mps = [en(nc.psum_tensor(_u(f"mps{i}"), [128, HF], F32)).ap()
                  for i in range(2)]
        g0 = [(t.s_ev, t.s_ev.n), (t.s_ya, t.s_ya.n), (t.s_u, t.s_u.n),
              (t.s_hd, t.s_hd.n), (t.s_rl, t.s_rl.n), (t.s_gt, t.s_gt.n),
              (t.s_row, t.s_row.n), (t.s_prep2, t.s_prep2.n),
              (t.s_oc, t.s_oc.n), (t.s_pc, t.s_pc.n), (t.s_exp, t.s_exp.n),
              (t.s_kv, t.s_kv.n), (t.s_qn, t.s_qn.n), (t.s_pn, t.s_pn.n)]
        st.hrel = [list(g0), list(g0)]
        st.yrel = [list(g0), list(g0)]
        st.mrel = [list(g0), list(g0)]
        st.rowv = [None, None]
        st.overlap = getattr(t, "overlap_stats", True)
        if first_call:
            _stats_compute(p, t, st, IN, 0, 0, parity=0)
        pend_evac = None
        for mb in range(n_mb):
            pend_evac = _ffn_mb(p, t, st, IN, OUT, mb, n_mb, first_call,
                                rows_src, rows_dst, pend_evac)
        pend_evac[0]()
        pend_evac[1]()
        if rows_dst is not None:
            mbl = n_mb - 1
            _stats_compute(p, t, st, OUT, mbl // 2, (mbl % 2) * MB, parity=None,
                           dram_rows=(rows_dst, mbl // 2, (mbl % 2) * MB),
                           wait0=[(s, s.n) for s in t.scr_rot.sems])


def _stats_compute(p, t, st, SRC, sq, c0, parity, dram_rows=None, wait0=()):
    srcr = SRC[sq].rearrange("(k pp) tt -> pp k tt", pp=128)
    first = True
    ybf_guard = [[(s, s.n) for s in t.scr_rot.sems]
                 + [(t.s_u, t.s_u.n), (t.s_ev, t.s_ev.n)]]
    rowv = []
    for h in range(2):
        hcol = c0 + h * HF
        sumv = st.mps[0][0:1, :]
        ssqv = st.mps[1][0:1, :]
        for kg in range(4):
            sl, w = t.xa_rot.acq()
            xsem = t.xa_rot.sems[sl]
            xgrp = t.xa[:, sl * 2048:(sl + 1) * 2048]
            p.dma("scalar", xgrp.rearrange("pp (k tt) -> pp k tt", k=4),
                  srcr[:, 4 * kg:4 * kg + 4, hcol:hcol + HF],
                  wait=w + (list(wait0) if first else []), inc=xsem)
            first = False
            ldv = xsem.n
            for kk in range(4):
                k = 4 * kg + kk
                xab = t.xa[:, sl * 2048 + kk * HF:sl * 2048 + (kk + 1) * HF]
                if STATS_BF16:
                    xc = t.ybf[:, 0:HF]
                    xq = t.ybf[:, HF:2 * HF]
                    p.v("tensor_copy", xc, xab,
                        wait=ybf_guard[0] + [(xsem, ldv)])
                    p.drain()
                    p.v("tensor_mul", xq, xc, xc, inc=t.s_sq)
                    ones = t.onescol_bf
                else:
                    xc = xab
                    xq = t.scr[:, 0:HF]
                    p.v("tensor_mul", xq, xab, xab,
                        wait=ybf_guard[0] + [(xsem, ldv)], inc=t.s_sq)
                    ones = t.onescol_sb
                sqv = t.s_sq.n
                p.mm(sumv, ones[:, :], xc, start=(k == 0),
                     stop=(k == KD - 1),
                     wait=((st.mrel[0] + st.mrel[1]) if k == 0 else [])
                     + [(t.s_sq, sqv), (xsem, ldv)], inc=t.s_pex)
                p.mm(ssqv, ones[:, :], xq, start=(k == 0),
                     stop=(k == KD - 1), inc=t.s_peq)
                ybf_guard[0] = [(t.s_peq, t.s_peq.n), (t.s_pex, t.s_pex.n)]
            t.xa_rot.set_rel(sl, [(t.s_pex, t.s_pex.n), (t.s_peq, t.s_peq.n)])
        mean = t.trow[0:1, 0:HF]
        e2 = t.trow[0:1, HF:2 * HF]
        tmp2 = t.trow[0:1, 2 * HF:3 * HF]
        wstat = [(t.s_pex, t.s_pex.n), (t.s_peq, t.s_peq.n)]
        p.v("tensor_scalar_mul", mean, sumv, 1.0 / D, wait=wstat)
        p.drain()
        p.v("tensor_scalar_mul", e2, ssqv, 1.0 / D)
        p.drain()
        p.v("tensor_mul", tmp2, mean, mean)
        p.drain()
        p.v("tensor_sub", e2, e2, tmp2, inc=t.s_row)
        vr0 = t.s_row.n
        st.mrel[0] = [(t.s_row, vr0)]
        st.mrel[1] = [(t.s_row, vr0)]
        p.drain()

        p.act(tmp2, e2, AF.Ln, scale=1.0, bias=t.eps_ln, wait=[(t.s_row, vr0)])
        p.drain("scalar")
        if parity is not None:
            rdst = t.r_row[0:1, parity * MB + h * HF:parity * MB + (h + 1) * HF]
            mrdst = t.mr_row[0:1, parity * MB + h * HF:parity * MB + (h + 1) * HF]
        else:
            rdst = t.sc_pre[0:1, 0:HF]
            mrdst = e2
        war = ([(t.s_pec, t.s_pec.n), (t.s_ev, t.s_ev.n)]
               if parity is not None else [])
        p.act(rdst, tmp2, AF.Exp, scale=-0.5, wait=war, inc=t.s_r)
        vr = t.s_r.n
        p.v("scalar_tensor_tensor", mrdst, mean, -1.0, rdst, ALU.mult, ALU.mult,
            wait=[(t.s_r, vr)], inc=t.s_row)
        vmr = t.s_row.n
        p.drain()
        if dram_rows is not None:
            rt, rsq, rc0 = dram_rows
            hc = rc0 + h * HF
            p.dma("sync", rt[rsq, 0:1, hc:hc + HF], rdst,
                  wait=[(t.s_r, vr)], inc=t.s_yrw)
            p.dma("sync", rt[rsq, 1:2, hc:hc + HF], mrdst,
                  wait=[(t.s_row, vmr)], inc=t.s_yrw)
        rowv.append(((t.s_r, vr), (t.s_row, vmr)))
    if parity is not None:
        st.rowv[parity] = rowv
        if getattr(t, "dbg", False) and not hasattr(t, "_dbgd"):
            t._dbgd = True
            p.dma("scalar", t.dbg_tr[0:1, 0:MB],
                  t.r_row[0:1, parity * MB:(parity + 1) * MB],
                  wait=[rowv[0][0], rowv[1][0]], inc=t.s_yrw)
            p.dma("scalar", t.dbg_tr[0:1, MB:2 * MB],
                  t.mr_row[0:1, parity * MB:(parity + 1) * MB],
                  wait=[rowv[0][1], rowv[1][1]], inc=t.s_yrw)
    t.scr_rot.rel[0] = tuple(list(t.scr_rot.rel[0])
                             + [(t.s_peq, t.s_peq.n)])


def _ffn_mb(p, t, st, IN, OUT, mb, n_mb, first_call, rows_src, rows_dst,
            pend_evac):
    nc = p.nc
    sq = mb // 2
    c0 = (mb % 2) * MB
    par = mb % 2

    if rows_src is not None:
        rsem = t.s_rwl[par]
        p.dma("scalar", t.r_row[0:1, par * MB:(par + 1) * MB],
              rows_src[sq, 0:1, c0:c0 + MB],
              wait=[(t.s_y2rw, t.s_y2rw.n), (t.s_row, t.s_row.n),
                    (t.s_pec, t.s_pec.n)], inc=rsem)
        p.dma("scalar", t.mr_row[0:1, par * MB:(par + 1) * MB],
              rows_src[sq, 1:2, c0:c0 + MB], inc=rsem)
        rv = (rsem, rsem.n)
        st.rowv[par] = [(rv, rv), (rv, rv)]
    rowv = st.rowv[par]

    inr = IN[sq].rearrange("(k pp) tt -> pp k tt", pp=128)
    uvals = {}
    for h in range(2):
        hcol = c0 + h * HF
        rh = t.r_row[0:1, par * MB + h * HF:par * MB + (h + 1) * HF]
        mrh = t.mr_row[0:1, par * MB + h * HF:par * MB + (h + 1) * HF]
        p.mm(st.yps[0][:, :], t.onesrow_sb[0:1, :], rh, start=True, stop=True,
             wait=st.yrel[0] + [rowv[h][0]], inc=t.s_pec)
        vub = t.s_pec.n
        p.mm(st.yps[1][:, :], t.onesrow_sb[0:1, :], mrh, start=True, stop=True,
             wait=st.yrel[1] + [rowv[h][1]], inc=t.s_pec)
        vub2 = t.s_pec.n
        for kg in range(4):
            sl, w = t.xa_rot.acq()
            xsem = t.xa_rot.sems[sl]
            xgrp = t.xa[:, sl * 2048:(sl + 1) * 2048]
            p.dma("scalar", xgrp.rearrange("pp (k tt) -> pp k tt", k=4),
                  inr[:, 4 * kg:4 * kg + 4, hcol:hcol + HF], wait=w, inc=xsem)
            ldv = xsem.n
            for kk in range(4):
                k = 4 * kg + kk
                xab = t.xa[:, sl * 2048 + kk * HF:sl * 2048 + (kk + 1) * HF]
                ssl, sw = t.scr_rot.acq()
                tmp = t.scr[:, ssl * HF:(ssl + 1) * HF]
                p.v("tensor_mul", tmp, xab, st.yps[0][:, :],
                    wait=sw + [(xsem, ldv), (t.s_pec, vub2)])
                p.v("tensor_add",
                    t.u_bf[:, k * MB + h * HF:k * MB + h * HF + HF],
                    tmp, st.yps[1][:, :], inc=t.s_u)
                uvals[(h, k)] = t.s_u.n
                t.scr_rot.set_rel(ssl, [(t.s_u, t.s_u.n)])
            t.xa_rot.set_rel(sl, [(t.s_u, t.s_u.n)])
        st.yrel[0] = [(t.s_u, t.s_u.n)]
        st.yrel[1] = [(t.s_u, t.s_u.n)]
    if t.dbg and mb == 0:
        p.dma("sync", t.dbg_u[:, :], t.u_bf[:, :],
              wait=[(t.s_u, t.s_u.n)], inc=t.s_yrw)

    for h in range(2):
        hcol = c0 + h * HF
        scxv = st.mps[0][0:16, :]
        for kg in range(4):
            sl, w = t.xa_rot.acq()
            xsem = t.xa_rot.sems[sl]
            xgrp = t.xa[:, sl * 2048:(sl + 1) * 2048]
            p.dma("scalar", xgrp.rearrange("pp (k tt) -> pp k tt", k=4),
                  inr[:, 4 * kg:4 * kg + 4, hcol:hcol + HF], wait=w, inc=xsem)
            ldv = xsem.n
            for kk in range(4):
                k = 4 * kg + kk
                xab = t.xa[:, sl * 2048 + kk * HF:sl * 2048 + (kk + 1) * HF]
                p.mm(scxv, t.cent_sb[:, k * 16:(k + 1) * 16], xab,
                     start=(k == 0), stop=(k == KD - 1),
                     wait=(st.mrel[0] if k == 0 else []) + [(xsem, ldv)],
                     inc=t.s_pec)
            t.xa_rot.set_rel(sl, [(t.s_pec, t.s_pec.n)])
        vscx = t.s_pec.n
        scx_sb = t.scx_sb[:, 0:HF]
        p.v("tensor_copy", scx_sb, scxv, wait=[(t.s_pec, vscx)], inc=t.s_gt)
        vcopy = t.s_gt.n
        p.drain()
        rh = t.r_row[0:1, par * MB + h * HF:par * MB + (h + 1) * HF]
        mrh = t.mr_row[0:1, par * MB + h * HF:par * MB + (h + 1) * HF]
        p.mm(st.mps[0][0:16, :], t.onesrow_sb[0:1, 0:16], rh, start=True,
             stop=True, wait=[(t.s_gt, vcopy), rowv[h][0]], inc=t.s_pec)
        v16a = t.s_pec.n
        p.mm(st.mps[1][0:16, :], t.onesrow_sb[0:1, 0:16], mrh, start=True,
             stop=True, wait=st.mrel[1] + [rowv[h][1]], inc=t.s_pec)
        v16b = t.s_pec.n
        p.v("tensor_mul", t.sc_pre[:, :], scx_sb, st.mps[0][0:16, :],
            wait=[(t.s_pec, v16a)])
        p.drain()
        p.v("scalar_tensor_tensor", t.sc_pre[:, :], st.mps[1][0:16, :],
            t.csum_sb[:, :], t.sc_pre[:, :], ALU.mult, ALU.add,
            wait=[(t.s_pec, v16b)], inc=t.s_row)
        vpre = t.s_row.n
        st.mrel[0] = [(t.s_row, vpre)]
        st.mrel[1] = [(t.s_row, vpre)]
        p.drain()
        scf = t.sc_f32[0:16, h * HF:(h + 1) * HF]
        p.act(scf, t.sc_pre[:, :], AF.Exp, scale=-1.0, wait=[(t.s_row, vpre)],
              inc=t.s_r)
        p.v("tensor_scalar_add", scf, scf, 1.0, wait=[(t.s_r, t.s_r.n)])
        p.drain()
        p.v("reciprocal", t.sc_pre[:, :], scf)
        p.drain()
        p.v("tensor_scalar_add", scf, t.sc_pre[:, :], t.rbias_sb[:, :],
            inc=t.s_scf)
        p.drain()
    scf_total = t.s_scf.n
    tp_rel = list(st.hrel[0])
    gt_rel = list(st.hrel[1])
    for tt in range(8):
        p.tr(st.hps[0][:, 0:16], t.sc_f32[0:16, tt * 128:(tt + 1) * 128],
             t.idf_sb[0:16, 0:16], wait=tp_rel + [(t.s_scf, scf_total)],
             inc=t.s_pec)
        vtp = t.s_pec.n
        stok = t.sc_tok[:, 0:16]
        p.v("tensor_copy", stok, st.hps[0][:, 0:16], wait=[(t.s_pec, vtp)],
            inc=t.s_gt)
        tp_rel = [(t.s_gt, t.s_gt.n)]
        p.drain()
        p.v("max", t.gmax[:, :], stok)
        p.drain()
        p.v("memset", t.gmax[:, 4:8], NEG)
        p.drain()
        p.v("match_replace", t.gpop[:, :], t.gmax[:, :], stok, NEG)
        p.drain()
        p.v("tensor_tensor", t.gmsk[:, :], stok, t.gpop[:, :], ALU.is_gt)
        p.drain()
        p.v("tensor_mul", t.gpop[:, :], stok, t.gmsk[:, :])
        p.drain()
        p.v("reduce_sum", t.gsum[:, 0:1], t.gpop[:, :], AX)
        p.drain()
        p.v("tensor_scalar_max", t.gsum[:, 0:1], t.gsum[:, 0:1], 1e-8)
        p.drain()
        p.v("reciprocal", t.gsum[:, 1:2], t.gsum[:, 0:1])
        p.drain()
        p.v("tensor_scalar", t.gtok[:, :], t.gpop[:, :], t.gsum[:, 1:2], 0.5,
            ALU.mult, ALU.mult, inc=t.s_gt)
        vg = t.s_gt.n
        p.drain()
        p.tr(st.hps[1][0:16, 0:128], t.gtok[:, :], t.idf_sb[:, :],
             wait=gt_rel + [(t.s_gt, vg)], inc=t.s_pec)
        vgt = t.s_pec.n
        p.v("tensor_copy", t.g_aug[0:16, tt * 128:(tt + 1) * 128],
            st.hps[1][0:16, 0:128], wait=[(t.s_pec, vgt)], inc=t.s_gt)
        gt_rel = [(t.s_gt, t.s_gt.n)]
        p.drain()
    gaug_done = t.s_gt.n
    st.hrel[0] = list(tp_rel)
    st.hrel[1] = list(gt_rel)
    if t.dbg and mb == 0:
        p.dma("sync", t.dbg_sc[:, :], t.sc_f32[0:16, :],
              wait=[(t.s_scf, scf_total)], inc=t.s_yrw)
        p.dma("sync", t.dbg_g[:, :], t.g_aug[0:NE, :],
              wait=[(t.s_gt, gaug_done)], inc=t.s_yrw)

    if pend_evac is not None:
        pend_evac[0]()

    idx0 = [0]

    def do_init():
        idx = 0
        for d in range(KD):
            for h in range(2):
                yp = st.yps[idx % 2]
                p.mm(yp[:, :], t.b2_sb[0:NE, d * 128:(d + 1) * 128],
                     t.g_aug[0:NE, h * HF:(h + 1) * HF], start=True, stop=True,
                     wait=st.yrel[idx % 2] + [(t.s_gt, gaug_done)], inc=t.s_p2)
                vp2 = t.s_p2.n
                p.v("tensor_copy",
                    t.y_sb[:, d * MB + h * HF:d * MB + h * HF + HF],
                    yp[:, :], wait=[(t.s_p2, vp2)], inc=t.s_ya)
                st.yrel[idx % 2] = [(t.s_ya, t.s_ya.n)]
                idx += 1
        idx0[0] = idx

    inj_x = first_call and (mb + 1 < n_mb) and st.overlap
    inj_y = first_call and (mb >= 1) and st.overlap
    if first_call and not st.overlap and mb + 1 < n_mb:
        _stats_compute(p, t, st, IN, (mb + 1) // 2, ((mb + 1) % 2) * MB,
                       parity=(mb + 1) % 2)
    if first_call and not st.overlap and mb >= 1 and rows_dst is not None:
        _stats_compute(p, t, st, OUT, (mb - 1) // 2, ((mb - 1) % 2) * MB,
                       parity=None,
                       dram_rows=(rows_dst, (mb - 1) // 2, ((mb - 1) % 2) * MB),
                       wait0=[(s, s.n) for s in t.scr_rot.sems])
    for ei, e_ in enumerate(EXPERT_ORDER):
        if ei == 5 and inj_x:
            _stats_compute(p, t, st, IN, (mb + 1) // 2, ((mb + 1) % 2) * MB,
                           parity=(mb + 1) % 2)
        if ei == 10 and inj_y:
            _stats_compute(p, t, st, OUT, (mb - 1) // 2, ((mb - 1) % 2) * MB,
                           parity=None,
                           dram_rows=(rows_dst, (mb - 1) // 2,
                                      ((mb - 1) % 2) * MB),
                           wait0=[(s, s.n) for s in t.scr_rot.sems])
        routed = e_ < NRT
        if routed:
            gsl, gw = t.gbc_rot.acq()
            for h in range(2):
                p.mm(st.hps[1][:, h * HF:(h + 1) * HF],
                     t.sel_sb[0:16, e_ * 128:(e_ + 1) * 128],
                     t.g_aug[0:16, h * HF:(h + 1) * HF], start=True, stop=True,
                     wait=st.hrel[1] + gw + [(t.s_gt, gaug_done)],
                     inc=(t.s_pec if h == 1 else None))
            vsel = t.s_pec.n
            gslice = t.g_bc[:, gsl * MB:(gsl + 1) * MB]
            p.v("tensor_copy", gslice, st.hps[1][:, :], wait=[(t.s_pec, vsel)],
                inc=t.s_gb)
            vgb = t.s_gb.n
            st.hrel[1] = [(t.s_gb, vgb)]
        hd_val = []
        for c in range(KF):
            pr = c % 2
            hp = st.hps[pr]
            wsl, ww = t.w1_rot.acq()
            wsem = t.w1_rot.sems[wsl]
            w1s = t.w1buf[:, wsl * 2048:(wsl + 1) * 2048]
            vld, wrel = load_cached(p, t, ("w1", e_, c), t.w1p[e_, c],
                                    t.w1b[e_, c], w1s, wsem, ww)
            for h in range(2):
                for k in range(KD):
                    wl = [(t.s_u, uvals[(h, k)])]
                    if h == 0 and k == 0:
                        wl = st.hrel[pr] + [(wsem, vld)] + wl
                    p.mm(hp[:, h * HF:(h + 1) * HF],
                         w1s[:, k * 128:(k + 1) * 128],
                         t.u_bf[:, k * MB + h * HF:k * MB + h * HF + HF],
                         start=(k == 0), stop=(k == KD - 1), wait=wl,
                         inc=(t.s_p1 if (h == 1 and k == KD - 1) else None))
            vp1 = t.s_p1.n
            t.w1_rot.set_rel(wsl, [(t.s_p1, vp1)] + wrel)
            bcol = t.b1c_sb[:, e_ * KF + c:e_ * KF + c + 1]
            if routed:
                hsl, hw = t.htmp_rot.acq()
                htt = t.h_tmp[:, :]
                p.act(htt, hp[:, :], AF.Relu, scale=1.0, bias=bcol,
                      wait=hw + [(t.s_p1, vp1)], inc=t.s_rl)
                vrl = t.s_rl.n
                st.hrel[pr] = [(t.s_rl, vrl)]
                p.v("tensor_mul", t.h_bf[:, c * MB:(c + 1) * MB], htt, gslice,
                    wait=[(t.s_rl, vrl), (t.s_gb, vgb)], inc=t.s_hd)
                t.htmp_rot.set_rel(hsl, [(t.s_hd, t.s_hd.n)])
            else:
                p.act(t.h_bf[:, c * MB:(c + 1) * MB], hp[:, :], AF.Relu,
                      scale=0.5, bias=bcol, wait=[(t.s_p1, vp1)], inc=t.s_hd)
                st.hrel[pr] = [(t.s_hd, t.s_hd.n)]
            hd_val.append(t.s_hd.n)
        if ei == 0:
            if pend_evac is not None:
                pend_evac[1]()
            do_init()
        for d in range(KD):
            wsl2, ww2 = t.w2_rot.acq()
            wsem2 = t.w2_rot.sems[wsl2]
            w2s = t.w2buf[:, wsl2 * 1408:(wsl2 + 1) * 1408]
            vld2, wrel2 = load_cached(p, t, ("w2", e_, d), t.w2p[e_, d],
                                      t.w2b[e_, d], w2s, wsem2, ww2)
            for h in range(2):
                yp = st.yps[idx0[0] % 2]
                for k in range(KF):
                    wl = [(t.s_hd, hd_val[k])]
                    if k == 0:
                        wl = st.yrel[idx0[0] % 2] + [(wsem2, vld2)] + wl
                    p.mm(yp[:, :], w2s[:, k * 128:(k + 1) * 128],
                         t.h_bf[:, k * MB + h * HF:k * MB + h * HF + HF],
                         start=(k == 0), stop=(k == KF - 1), wait=wl,
                         inc=(t.s_p2 if k == KF - 1 else None))
                vp2 = t.s_p2.n
                p.v("tensor_add",
                    t.y_sb[:, d * MB + h * HF:d * MB + h * HF + HF],
                    t.y_sb[:, d * MB + h * HF:d * MB + h * HF + HF],
                    yp[:, :], wait=[(t.s_p2, vp2)], inc=t.s_ya)
                st.yrel[idx0[0] % 2] = [(t.s_ya, t.s_ya.n)]
                idx0[0] += 1
            t.w2_rot.set_rel(wsl2, [(t.s_p2, t.s_p2.n)] + wrel2)
        if routed:
            t.gbc_rot.set_rel(gsl, [(t.s_hd, t.s_hd.n)])

    if t.dbg and mb == 0:
        p.dma("sync", t.dbg_y[:, :], t.y_sb[:, :],
              wait=[(t.s_ya, t.s_ya.n)], inc=t.s_yrw)

    def ev0():
        _ffn_evac(p, t, st, IN, OUT, sq, c0, par, 0)

    def ev1():
        _ffn_evac(p, t, st, IN, OUT, sq, c0, par, 1)
    return (ev0, ev1)


def _ffn_evac(p, t, st, IN, OUT, sq, c0, par, hsel):
    inr = IN[sq].rearrange("(k pp) tt -> pp k tt", pp=128)
    evg = [(t.s_ya, t.s_ya.n), (t.s_hd, t.s_hd.n), (t.s_rl, t.s_rl.n),
           (t.s_gb, t.s_gb.n)]
    for h in (hsel,):
        hcol = c0 + h * HF
        c1r = t.trow[0:1, 0:HF]
        c0r = t.trow[0:1, HF:2 * HF]
        rh = t.r_row[0:1, par * MB + h * HF:par * MB + (h + 1) * HF]
        mrh = t.mr_row[0:1, par * MB + h * HF:par * MB + (h + 1) * HF]
        p.v("tensor_scalar", c1r, rh, 0.5, 1.0, ALU.mult, ALU.add,
            wait=[(t.s_r, t.s_r.n), (t.s_row, t.s_row.n)])
        p.drain()
        p.v("tensor_scalar_mul", c0r, mrh, 0.5, inc=t.s_row)
        vcr = t.s_row.n
        p.drain()
        p.mm(st.mps[0][:, :], t.onesrow_sb[0:1, :], c1r, start=True, stop=True,
             wait=evg + st.mrel[0] + [(t.s_row, vcr)], inc=t.s_pec)
        vb1 = t.s_pec.n
        p.mm(st.mps[1][:, :], t.onesrow_sb[0:1, :], c0r, start=True, stop=True,
             wait=st.mrel[1], inc=t.s_pec)
        vb0 = t.s_pec.n
        for dg in range(4):
            sl, w = t.xa_rot.acq()
            xsem = t.xa_rot.sems[sl]
            xgrp = t.xa[:, sl * 2048:(sl + 1) * 2048]
            p.dma("scalar", xgrp.rearrange("pp (k tt) -> pp k tt", k=4),
                  inr[:, 4 * dg:4 * dg + 4, hcol:hcol + HF], wait=w, inc=xsem)
            ldv = xsem.n
            for dd in range(4):
                d = 4 * dg + dd
                xab = t.xa[:, sl * 2048 + dd * HF:sl * 2048 + (dd + 1) * HF]
                ssl, sw = t.scr_rot.acq()
                ssem = t.scr_rot.sems[ssl]
                stg = t.scr[:, ssl * HF:(ssl + 1) * HF]
                p.v("tensor_mul", stg, xab, st.mps[0][:, :],
                    wait=sw + [(xsem, ldv), (t.s_pec, vb1)])
                p.v("tensor_add", stg, stg, st.mps[1][:, :],
                    wait=[(t.s_pec, vb0)])
                p.v("tensor_add", stg, stg,
                    t.y_sb[:, d * MB + h * HF:d * MB + h * HF + HF],
                    inc=t.s_ev)
                vev = t.s_ev.n
                p.dma("sync", OUT[sq, d * 128:(d + 1) * 128, hcol:hcol + HF],
                      stg, wait=[(t.s_ev, vev)], inc=ssem)
                t.scr_rot.set_rel(ssl, [(ssem, ssem.n)])
            t.xa_rot.set_rel(sl, [(t.s_ev, t.s_ev.n)])
        st.mrel[0] = [(t.s_ev, t.s_ev.n)]
        st.mrel[1] = [(t.s_ev, t.s_ev.n)]


def emit_attn(p, t, nc, ctx, cfg):
    e = ctx.enter_context
    bar = p.stores_done()
    pe_bar = bar + [(t.s_ev, t.s_ev.n), (t.s_ya, t.s_ya.n), (t.s_u, t.s_u.n),
                    (t.s_hd, t.s_hd.n), (t.s_rl, t.s_rl.n), (t.s_gt, t.s_gt.n),
                    (t.s_scf, t.s_scf.n)]
    t.KT = e(nc.sbuf_tensor("KT", [128, NKV * T], BF16)).ap()
    t.V = e(nc.sbuf_tensor("V", [128, 16 * 512], BF16)).ap()
    t.u2 = e(nc.sbuf_tensor("u2", [128, KD * 512], BF16)).ap()
    t.ya = e(nc.sbuf_tensor("ya", [128, 2 * 4 * HF], F32)).ap()
    t.stg2 = e(nc.sbuf_tensor("stg2", [128, 2 * HF], F32)).ap()
    t.sq2 = e(nc.sbuf_tensor("sq2", [128, 2 * HF], F32)).ap()
    t.probs = e(nc.sbuf_tensor("probs", [128, 3 * T], BF16)).ap()
    t.probsT = e(nc.sbuf_tensor("probsT", [128, T], BF16)).ap()
    t.attnT = e(nc.sbuf_tensor("attnT", [128, NH * 512], BF16)).ap()
    t.abf = e(nc.sbuf_tensor("abf", [128, 4 * HF], BF16)).ap()
    t.mxc = e(nc.sbuf_tensor("mxc", [128, 2], F32)).ap()
    t.srec = e(nc.sbuf_tensor("srec", [128, 4], F32)).ap()
    ya_rot = Rot(2, p, "yas")
    st2_rot = Rot(2, p, "st2s")
    sq2_rot = Rot(2)
    s_wkv = p.sem(_u("wkv"))
    s_wq = p.sem(_u("wq"))
    p.store_sems += st2_rot.sems
    for rot in (ya_rot, st2_rot, sq2_rot):
        for sl in range(rot.n):
            rot.set_rel(sl, bar)

    def _rows(sum_ps, ssq_ps, off):
        mean = t.trow[0:1, 0:HF]
        e2 = t.trow[0:1, HF:2 * HF]
        msq = t.trow[0:1, 2 * HF:3 * HF]
        rh = t.r_row[0:1, off:off + HF]
        mrh = t.mr_row[0:1, off:off + HF]
        wstat = [(t.s_pex, t.s_pex.n), (t.s_peq, t.s_peq.n)]
        p.v("tensor_scalar_mul", mean, sum_ps[:, :], 1.0 / D, wait=wstat)
        p.drain()
        p.v("tensor_scalar_mul", e2, ssq_ps[:, :], 1.0 / D)
        p.drain()
        p.v("tensor_mul", msq, mean, mean)
        p.drain()
        p.v("tensor_sub", e2, e2, msq, inc=t.s_row)
        vrow = t.s_row.n
        p.drain()
        lnv = t.trow[0:1, 2 * HF:3 * HF]
        p.act(lnv, e2, AF.Ln, scale=1.0, bias=t.eps_ln, wait=[(t.s_row, vrow)])
        p.drain("scalar")
        p.act(rh, lnv, AF.Exp, scale=-0.5, inc=t.s_r)
        vr = t.s_r.n
        p.v("scalar_tensor_tensor", mrh, mean, -1.0, rh, ALU.mult, ALU.mult,
            wait=[(t.s_r, vr)], inc=t.s_row)
        p.drain()
        return t.s_row.n

    def _u2_group(s_i, gc, rb_ps, mrb_ps, vrows, extra_pe_wait):
        p.mm(rb_ps[:, :], t.onesrow_sb[0:1, :], t.r_row[0:1, gc:gc + HF],
             start=True, stop=True,
             wait=extra_pe_wait + [(t.s_arow, vrows), (t.s_u, t.s_u.n)],
             inc=t.s_pec)
        p.mm(mrb_ps[:, :], t.onesrow_sb[0:1, :], t.mr_row[0:1, gc:gc + HF],
             start=True, stop=True, inc=t.s_pec)
        vub = t.s_pec.n
        y1r = t.y1d[s_i].rearrange("(k pp) tt -> pp k tt", pp=128)
        for kg in range(4):
            sl, w = ya_rot.acq()
            ysem = ya_rot.sems[sl]
            ygrp = t.ya[:, sl * 2048:(sl + 1) * 2048]
            p.dma("scalar", ygrp.rearrange("pp (k tt) -> pp k tt", k=4),
                  y1r[:, 4 * kg:4 * kg + 4, gc:gc + HF], wait=w, inc=ysem)
            ldv = ysem.n
            for kk in range(4):
                k = 4 * kg + kk
                yab = t.ya[:, sl * 2048 + kk * HF:sl * 2048 + (kk + 1) * HF]
                ssl, sw = sq2_rot.acq()
                tmp = t.sq2[:, ssl * HF:(ssl + 1) * HF]
                p.v("tensor_mul", tmp, yab, rb_ps[:, :],
                    wait=sw + [(ysem, ldv), (t.s_pec, vub), (t.s_p1, t.s_p1.n)])
                p.v("tensor_add", t.u2[:, k * HF:(k + 1) * HF], tmp,
                    mrb_ps[:, :], inc=t.s_u)
                uv = t.s_u.n
                sq2_rot.set_rel(ssl, [(t.s_u, uv)])
            ya_rot.set_rel(sl, [(t.s_u, t.s_u.n)])
        return t.s_u.n

    for s_i in range(2):
        p.dma("scalar", t.r_row[0:1, :], t.y1rows[s_i, 0:1, :],
              wait=[(t.s_yrw, t.s_yrw.n), (t.s_pec, t.s_pec.n),
                    (t.s_row, t.s_row.n), (t.s_ev, t.s_ev.n)], inc=t.s_arow)
        p.dma("scalar", t.mr_row[0:1, :], t.y1rows[s_i, 1:2, :], inc=t.s_arow)
        vrows_ld = t.s_arow.n

        with ExitStack() as ps:
            en = ps.enter_context
            wk_sb = en(nc.sbuf_tensor(_u("wk_sb"), [128, KD * 512], BF16)).ap()
            wv_sb = en(nc.sbuf_tensor(_u("wv_sb"), [128, KD * 512], BF16)).ap()
            kraw = en(nc.sbuf_tensor(_u("kraw"), [128, HF], F32)).ap()
            ksq = en(nc.sbuf_tensor(_u("ksq"), [128, HF], F32)).ap()
            rb_ps = en(nc.psum_tensor(_u("b_rb"), [128, HF], F32)).ap()
            mrb_ps = en(nc.psum_tensor(_u("b_mrb"), [128, HF], F32)).ap()
            kv_ps = [en(nc.psum_tensor(_u(f"b_kv{i}"), [128, HF], F32)).ap()
                     for i in range(2)]
            kn_ps = en(nc.psum_tensor(_u("b_kn"), [1, HF], F32)).ap()
            knb_ps = en(nc.psum_tensor(_u("b_knb"), [128, HF], F32)).ap()
            p.dma("gpsimd", wk_sb[:, :], t.wkp[:, :], wait=bar, inc=s_wkv)
            p.dma("gpsimd", wv_sb[:, :], t.wvp[:, :], inc=s_wkv)
            vwk = s_wkv.n
            kvrel = [pe_bar, pe_bar]
            for grp in range(4):
                gc = grp * HF
                uv = _u2_group(s_i, gc, rb_ps, mrb_ps, vrows_ld, pe_bar)
                pend_k = [None]
                for c in range(NKV):
                    pr = c % 2
                    for k in range(KD):
                        p.mm(kv_ps[pr][:, :],
                             wk_sb[:, k * 512 + c * 128:k * 512 + c * 128 + 128],
                             t.u2[:, k * HF:(k + 1) * HF],
                             start=(k == 0), stop=(k == KD - 1),
                             wait=(kvrel[pr] + [(s_wkv, vwk), (t.s_u, uv)]
                                   if k == 0 else []),
                             inc=(t.s_p1 if k == KD - 1 else None))
                    vk = t.s_p1.n
                    if pend_k[0] is not None:
                        pend_k[0]()

                    def make_kn(c=c, pr=pr, vk=vk, gc=gc):
                        def run():
                            p.v("tensor_copy", kraw[:, :], kv_ps[pr][:, :],
                                wait=[(t.s_p1, vk), (t.s_kv, t.s_kv.n)],
                                inc=t.s_kv)
                            kvrel[pr] = [(t.s_kv, t.s_kv.n)]
                            p.drain()
                            p.v("tensor_mul", ksq[:, :], kraw[:, :], kraw[:, :],
                                inc=t.s_sq)
                            vsq = t.s_sq.n
                            p.mm(kn_ps[:, :], t.onescol_sb[:, :], ksq[:, :],
                                 start=True, stop=True,
                                 wait=[(t.s_sq, vsq), (t.s_kv, t.s_kv.n)],
                                 inc=t.s_pec)
                            vkn = t.s_pec.n
                            kr0 = t.trow[0:1, 0:HF]
                            kr1 = t.trow[0:1, HF:2 * HF]
                            p.act(kr0, kn_ps[:, :], AF.Ln, scale=1.0 / 128.0,
                                  bias=t.eps_qk, wait=[(t.s_pec, vkn)])
                            p.drain("scalar")
                            p.act(kr1, kr0, AF.Exp, scale=-0.5, inc=t.s_r)
                            vr2 = t.s_r.n
                            p.mm(knb_ps[:, :], t.onesrow_sb[0:1, :], kr1,
                                 start=True, stop=True,
                                 wait=[(t.s_r, vr2), (t.s_kv, t.s_kv.n)],
                                 inc=t.s_pec)
                            vknb = t.s_pec.n
                            p.v("scalar_tensor_tensor",
                                t.KT[:, c * T + gc:c * T + gc + HF],
                                kraw[:, :], t.ksc_sb[:, :], knb_ps[:, :],
                                ALU.mult, ALU.mult,
                                wait=[(t.s_pec, vknb)], inc=t.s_kv)
                            p.drain()
                        return run
                    pend_k[0] = make_kn()
                pend_k[0]()
                for tt in range(4):
                    pr = tt % 2
                    for k in range(KD):
                        p.mm(kv_ps[pr][:, :],
                             t.u2[:, k * HF + tt * 128:k * HF + tt * 128 + 128],
                             wv_sb[:, k * 512:(k + 1) * 512],
                             start=(k == 0), stop=(k == KD - 1),
                             wait=(kvrel[pr] if k == 0 else []),
                             inc=(t.s_p1 if k == KD - 1 else None))
                    vv = t.s_p1.n
                    p.v("tensor_copy",
                        t.V[:, (grp * 4 + tt) * 512:(grp * 4 + tt) * 512 + 512],
                        kv_ps[pr][:, :], wait=[(t.s_p1, vv)], inc=t.s_kv)
                    kvrel[pr] = [(t.s_kv, t.s_kv.n)]
            b2_p1 = t.s_p1.n
            b2_kv = t.s_kv.n

        with ExitStack() as ps:
            en = ps.enter_context
            wq_sb = en(nc.sbuf_tensor(_u("wq_sb"), [128, 2 * 2048], BF16)).ap()
            wq_rot = Rot(2, p, "wqs")
            q4 = en(nc.sbuf_tensor(_u("q4"), [128, NH * 512], BF16)).ap()
            qraw = en(nc.sbuf_tensor(_u("qraw"), [128, HF], F32)).ap()
            qsq = en(nc.sbuf_tensor(_u("qsq"), [128, HF], F32)).ap()
            wo_buf = en(nc.sbuf_tensor(_u("wo_buf"), [128, 2 * 2048], BF16)).ap()
            wo_rot = Rot(2, p, "wos")
            for sl in range(2):
                wo_rot.set_rel(sl, [(t.s_p1, b2_p1), (t.s_kv, b2_kv)])
                wq_rot.set_rel(sl, [(t.s_p1, b2_p1), (t.s_kv, b2_kv)])
            oc_prev_grp = t.s_oc.n
            for grp in range(4):
                gc = grp * HF
                with ExitStack() as p2:
                    en2 = p2.enter_context
                    rb_ps = en2(nc.psum_tensor(_u("c_rb"), [128, HF], F32)).ap()
                    mrb_ps = en2(nc.psum_tensor(_u("c_mrb"), [128, HF], F32)).ap()
                    q_ps = [en2(nc.psum_tensor(_u(f"c_q{i}"), [128, HF], F32)).ap()
                            for i in range(2)]
                    qn_ps = en2(nc.psum_tensor(_u("c_qn"), [1, HF], F32)).ap()
                    qnb_ps = en2(nc.psum_tensor(_u("c_qnb"), [128, HF], F32)).ap()
                    ga = [(t.s_ev, t.s_ev.n), (t.s_kv, t.s_kv.n),
                          (t.s_qn, t.s_qn.n)]
                    uv = _u2_group(s_i, gc, rb_ps, mrb_ps, vrows_ld, ga)
                    qrel = [ga, ga]
                    pend_q = [None]
                    for m in range(NH):
                        pr = m % 2
                        qwsl, qww = wq_rot.acq()
                        qwsem = wq_rot.sems[qwsl]
                        wqs = wq_sb[:, qwsl * 2048:(qwsl + 1) * 2048]
                        vldq, qwrel = load_cached(p, t, ("wq", m), t.wqp[m],
                                                  t.wqb[m], wqs, qwsem, qww)
                        for k in range(KD):
                            p.mm(q_ps[pr][:, :],
                                 wqs[:, k * 128:(k + 1) * 128],
                                 t.u2[:, k * HF:(k + 1) * HF],
                                 start=(k == 0), stop=(k == KD - 1),
                                 wait=(qrel[pr] + [(qwsem, vldq), (t.s_u, uv)]
                                       if k == 0 else []),
                                 inc=(t.s_p1 if k == KD - 1 else None))
                        vq = t.s_p1.n
                        wq_rot.set_rel(qwsl, [(t.s_p1, vq)] + qwrel)
                        if pend_q[0] is not None:
                            pend_q[0]()

                        def make_qn(m=m, pr=pr, vq=vq):
                            def run():
                                p.v("tensor_copy", qraw[:, :], q_ps[pr][:, :],
                                    wait=[(t.s_p1, vq), (t.s_qn, t.s_qn.n)],
                                    inc=t.s_kv)
                                qrel[pr] = [(t.s_kv, t.s_kv.n)]
                                p.drain()
                                p.v("tensor_mul", qsq[:, :], qraw[:, :],
                                    qraw[:, :], inc=t.s_sq)
                                vsq = t.s_sq.n
                                p.mm(qn_ps[:, :], t.onescol_sb[:, :], qsq[:, :],
                                     start=True, stop=True,
                                     wait=[(t.s_sq, vsq), (t.s_qn, t.s_qn.n)],
                                     inc=t.s_pec)
                                vqn = t.s_pec.n
                                qr0 = t.trow[0:1, 0:HF]
                                qr1 = t.trow[0:1, HF:2 * HF]
                                p.act(qr0, qn_ps[:, :], AF.Ln, scale=1.0 / 128.0,
                                      bias=t.eps_qk, wait=[(t.s_pec, vqn)])
                                p.drain("scalar")
                                p.act(qr1, qr0, AF.Exp, scale=-0.5, inc=t.s_r)
                                vr2 = t.s_r.n
                                p.mm(qnb_ps[:, :], t.onesrow_sb[0:1, :], qr1,
                                     start=True, stop=True,
                                     wait=[(t.s_r, vr2), (t.s_qn, t.s_qn.n)],
                                     inc=t.s_pec)
                                vqnb = t.s_pec.n
                                p.v("scalar_tensor_tensor",
                                    q4[:, m * 512 + 0:m * 512 + 512],
                                    qraw[:, :], t.qsc_sb[:, :], qnb_ps[:, :],
                                    ALU.mult, ALU.mult,
                                    wait=[(t.s_pec, vqnb)], inc=t.s_qn)
                                p.drain()
                            return run
                        pend_q[0] = make_qn()
                    pend_q[0]()
                    q4_done = t.s_qn.n
                with ExitStack() as p2:
                    en2 = p2.enter_context
                    R = en2(nc.psum_tensor(_u("c_R"), [128, 5 * HF], F32)).ap()
                    pT_ps = [en2(nc.psum_tensor(_u(f"c_pT{i}"), [128, 512], BF16)).ap()
                             for i in range(2)]
                    out_ps = en2(nc.psum_tensor(_u("c_out"), [128, 128], F32)).ap()
                    ga0 = [(t.s_qn, q4_done), (t.s_kv, t.s_kv.n)]
                    unit_rel = [list(ga0) for _ in range(5)]
                    pT_rel = [list(ga0), list(ga0)]
                    out_rel = [list(ga0)]
                    pp_rel = [[], [], []]
                    upos = 0
                    pend_tm = []
                    for tl in range(4):
                        ti = grp * 4 + tl
                        nch = ti + 1
                        tsl = nch * 128
                        nun = (tsl + HF - 1) // HF
                        for hh in range(NH):
                            kv = hh // NKV
                            pp = (tl * NH + hh) % 3
                            units = [(upos + i) % 5 for i in range(nun)]
                            upos = (upos + nun) % 5
                            segs = []
                            sb = 0
                            for i in range(1, nun + 1):
                                if i == nun or units[i] != units[i - 1] + 1:
                                    segs.append((sb, i))
                                    sb = i
                            for b, u in enumerate(units):
                                nb = min(HF, tsl - b * HF)
                                p.mm(R[:, u * HF:u * HF + nb],
                                     q4[:, hh * 512 + tl * 128:hh * 512 + tl * 128 + 128],
                                     t.KT[:, kv * T + b * HF:kv * T + b * HF + nb],
                                     start=True, stop=True,
                                     wait=unit_rel[u] + ([(t.s_qn, q4_done)]
                                                         if b == 0 else []),
                                     inc=(t.s_sc2 if b == nun - 1 else None))
                            vsc = t.s_sc2.n
                            bd = (ti * 128) // HF
                            ud = units[bd]
                            off = (ti * 128) % HF
                            p.v("tensor_add", R[:, ud * HF + off:ud * HF + off + 128],
                                R[:, ud * HF + off:ud * HF + off + 128],
                                t.causal_sb[:, :], wait=[(t.s_sc2, vsc)])
                            p.drain()
                            for si, (a, b_) in enumerate(segs):
                                u0 = units[a]
                                ln = min((b_ - a) * HF, tsl - a * HF)
                                p.v("reduce_max", t.mxc[:, si:si + 1],
                                    R[:, u0 * HF:u0 * HF + ln], AX, negate=True)
                                p.drain()
                            if len(segs) == 2:
                                p.v("tensor_tensor", t.mxc[:, 0:1], t.mxc[:, 0:1],
                                    t.mxc[:, 1:2], ALU.min)
                                p.drain()
                            p.v("tensor_scalar_mul", t.mxc[:, 1:2], t.mxc[:, 0:1],
                                SCL, inc=t.s_mx)
                            vmx = t.s_mx.n
                            for si, (a, b_) in enumerate(segs):
                                u0 = units[a]
                                ln = min((b_ - a) * HF, tsl - a * HF)
                                p.act(t.probs[:, pp * T + a * HF:pp * T + a * HF + ln],
                                      R[:, u0 * HF:u0 * HF + ln], AF.Exp,
                                      scale=SCL, bias=t.mxc[:, 1:2],
                                      wait=(([(t.s_mx, vmx)] + pp_rel[pp])
                                            if si == 0 else []),
                                      inc=t.s_exp, acc=t.srec[:, 2 + si:3 + si])
                                self_rel = [(t.s_exp, t.s_exp.n)]
                                for u in units[a:b_]:
                                    unit_rel[u] = self_rel
                            vexp = t.s_exp.n
                            prb = t.probs[:, pp * T:pp * T + tsl]
                            if len(segs) == 2:
                                p.v("tensor_add", t.srec[:, 2:3], t.srec[:, 2:3],
                                    t.srec[:, 3:4], wait=[(t.s_exp, vexp)])
                                p.drain()
                            p.v("reciprocal", t.srec[:, 1:2], t.srec[:, 2:3],
                                wait=[(t.s_exp, vexp)])
                            p.drain()
                            p.v("tensor_scalar_mul", prb, prb, t.srec[:, 1:2],
                                inc=t.s_pn)
                            vpn = t.s_pn.n
                            if len(pend_tm) >= 2:
                                pend_tm.pop(0)()

                            def make_tm(nch=nch, pp=pp, kv=kv, tl=tl, hh=hh,
                                        vpn=vpn):
                                def run():
                                    nbt = (nch + 3) // 4
                                    vpc_b = {}

                                    def emit_tbatch(g):
                                        j0, j1 = 4 * g, min(4 * g + 4, nch)
                                        for j in range(j0, j1):
                                            p.tr(pT_ps[g % 2][:, (j - j0) * 128:(j - j0 + 1) * 128],
                                                 t.probs[:, pp * T + j * 128:pp * T + (j + 1) * 128],
                                                 t.idb_sb[:, :],
                                                 wait=(pT_rel[g % 2] + [(t.s_pn, vpn)]
                                                       if j == j0 else []),
                                                 inc=(t.s_pec if j == j1 - 1 else None))
                                        vtr = t.s_pec.n
                                        p.acopy(t.probsT[:, j0 * 128:j1 * 128],
                                                pT_ps[g % 2][:, 0:(j1 - j0) * 128],
                                                wait=[(t.s_pec, vtr)], inc=t.s_pc)
                                        vpc_b[g] = t.s_pc.n
                                        pT_rel[g % 2] = [(t.s_pc, vpc_b[g])]

                                    def emit_mbatch(g):
                                        j0, j1 = 4 * g, min(4 * g + 4, nch)
                                        for j in range(j0, j1):
                                            p.mm(out_ps[:, :],
                                                 t.V[:, j * 512 + kv * 128:j * 512 + kv * 128 + 128],
                                                 t.probsT[:, j * 128:(j + 1) * 128],
                                                 start=(j == 0), stop=(j == nch - 1),
                                                 wait=(([(t.s_pc, vpc_b[g])] +
                                                        (out_rel[0] if j == 0 else []))
                                                       if j == j0 else []),
                                                 inc=(t.s_out if j == nch - 1 else None))

                                    emit_tbatch(0)
                                    for g in range(1, nbt):
                                        emit_tbatch(g)
                                        emit_mbatch(g - 1)
                                    emit_mbatch(nbt - 1)
                                    pp_rel[pp] = [(t.s_pec, t.s_pec.n)]
                                    vout = t.s_out.n
                                    p.acopy(t.attnT[:, hh * 512 + tl * 128:hh * 512 + tl * 128 + 128],
                                            out_ps[:, :],
                                            wait=[(t.s_out, vout), (t.s_p2, t.s_p2.n)],
                                            inc=t.s_oc)
                                    out_rel[0] = [(t.s_oc, t.s_oc.n)]
                                return run
                            pend_tm.append(make_tm())
                    while pend_tm:
                        pend_tm.pop(0)()
                    grp_oc = t.s_oc.n
                with ExitStack() as p2:
                    en2 = p2.enter_context
                    y2_ps = [en2(nc.psum_tensor(_u(f"c_y2{i}"), [128, HF], F32)).ap()
                             for i in range(2)]
                    ysum_ps = en2(nc.psum_tensor(_u("c_ysum"), [1, HF], F32)).ap()
                    yssq_ps = en2(nc.psum_tensor(_u("c_yssq"), [1, HF], F32)).ap()
                    y2rel = [[(t.s_oc, grp_oc)], [(t.s_oc, grp_oc)]]
                    abf_guard = [[], []]
                    y1r = t.y1d[s_i].rearrange("(k pp) tt -> pp k tt", pp=128)
                    for d in range(KD):
                        pr = d % 2
                        if d % 4 == 0:
                            ysl, yw = ya_rot.acq()
                            ysem = ya_rot.sems[ysl]
                            ygrp = t.ya[:, ysl * 2048:(ysl + 1) * 2048]
                            p.dma("scalar",
                                  ygrp.rearrange("pp (k tt) -> pp k tt", k=4),
                                  y1r[:, d:d + 4, gc:gc + HF], wait=yw, inc=ysem)
                            vldy = ysem.n
                        wsl, ww = wo_rot.acq()
                        wosem = wo_rot.sems[wsl]
                        wos = wo_buf[:, wsl * 2048:(wsl + 1) * 2048]
                        vldo, worel = load_cached(p, t, ("wo", d), t.wop[d],
                                                  t.wob[d], wos, wosem, ww)
                        for k in range(KD):
                            p.mm(y2_ps[pr][:, :], wos[:, k * 128:(k + 1) * 128],
                                 t.attnT[:, k * 512:(k + 1) * 512],
                                 start=(k == 0), stop=(k == KD - 1),
                                 wait=(y2rel[pr] + [(wosem, vldo),
                                                    (t.s_oc, grp_oc)]
                                       if k == 0 else []),
                                 inc=(t.s_p2 if k == KD - 1 else None))
                        vwo = t.s_p2.n
                        wo_rot.set_rel(wsl, [(t.s_p2, vwo)] + worel)
                        yab = t.ya[:, ysl * 2048 + (d % 4) * HF:
                                   ysl * 2048 + (d % 4 + 1) * HF]
                        ssl, sw = st2_rot.acq()
                        ssem = st2_rot.sems[ssl]
                        stg = t.stg2[:, ssl * HF:(ssl + 1) * HF]
                        p.v("tensor_add", stg, y2_ps[pr][:, :], yab,
                            wait=sw + [(t.s_p2, vwo), (ysem, vldy)],
                            inc=t.s_ev)
                        vev = t.s_ev.n
                        y2rel[pr] = [(t.s_ev, vev)]
                        p.dma("sync", t.y2d[s_i, d * 128:(d + 1) * 128, gc:gc + HF],
                              stg, wait=[(t.s_ev, vev)], inc=ssem)
                        st2_rot.set_rel(ssl, [(ssem, ssem.n)])
                        if d % 4 == 3:
                            ya_rot.set_rel(ysl, [(t.s_ev, t.s_ev.n)])
                        base = (d % 2) * 2 * HF
                        yc = t.abf[:, base:base + HF]
                        yq = t.abf[:, base + HF:base + 2 * HF]
                        p.v("tensor_copy", yc, stg, wait=abf_guard[d % 2])
                        p.drain()
                        p.v("tensor_mul", yq, yc, yc, inc=t.s_sq)
                        vsq2 = t.s_sq.n
                        p.mm(ysum_ps[:, :], t.onescol_bf[:, :], yc,
                             start=(d == 0), stop=(d == KD - 1),
                             wait=([(t.s_row, t.s_row.n)] if d == 0 else [])
                             + [(t.s_sq, vsq2)], inc=t.s_pex)
                        p.mm(yssq_ps[:, :], t.onescol_bf[:, :], yq,
                             start=(d == 0), stop=(d == KD - 1), inc=t.s_peq)
                        abf_guard[d % 2] = [(t.s_peq, t.s_peq.n)]
                    mean = t.trow[0:1, 0:HF]
                    e2r = t.trow[0:1, HF:2 * HF]
                    tmp2 = t.trow[0:1, 2 * HF:3 * HF]
                    rr = t.sq2[0:1, 0:HF]
                    wstat = [(t.s_pex, t.s_pex.n), (t.s_peq, t.s_peq.n)]
                    p.v("tensor_scalar_mul", mean, ysum_ps[:, :], 1.0 / D,
                        wait=wstat)
                    p.drain()
                    p.v("tensor_scalar_mul", e2r, yssq_ps[:, :], 1.0 / D)
                    p.drain()
                    p.v("tensor_mul", tmp2, mean, mean)
                    p.drain()
                    p.v("tensor_sub", e2r, e2r, tmp2, inc=t.s_row)
                    vr0 = t.s_row.n
                    p.drain()
                    p.act(tmp2, e2r, AF.Ln, scale=1.0, bias=t.eps_ln,
                          wait=[(t.s_row, vr0)])
                    p.drain("scalar")
                    p.act(rr, tmp2, AF.Exp, scale=-0.5, inc=t.s_r)
                    vrr = t.s_r.n
                    p.v("scalar_tensor_tensor", e2r, mean, -1.0, rr,
                        ALU.mult, ALU.mult, wait=[(t.s_r, vrr)], inc=t.s_row)
                    vmr2 = t.s_row.n
                    p.drain()
                    p.dma("sync", t.y2rows[s_i, 0:1, gc:gc + HF], rr,
                          wait=[(t.s_r, vrr)], inc=t.s_y2rw)
                    p.dma("sync", t.y2rows[s_i, 1:2, gc:gc + HF], e2r,
                          wait=[(t.s_row, vmr2)], inc=t.s_y2rw)



def _prep_host(inputs):
    def gi(k):
        return np.ascontiguousarray(np.asarray(inputs[k], dtype=np.float32))
    x = gi("L0_attn_x")
    w1 = np.concatenate([gi("L0_ffn_routed_w1"), gi("L0_ffn_shared_w1")], 0)
    w2 = np.concatenate([gi("L0_ffn_routed_w2"), gi("L0_ffn_shared_w2")], 0)
    b1 = np.concatenate([gi("L0_ffn_routed_b1"), gi("L0_ffn_shared_b1")], 0)
    b2 = np.concatenate([gi("L0_ffn_routed_b2"), gi("L0_ffn_shared_b2")], 0)
    wq, wk = gi("L0_attn_Wq"), gi("L0_attn_Wk")
    wv, wo = gi("L0_attn_Wv"), gi("L0_attn_Wo")
    cent = gi("L0_ffn_centroids")
    d = {}
    d["w1p"] = np.ascontiguousarray(
        w1.reshape(NE, KD, 128, KF, 128).transpose(0, 3, 2, 1, 4).reshape(NE, KF, 128, KD * 128))
    d["w2p"] = np.ascontiguousarray(
        w2.reshape(NE, KF, 128, KD, 128).transpose(0, 3, 2, 1, 4).reshape(NE, KD, 128, KF * 128))
    d["wqp"] = np.ascontiguousarray(
        wq.reshape(NH, 128, KD, 128).transpose(0, 3, 2, 1).reshape(NH, 128, KD * 128))
    d["wop"] = np.ascontiguousarray(
        wo.reshape(KD, 128, KD, 128).transpose(0, 3, 2, 1).reshape(KD, 128, KD * 128))
    d["wkp"] = np.ascontiguousarray(wk.reshape(512, KD, 128).transpose(2, 1, 0).reshape(128, KD * 512))
    d["wvp"] = np.ascontiguousarray(wv.reshape(512, KD, 128).transpose(2, 1, 0).reshape(128, KD * 512))
    d["centp"] = np.ascontiguousarray(cent.reshape(16, KD, 128).transpose(2, 1, 0).reshape(128, KD * 16))
    d["b1c"] = np.ascontiguousarray(b1.reshape(NE, KF, 128).transpose(2, 0, 1).reshape(128, NE * KF))
    d["b2r"] = b2
    d["rbias"] = gi("L0_ffn_bias").reshape(16, 1)
    d["qsc"] = gi("L0_attn_q_scale").reshape(128, 1)
    d["ksc"] = gi("L0_attn_k_scale").reshape(128, 1)
    d["idf"] = np.eye(128, dtype=np.float32)
    d["causal"] = np.triu(np.full((128, 128), NEG, dtype=np.float32), k=1)
    d["onescol"] = np.ones((128, 1), np.float32)
    d["onesrow"] = np.ones((1, 128), np.float32)
    d["sel"] = np.repeat(np.eye(16, dtype=np.float32), 128, axis=1)
    d["cvec"] = np.array([[1e-5, 1e-6, 0, 0, 0, 0, 0, 0]], np.float32)
    xcores = [np.ascontiguousarray(x[2 * c:2 * c + 2].transpose(0, 2, 1)) for c in range(8)]
    return d, xcores


def kernel(**inputs):
    shared, xcores = _prep_host(inputs)
    cfg = dict(ffn1=True, attn=True, ffn2=True, n_mb=4)
    nc = build_nc(cfg)
    in_maps = [dict(shared, xT=xcores[c]) for c in range(8)]
    res = run_bass_kernel_spmd(nc, in_maps, core_ids=list(range(8)))
    parts = [res.results[c]["outT"].transpose(0, 2, 1) for c in range(8)]
    return np.ascontiguousarray(np.concatenate(parts, axis=0))
